# version 02
# baseline (speedup 1.0000x reference)
import contextlib
import numpy as np
import ml_dtypes
import concourse.bass as bass
import concourse.mybir as mybir
from concourse.bass_utils import run_bass_kernel_spmd

F32 = mybir.dt.float32
BF16 = mybir.dt.bfloat16
U32 = mybir.dt.uint32
AF = mybir.ActivationFunctionType
AX = mybir.AxisListType
ALU = mybir.AluOpType
BF = ml_dtypes.bfloat16

NCORES = 8
NTOK = 8192
PTOK = NTOK // NCORES
D = 2048
NTILE = PTOK // 128
EPS_LN = 1e-5


class Ctr:
    def __init__(self):
        self.c = {}

    def inc(self, instr, sem, n=1):
        assert instr is not None, "tried to attach semaphore to a None instruction"
        instr.then_inc(sem, n)
        self.c[id(sem)] = self.c.get(id(sem), 0) + n
        return self.c[id(sem)]

    def val(self, sem):
        return self.c.get(id(sem), 0)

    def fence(self, eng, instr, sem):
        n = self.inc(instr, sem)
        eng.wait_ge(sem, n)


def _consts():
    return {
        "ident_bf": np.eye(128, dtype=BF),
        "ident_f32": np.eye(128, dtype=np.float32),
    }



def build_ln0():
    nc = bass.Bass("TRN2", target_bir_lowering=False)
    x = nc.dram_tensor("x", [PTOK, D], F32, kind="ExternalInput").ap()
    idf = nc.dram_tensor("ident_f32", [128, 128], F32, kind="ExternalInput").ap()
    ln = nc.dram_tensor("ln", [PTOK, D], BF16, kind="ExternalOutput").ap()
    hn = nc.dram_tensor("hn", [16, 1], F32, kind="ExternalOutput").ap()

    x3 = x.rearrange("(t p) d -> t p d", p=128)
    ln3 = ln.rearrange("(t p) d -> t p d", p=128)
    C = Ctr()
    with contextlib.ExitStack() as es:
        sb = lambda n, s, d: es.enter_context(nc.sbuf_tensor(n, s, d))
        ps = lambda n, s, d: es.enter_context(nc.psum_tensor(n, s, d))
        sem = lambda n: es.enter_context(nc.semaphore(n))
        xt = sb("xt", [128, 2 * D], F32)
        xm = sb("xm", [128, D], F32)
        sq = sb("sq", [128, D], F32)
        lnb = sb("lnb", [128, 2 * D], BF16)
        m = sb("lm", [128, 1], F32)
        v2 = sb("lv2", [128, 1], F32)
        sd = sb("lsd", [128, 1], F32)
        rstd = sb("lrstd", [128, 1], F32)
        rstd2 = sb("lrstd2", [128, 1], F32)
        epsb = sb("epsb", [128, 1], F32)
        hs = sb("hs", [128, 16], F32)
        hmax = sb("hmax", [128, 16], F32)
        idf_sb = sb("idf_sb", [128, 128], F32)
        hn_sb = sb("hn_sb", [16, 1], F32)
        tp_ps = ps("tp_ps", [16, 128], F32)
        ld = sem("ld"); vs = sem("vs"); vf = sem("vf")
        sc = sem("sc"); tt = sem("tt"); st = sem("st")
        block = es.enter_context(nc.Block())
        @block.sync
        def _(s):
            C.inc(s.dma_start(out=idf_sb[:], in_=idf), ld, 16)
            for i in range(NTILE):
                b = (i % 2) * D
                if i >= 2:
                    s.wait_ge(vs, 3 * (i - 2) + 1)
                C.inc(s.dma_start(out=xt[:, b:b + D], in_=x3[i]), ld, 16)

        @block.vector
        def _(v):
            v.memset(hmax[:], 0.0)
            v.memset(epsb[:], EPS_LN)
            for i in range(NTILE):
                b = (i % 2) * D
                v.wait_ge(ld, 16 * (i + 2))
                if i >= 1:
                    v.wait_ge(sc, 3 * (i - 1) + 3)
                C.fence(v, v.reduce_sum(m[:], xt[:, b:b + D], axis=AX.X), vf)
                C.fence(v, v.tensor_scalar_mul(m[:], m[:], 1.0 / D), vf)
                C.inc(v.tensor_scalar_sub(xm[:], xt[:, b:b + D], m[:]), vs, 1)
                v.wait_ge(sc, 3 * i + 2)
                C.fence(v, v.reciprocal(rstd[:], sd[:]), vf)
                i_r2 = v.tensor_scalar(rstd2[:], rstd[:], rstd[:], None, op0=ALU.mult)
                C.inc(i_r2, vs, 1)
                v.wait_ge(vs, C.val(vs))
                C.fence(v, v.reduce_sum(hs[:], sq[:].rearrange("p (h k) -> p h k", k=128), axis=AX.X), vf)
                C.fence(v, v.tensor_scalar_mul(hs[:], hs[:], rstd2[:]), vf)
                i_mx = v.tensor_max(hmax[:], hmax[:], hs[:])
                C.inc(i_mx, vs, 1)
                v.wait_ge(vs, C.val(vs))
            v.wait_ge(tt, 1)
            C.inc(v.reduce_max(hn_sb[:], tp_ps[:], axis=AX.X), vs, 1)

        @block.scalar
        def _(s):
            for i in range(NTILE):
                b = (i % 2) * D
                s.wait_ge(vs, 3 * i + 1)
                C.inc(s.activation(out=sq[:], in_=xm[:], func=AF.Square, accum_out=v2[:]), sc, 1)
                s.wait_ge(sc, C.val(sc))
                C.inc(s.activation(out=sd[:], in_=v2[:], func=AF.Sqrt, scale=1.0 / D, bias=epsb[:, 0:1]), sc, 1)
                s.wait_ge(vs, 3 * i + 2)
                if i >= 2:
                    s.wait_ge(st, 16 * (i - 1))
                C.inc(s.activation(out=lnb[:, b:b + D], in_=xm[:], func=AF.Identity, scale=rstd[:], bias=0.0), sc, 1)

        @block.tensor
        def _(t):
            t.wait_ge(vs, 3 * NTILE)
            t.wait_ge(ld, 16)
            C.inc(t.transpose(tp_ps[:], hmax[:, 0:16], idf_sb[:]), tt, 1)

        @block.gpsimd
        def _(g):
            for i in range(NTILE):
                g.wait_ge(sc, 3 * i + 3)
                b = (i % 2) * D
                C.inc(g.dma_start(out=ln3[i], in_=lnb[:, b:b + D]), st, 16)
            g.wait_ge(vs, 3 * NTILE + 1)
            C.inc(g.dma_start(out=hn, in_=hn_sb[:]), st, 16)
            g.wait_ge(st, 16 * (NTILE + 1))
    return nc


def run_ln0(x_full):
    nc = build_ln0()
    cns = _consts()
    in_maps = [{"x": np.ascontiguousarray(x_full[c * PTOK:(c + 1) * PTOK]),
                "ident_f32": cns["ident_f32"]} for c in range(NCORES)]
    res = run_bass_kernel_spmd(nc, in_maps, core_ids=list(range(NCORES)))
    ln = np.concatenate([res.results[c]["ln"] for c in range(NCORES)], 0)
    hn = np.stack([res.results[c]["hn"][:, 0] for c in range(NCORES)], 0)
    return ln, hn



def build_attn0():
    nc = bass.Bass("TRN2", target_bir_lowering=False)
    lnh = nc.dram_tensor("lnh", [4, 4096, 128], BF16, kind="ExternalInput").ap()
    hn_sel = nc.dram_tensor("hn_sel", [4, 4], F32, kind="ExternalInput").ap()
    masks = nc.dram_tensor("masks", [4, 128, 512], BF16, kind="ExternalInput").ap()
    idb = nc.dram_tensor("ident_bf", [128, 128], BF16, kind="ExternalInput").ap()
    idf = nc.dram_tensor("ident_f32", [128, 128], F32, kind="ExternalInput").ap()
    onescol = nc.dram_tensor("ones_col_bf", [128, 1], BF16, kind="ExternalInput").ap()
    onesrow = nc.dram_tensor("ones_row_f32", [1, 128], F32, kind="ExternalInput").ap()
    attn = nc.dram_tensor("attn", [4, 4096, 128], F32, kind="ExternalOutput").ap()

    SC = 1.0 / float(np.sqrt(128.0))
    steps = []
    for p in range(4):
        for qg in range(8):
            nk = 4 * (qg + 1)
            for k in range(nk):
                steps.append((p, qg, k, k == 0, k == nk - 1,
                              (k - 4 * qg) if k >= 4 * qg else None))
    NSTEP = len(steps)
    nmask_upto = []
    nm = 0
    for s in steps:
        if s[5] is not None:
            nm += 1
        nmask_upto.append(nm)

    C = Ctr()
    with contextlib.ExitStack() as es:
        sb = lambda n, s, d: es.enter_context(nc.sbuf_tensor(n, s, d))
        ps = lambda n, s, d: es.enter_context(nc.psum_tensor(n, s, d))
        sem = lambda n: es.enter_context(nc.semaphore(n))
        lnh_sb = [sb(f"lnh{p}", [128, 4096], BF16) for p in range(4)]
        lnfm = [sb(f"lnfm{p}", [128, 4096], BF16) for p in range(4)]
        mask_sb = sb("mask_sb", [128, 4 * 512], BF16)
        idb_sb = sb("idb_sb", [128, 128], BF16)
        idf_sb = sb("idf_sb", [128, 128], F32)
        ones_c = sb("ones_c", [128, 1], BF16)
        ones_r = sb("ones_r", [1, 128], F32)
        hns_sb = sb("hns_sb", [4, 4], F32)
        mb4 = sb("mb4", [4, 1], F32)
        negm4 = sb("negm4", [4, 1], F32)
        negmrow = sb("negmrow", [1, 4], F32)
        negMb = [sb(f"negMb{p}", [128, 1], F32) for p in range(4)]
        pt_sb = sb("pt_sb", [128, 2 * 512], BF16)
        sums_sb = sb("sums_sb", [1, 512], F32)
        recip_sb = sb("recip_sb", [1, 512], F32)
        sums_cols = sb("sums_cols", [128, 4], F32)
        acc_sb = sb("acc_sb", [128, 512], F32)
        out_sb = sb("out_sb", [128, 512], F32)
        st_ps = ps("st_ps", [128, 2 * 512], F32)
        acc_ps = ps("acc_ps", [128, 512], F32)
        sum_ps = ps("sum_ps", [1, 512], F32)
        tp_ps = [ps("tp_ps0", [128, 128], BF16), ps("tp_ps1", [128, 128], BF16)]
        att_ps = ps("att_ps", [128, 512], F32)
        mp_ps = ps("mp_ps", [128, 8], F32)
        ld = sem("ld"); tt = sem("tt"); vc = sem("vc"); vf = sem("vf")
        stm = sem("stm"); se = sem("se"); vm = sem("vm"); pv = sem("pv")
        ve = sem("ve"); ta = sem("ta"); so = sem("so")
        block = es.enter_context(nc.Block())

        NLOAD = 10

        @block.sync
        def _(s):
            C.inc(s.dma_start(out=idb_sb[:], in_=idb), ld, 16)
            C.inc(s.dma_start(out=idf_sb[:], in_=idf), ld, 16)
            C.inc(s.dma_start(out=ones_c[:], in_=onescol), ld, 16)
            C.inc(s.dma_start(out=ones_r[:], in_=onesrow), ld, 16)
            C.inc(s.dma_start(out=hns_sb[:], in_=hn_sel), ld, 16)
            C.inc(s.dma_start(out=mask_sb[:].rearrange("p (m q) -> p m q", q=512),
                              in_=masks.rearrange("m p q -> p m q")), ld, 16)
            for p in range(4):
                C.inc(s.dma_start(out=lnh_sb[p][:].rearrange("p (t d) -> p t d", d=128),
                                  in_=lnh[p].rearrange("(t p) d -> p t d", p=128)), ld, 16)

        @block.tensor
        def _(t):
            t.wait_ge(ld, 16 * NLOAD)
            t.wait_ge(vc, 1)
            C.inc(t.transpose(mp_ps[0:1, 0:4], negm4[:], idf_sb[0:4, 0:4]), tt, 1)
            t.wait_ge(vc, 2)
            C.inc(t.matmul(mp_ps[:, 4:8], ones_r[:], negmrow[:], start=True, stop=True), tt, 1)
            for i in range(128):
                p, tl = divmod(i, 32)
                if i >= 2:
                    t.wait_ge(vc, 6 + (i - 1))
                C.inc(t.transpose(tp_ps[i % 2][:], lnh_sb[p][:, tl * 128:(tl + 1) * 128],
                                  idb_sb[:]), tt, 1)
            for si, (p, qg, k, kf, kl, mj) in enumerate(steps):
                e = p * 8 + qg
                if kf:
                    t.wait_ge(vc, 6 + 128)
                    if e >= 1:
                        t.wait_ge(ve, 5 * (e - 1) + 4)
                if si >= 2:
                    t.wait_ge(se, si - 1)
                C.inc(t.matmul(st_ps[:, (si % 2) * 512:(si % 2 + 1) * 512],
                               lnfm[p][:, k * 128:(k + 1) * 128],
                               lnfm[p][:, qg * 512:(qg + 1) * 512],
                               start=True, stop=True), stm, 1)
                if mj is not None:
                    t.wait_ge(vm, nmask_upto[si])
                else:
                    t.wait_ge(se, si + 1)
                t.matmul(acc_ps[:], lnh_sb[p][:, k * 128:(k + 1) * 128],
                         pt_sb[:, (si % 2) * 512:(si % 2 + 1) * 512],
                         start=kf, stop=kl)
                C.inc(t.matmul(sum_ps[:], ones_c[:],
                               pt_sb[:, (si % 2) * 512:(si % 2 + 1) * 512],
                               start=kf, stop=kl), pv, 1)
                if kl:
                    t.wait_ge(ve, 5 * e + 2)
                    for j in range(4):
                        C.inc(t.transpose(att_ps[:, j:j + 1],
                                          recip_sb[0:1, j * 128:(j + 1) * 128],
                                          idf_sb[0:1, 0:1]), ta, 1)
                    t.wait_ge(ve, 5 * e + 4)
                    for j in range(4):
                        C.inc(t.transpose(att_ps[:, j * 128:(j + 1) * 128],
                                          acc_sb[:, j * 128:(j + 1) * 128],
                                          idf_sb[:]), ta, 1)

        @block.vector
        def _(v):
            v.wait_ge(ld, 16 * NLOAD)
            C.fence(v, v.reduce_max(mb4[:], hns_sb[:], axis=AX.X), vf)
            C.inc(v.tensor_scalar_mul(negm4[:], mb4[:], -SC), vc, 1)
            v.wait_ge(tt, 1)
            C.inc(v.tensor_copy(negmrow[:], mp_ps[0:1, 0:4]), vc, 1)
            v.wait_ge(tt, 2)
            for j in range(4):
                C.inc(v.tensor_copy(negMb[j][:], mp_ps[:, 4 + j:5 + j]), vc, 1)
            for i in range(128):
                p, tl = divmod(i, 32)
                v.wait_ge(tt, 2 + i + 1)
                C.inc(v.tensor_copy(lnfm[p][:, tl * 128:(tl + 1) * 128],
                                    tp_ps[i % 2][:]), vc, 1)
            for si, (p, qg, k, kf, kl, mj) in enumerate(steps):
                e = p * 8 + qg
                if mj is not None:
                    v.wait_ge(se, si + 1)
                    C.inc(v.tensor_tensor(pt_sb[:, (si % 2) * 512:(si % 2 + 1) * 512],
                                          pt_sb[:, (si % 2) * 512:(si % 2 + 1) * 512],
                                          mask_sb[:, mj * 512:(mj + 1) * 512],
                                          op=ALU.mult), vm, 1)
                if kl:
                    v.wait_ge(pv, si + 1)
                    C.inc(v.tensor_copy(sums_sb[:], sum_ps[:]), ve, 1)
                    v.wait_ge(ve, C.val(ve))
                    C.inc(v.reciprocal(recip_sb[:], sums_sb[:]), ve, 1)
                    v.wait_ge(ta, 8 * e + 4)
                    C.inc(v.tensor_copy(sums_cols[:], att_ps[:, 0:4]), ve, 1)
                    v.wait_ge(ve, C.val(ve))
                    C.inc(v.tensor_copy(acc_sb[:], acc_ps[:]), ve, 1)
                    v.wait_ge(ta, 8 * e + 8)
                    if e >= 1:
                        v.wait_ge(so, 16 * e)
                    for j in range(4):
                        i_ts = v.tensor_scalar_mul(out_sb[:, j * 128:(j + 1) * 128],
                                                   att_ps[:, j * 128:(j + 1) * 128],
                                                   sums_cols[:, j:j + 1])
                        if j == 3:
                            C.inc(i_ts, ve, 1)

        @block.scalar
        def _(s):
            s.wait_ge(vc, 6)
            for si, (p, qg, k, kf, kl, mj) in enumerate(steps):
                s.wait_ge(stm, si + 1)
                if si >= 2:
                    s.wait_ge(pv, si - 1)
                C.inc(s.activation(out=pt_sb[:, (si % 2) * 512:(si % 2 + 1) * 512],
                                   in_=st_ps[:, (si % 2) * 512:(si % 2 + 1) * 512],
                                   func=AF.Exp, scale=SC, bias=negMb[p][:]), se, 1)

        @block.gpsimd
        def _(g):
            for e in range(32):
                p, qg = divmod(e, 8)
                g.wait_ge(ve, 5 * e + 5)
                C.inc(g.dma_start(
                    out=attn[p].rearrange("(g j p) d -> g p j d", p=128, j=4)[qg],
                    in_=out_sb[:].rearrange("p (j d) -> p j d", d=128)), so, 16)
            g.wait_ge(so, 16 * 32)
    return nc



def build_resid_ln(with_ssum=True):
    nc = bass.Bass("TRN2", target_bir_lowering=False)
    x = nc.dram_tensor("x", [PTOK, D], F32, kind="ExternalInput").ap()
    add = nc.dram_tensor("add", [PTOK, D], F32, kind="ExternalInput").ap()
    onescol = nc.dram_tensor("ones_col_bf", [128, 1], BF16, kind="ExternalInput").ap()
    x1o = nc.dram_tensor("x1", [PTOK, D], F32, kind="ExternalOutput").ap()
    lno = nc.dram_tensor("ln", [PTOK, D], BF16, kind="ExternalOutput").ap()
    sso = nc.dram_tensor("ssum", [1, D], F32, kind="ExternalOutput").ap()

    x3 = x.rearrange("(t p) d -> t p d", p=128)
    a3 = add.rearrange("(t p) d -> t p d", p=128)
    x13 = x1o.rearrange("(t p) d -> t p d", p=128)
    ln3 = lno.rearrange("(t p) d -> t p d", p=128)
    C = Ctr()
    with contextlib.ExitStack() as es:
        sb = lambda n, s, d: es.enter_context(nc.sbuf_tensor(n, s, d))
        ps = lambda n, s, d: es.enter_context(nc.psum_tensor(n, s, d))
        sem = lambda n: es.enter_context(nc.semaphore(n))
        xt = sb("xt", [128, 2 * D], F32)
        at = sb("at", [128, 2 * D], F32)
        x1b = sb("x1b", [128, 2 * D], F32)
        xm = sb("xm", [128, D], F32)
        sq = sb("sq", [128, D], F32)
        lnb = sb("lnb", [128, 2 * D], BF16)
        m = sb("lm", [128, 1], F32)
        v2 = sb("lv2", [128, 1], F32)
        sd = sb("lsd", [128, 1], F32)
        rstd = sb("lrstd", [128, 1], F32)
        epsb = sb("epsb", [128, 1], F32)
        ones_c = sb("ones_c", [128, 1], BF16)
        ss_sb = sb("ss_sb", [1, D], F32)
        ss_ps = ps("ss_ps", [1, D], F32)
        ld = sem("ld"); vs = sem("vs"); vx = sem("vx"); vf = sem("vf")
        sc = sem("sc"); tm = sem("tm"); st = sem("st")
        block = es.enter_context(nc.Block())

        @block.sync
        def _(s):
            C.inc(s.dma_start(out=ones_c[:], in_=onescol), ld, 16)
            for i in range(NTILE):
                b = (i % 2) * D
                if i >= 2:
                    s.wait_ge(vx, i - 1)
                C.inc(s.dma_start(out=xt[:, b:b + D], in_=x3[i]), ld, 16)
                C.inc(s.dma_start(out=at[:, b:b + D], in_=a3[i]), ld, 16)

        @block.vector
        def _(v):
            v.memset(epsb[:], EPS_LN)
            for i in range(NTILE):
                b = (i % 2) * D
                v.wait_ge(ld, 16 * (2 * i + 3))
                if i >= 2:
                    v.wait_ge(st, 16 * (2 * (i - 2) + 1))
                C.inc(v.tensor_add(x1b[:, b:b + D], xt[:, b:b + D], at[:, b:b + D]), vx, 1)
                C.fence(v, v.reduce_sum(m[:], x1b[:, b:b + D], axis=AX.X), vf)
                C.fence(v, v.tensor_scalar_mul(m[:], m[:], 1.0 / D), vf)
                if i >= 1:
                    v.wait_ge(sc, 3 * (i - 1) + 3)
                    if with_ssum:
                        v.wait_ge(tm, i)
                C.inc(v.tensor_scalar_sub(xm[:], x1b[:, b:b + D], m[:]), vs, 1)
                v.wait_ge(sc, 3 * i + 2)
                C.inc(v.reciprocal(rstd[:], sd[:]), vs, 1)
            if with_ssum:
                v.wait_ge(tm, NTILE)
                C.inc(v.tensor_copy(ss_sb[:], ss_ps[:]), vs, 1)

        @block.scalar
        def _(s):
            for i in range(NTILE):
                b = (i % 2) * D
                s.wait_ge(vs, 2 * i + 1)
                if i >= 2:
                    s.wait_ge(tm, i - 1)
                    s.wait_ge(st, 16 * (2 * (i - 2) + 2))
                C.inc(s.activation(out=sq[:], in_=xm[:], func=AF.Square, accum_out=v2[:]), sc, 1)
                s.wait_ge(sc, C.val(sc))
                C.inc(s.activation(out=sd[:], in_=v2[:], func=AF.Sqrt, scale=1.0 / D,
                                   bias=epsb[:, 0:1]), sc, 1)
                s.wait_ge(vs, 2 * i + 2)
                C.inc(s.activation(out=lnb[:, b:b + D], in_=xm[:], func=AF.Identity,
                                   scale=rstd[:], bias=0.0), sc, 1)

        @block.tensor
        def _(t):
            if with_ssum:
                for i in range(NTILE):
                    t.wait_ge(vs, 2 * i + 2)
                    ii = None
                    for j in range(4):
                        ii = t.matmul(ss_ps[0:1, j * 512:(j + 1) * 512], rstd[:],
                                      xm[:, j * 512:(j + 1) * 512],
                                      start=(i == 0), stop=(i == NTILE - 1))
                    C.inc(ii, tm, 1)

        @block.gpsimd
        def _(g):
            for i in range(NTILE):
                b = (i % 2) * D
                g.wait_ge(vx, i + 1)
                C.inc(g.dma_start(out=x13[i], in_=x1b[:, b:b + D]), st, 16)
                g.wait_ge(sc, 3 * i + 3)
                if with_ssum:
                    g.wait_ge(tm, i + 1)
                C.inc(g.dma_start(out=ln3[i], in_=lnb[:, b:b + D]), st, 16)
            if with_ssum:
                g.wait_ge(vs, 2 * NTILE + 1)
                C.inc(g.dma_start(out=sso, in_=ss_sb[:]), st, 16)
            g.wait_ge(st, C.val(st))
    return nc


def run_resid_ln(x_full, add_full, with_ssum=True):
    nc = build_resid_ln(with_ssum)
    ones = np.ones((128, 1), dtype=BF)
    in_maps = [{"x": np.ascontiguousarray(x_full[c * PTOK:(c + 1) * PTOK]),
                "add": np.ascontiguousarray(add_full[c * PTOK:(c + 1) * PTOK]),
                "ones_col_bf": ones} for c in range(NCORES)]
    res = run_bass_kernel_spmd(nc, in_maps, core_ids=list(range(NCORES)))
    x1 = np.concatenate([res.results[c]["x1"] for c in range(NCORES)], 0)
    ln = np.concatenate([res.results[c]["ln"] for c in range(NCORES)], 0)
    ss = np.stack([res.results[c]["ssum"][0] for c in range(NCORES)], 0)
    return x1, ln, ss



class Plan:

    def __init__(self):
        self.ops = []

    def add(self, eng, fn, inc=1, chain=False):
        self.ops.append((eng, fn, inc, chain))

    def ch(self, eng, fn):
        self.add(eng, fn, 0, chain=True)

    def dma(self, eng, chain=False, **kw):
        self.add(eng, lambda e: e.dma_start(**kw), 16, chain)

    def emit(self, block, nc, gs, engines=("sync", "tensor", "vector", "scalar", "gpsimd")):
        pre = [0]
        for (_, _, inc, _) in self.ops:
            pre.append(pre[-1] + inc)

        def mk(name):
            def closure(eng):
                for idx, (ename, fn, inc, chain) in enumerate(self.ops):
                    if ename == name:
                        if not chain:
                            if pre[idx] > 0:
                                eng.wait_ge(gs, pre[idx])
                            instr = fn(eng)
                            if inc:
                                instr.then_inc(gs, inc)
                        else:
                            instr = fn(eng)
                            if inc:
                                instr.then_inc(gs, inc)
                if name == "gpsimd":
                    eng.wait_ge(gs, pre[-1])
            return closure

        for name in engines:
            getattr(block, name)(mk(name))



def build_ffn0():
    nc = bass.Bass("TRN2", target_bir_lowering=False)
    x1 = nc.dram_tensor("x1", [PTOK, D], F32, kind="ExternalInput").ap()
    ln1 = nc.dram_tensor("ln1", [PTOK, D], BF16, kind="ExternalInput").ap()
    ssall = nc.dram_tensor("ssall", [8, D], F32, kind="ExternalInput").ap()
    sim = nc.dram_tensor("sim", [D, 32], F32, kind="ExternalInput").ap()
    gates = nc.dram_tensor("gates", [1, 32], F32, kind="ExternalInput").ap()
    w_v = nc.dram_tensor("w_v", [32, 64 * D], F32, kind="ExternalInput").ap()
    o_w = nc.dram_tensor("o_w", [32, 64 * D], F32, kind="ExternalInput").ap()
    bselt = nc.dram_tensor("bsel", [8, 2], F32, kind="ExternalInput").ap()
    wselt = nc.dram_tensor("wsel", [1, 2], F32, kind="ExternalInput").ap()
    idf = nc.dram_tensor("ident_f32", [128, 128], F32, kind="ExternalInput").ap()
    idb = nc.dram_tensor("ident_bf", [128, 128], BF16, kind="ExternalInput").ap()
    onesr = nc.dram_tensor("ones_row_f32", [1, 128], F32, kind="ExternalInput").ap()
    onesc32 = nc.dram_tensor("ones_col_f32", [128, 1], F32, kind="ExternalInput").ap()
    x2o = nc.dram_tensor("x2", [PTOK, D], F32, kind="ExternalOutput").ap()

    wv_dyn = nc.dram_tensor("wv_dyn", [2, 64 * D], BF16)
    wo_dyn = nc.dram_tensor("wo_dyn", [2, 64 * D], BF16)
    wv3 = wv_dyn.rearrange("b (h c) -> b h c", c=D)
    wo3 = wo_dyn.rearrange("b (h c) -> b h c", c=D)

    x13 = x1.rearrange("(t p) d -> t p d", p=128)
    x23 = x2o.rearrange("(t p) d -> t p d", p=128)

    with contextlib.ExitStack() as es:
        sb = lambda n, s, d: es.enter_context(nc.sbuf_tensor(n, s, d))
        ps = lambda n, s, d: es.enter_context(nc.psum_tensor(n, s, d))
        sem = lambda n: es.enter_context(nc.semaphore(n))
        ss_sb = sb("ss_sb", [8, D], F32)
        bsel_sb = sb("bsel_sb", [8, 2], F32)
        wsel_sb = sb("wsel_sb", [1, 2], F32)
        sim_sb = sb("sim_sb", [128, 16 * 32], F32)
        simsq = sb("simsq", [128, 16 * 32], F32)
        srep = sb("srep", [2, D], F32)
        srsq = sb("srsq", [2, D], F32)
        srf = sb("srf", [128, 32], F32)
        gsg = sb("gsg", [1, 32], F32)
        sgrow = sb("sgrow", [1, 32], F32)
        s2 = sb("s2", [2, 1], F32)
        snorm = sb("snorm", [2, 1], F32)
        rn = sb("rn", [2, 1], F32)
        cn2 = sb("cn2", [1, 32], F32)
        cnq = sb("cnq", [1, 32], F32)
        rcn = sb("rcn", [1, 32], F32)
        rcn2 = sb("rcn2", [2, 32], F32)
        sg2 = sb("sg2", [2, 32], F32)
        aff = sb("aff", [2, 32], F32)
        logit = sb("logit", [2, 32], F32)
        gated = sb("gated", [2, 32], F32)
        maskt = sb("maskt", [2, 32], F32)
        msum = sb("msum", [2, 1], F32)
        inact = sb("inact", [2, 1], F32)
        top8 = sb("top8", [2, 8], F32)
        m8 = sb("m8", [2, 1], F32)
        tmpa = sb("tmpa", [2, 32], F32)
        m16 = sb("m16", [2, 1], F32)
        fbm = sb("fbm", [2, 32], F32)
        selm = sb("selm", [2, 32], F32)
        ml = sb("ml", [2, 32], F32)
        rmax = sb("rmax", [2, 1], F32)
        nrmax = sb("nrmax", [2, 1], F32)
        esf = sb("esf", [2, 32], F32)
        essum = sb("essum", [2, 1], F32)
        rsum = sb("rsum", [2, 1], F32)
        routing = sb("routing", [2, 32], F32)
        routm = sb("routm", [2, 32], F32)
        rt_bf = sb("rt_bf", [32, 2], BF16)
        ro_bf = sb("ro_bf", [32, 2], BF16)
        wvf = sb("wvf", [32, 2 * D], BF16)
        dyn_sb = sb("dyn_sb", [2, 2 * D], BF16)
        ws0 = sb("ws0", [128, 1], F32)
        ws1 = sb("ws1", [128, 1], F32)
        wvT = sb("wvT", [128, 2 * 16 * 64], BF16)
        wvsel = sb("wvsel", [128, 16 * 64], BF16)
        wvtmp = sb("wvtmp", [128, 16 * 64], BF16)
        wo_sb = sb("wo_sb", [64, 2 * D], BF16)
        wo_sel = sb("wo_sel", [64, D], BF16)
        wo_tmp = sb("wo_tmp", [64, D], BF16)
        ln1_sb = sb("ln1_sb", [128, NTILE * D], BF16)
        lnfm = sb("lnfm", [128, NTILE * D], BF16)
        cb_sb = sb("cb_sb", [128, NTILE * 64], BF16)
        cbT = sb("cbT", [64, PTOK], BF16)
        x1t = sb("x1t", [128, D], F32)
        x2t = sb("x2t", [128, D], F32)
        idf_sb = sb("idf_sb", [128, 128], F32)
        idb_sb = sb("idb_sb", [128, 128], BF16)
        ones_r = sb("ones_r", [1, 128], F32)
        ones_c32 = sb("ones_c32", [128, 1], F32)
        pb = [ps(f"pb{i}", [128, 512], F32) for i in range(5)]
        pbb = [ps(f"pbb{i}", [128, 1024], BF16) for i in range(2)]
        gs = sem("gs")
        block = es.enter_context(nc.Block())

        P = Plan()
        A = P.add
        P.dma("sync", out=ss_sb[:], in_=ssall)
        P.dma("sync", out=sim_sb[:].rearrange("p (c e) -> p c e", e=32),
              in_=sim.rearrange("(c p) e -> p c e", p=128))
        P.dma("sync", out=gsg[:], in_=gates)
        P.dma("sync", out=idf_sb[:], in_=idf)
        P.dma("sync", out=idb_sb[:], in_=idb)
        P.dma("sync", out=ones_r[:], in_=onesr)
        P.dma("sync", out=ones_c32[:], in_=onesc32)
        P.dma("sync", out=bsel_sb[:], in_=bselt)
        P.dma("sync", out=wsel_sb[:], in_=wselt)
        P.dma("sync", out=ln1_sb[:].rearrange("p (t d) -> p t d", d=D),
              in_=ln1.rearrange("(t p) d -> p t d", p=128))
        for j in range(4):
            A("tensor", lambda e, j=j: e.matmul(pb[j][0:2, :], bsel_sb[:],
                                                ss_sb[:, j * 512:(j + 1) * 512],
                                                start=True, stop=True))
        for j in range(4):
            A("vector", lambda e, j=j: e.tensor_scalar_mul(
                srep[0:2, j * 512:(j + 1) * 512], pb[j][0:2, :], 1.0 / 4096.0))
        A("scalar", lambda e: e.activation(out=srsq[:], in_=srep[:], func=AF.Square,
                                           accum_out=s2[:]))
        A("scalar", lambda e: e.activation(out=snorm[:], in_=s2[:], func=AF.Sqrt))
        A("vector", lambda e: e.reciprocal(rn[:], snorm[:]))
        for ch in range(16):
            A("tensor", lambda e, ch=ch: e.transpose(
                pb[4][0:128, 0:2], srep[0:2, ch * 128:(ch + 1) * 128], idf_sb[0:2, 0:2]))
            A("vector", lambda e, ch=ch: e.tensor_copy(
                srf[:, 2 * ch:2 * ch + 2], pb[4][0:128, 0:2]))
        for ch in range(16):
            A("tensor", lambda e, ch=ch: e.matmul(
                pb[0][0:2, 0:32], srf[:, 2 * ch:2 * ch + 2],
                sim_sb[:, ch * 32:(ch + 1) * 32],
                start=(ch == 0), stop=(ch == 15)))
        A("scalar", lambda e: e.activation(out=simsq[:], in_=sim_sb[:], func=AF.Square))
        A("tensor", lambda e: e.matmul(pb[1][0:1, 0:512], ones_c32[:], simsq[:],
                                       start=True, stop=True))
        A("vector", lambda e: e.reduce_sum(
            cn2[:], pb[1][0:1, 0:512].rearrange("p (c e) -> p e c", e=32), axis=AX.X))
        A("scalar", lambda e: e.activation(out=cnq[:], in_=cn2[:], func=AF.Sqrt))
        A("vector", lambda e: e.reciprocal(rcn[:], cnq[:]))
        A("vector", lambda e: e.tensor_copy(aff[:], pb[0][0:2, 0:32]))
        A("vector", lambda e: e.tensor_scalar_mul(aff[:], aff[:], rn[:]))
        A("tensor", lambda e: e.matmul(pb[2][0:2, 0:32], ones_r[0:1, 0:2], rcn[:],
                                       start=True, stop=True))
        A("vector", lambda e: e.tensor_copy(rcn2[:], pb[2][0:2, 0:32]))
        A("vector", lambda e: e.tensor_tensor(aff[:], aff[:], rcn2[:], op=ALU.mult))
        A("scalar", lambda e: e.activation(out=sgrow[:], in_=gsg[:], func=AF.Sigmoid))
        A("tensor", lambda e: e.matmul(pb[2][0:2, 0:32], ones_r[0:1, 0:2], sgrow[:],
                                       start=True, stop=True))
        A("vector", lambda e: e.tensor_copy(sg2[:], pb[2][0:2, 0:32]))
        A("vector", lambda e: e.tensor_sub(logit[:], aff[:], sg2[:]))
        A("vector", lambda e: e.tensor_scalar_max(gated[:], logit[:], 0.0))
        A("vector", lambda e: e.tensor_scalar(maskt[:], logit[:], 0.0, None, op0=ALU.is_gt))
        A("vector", lambda e: e.reduce_sum(msum[:], maskt[:], axis=AX.X))
        A("vector", lambda e: e.tensor_scalar(inact[:], msum[:], 0.0, None, op0=ALU.is_equal))
        A("vector", lambda e: e.max(top8[:], logit[:]))
        A("vector", lambda e: e.tensor_copy(m8[:], top8[:, 7:8]))
        A("vector", lambda e: e.tensor_scalar(tmpa[:], logit[:], m8[:], -1e30,
                                              op0=ALU.is_ge, op1=ALU.mult))
        A("vector", lambda e: e.tensor_add(tmpa[:], logit[:], tmpa[:]))
        A("vector", lambda e: e.max(top8[:], tmpa[:]))
        A("vector", lambda e: e.tensor_copy(m16[:], top8[:, 7:8]))
        A("vector", lambda e: e.tensor_scalar(fbm[:], logit[:], m16[:], None, op0=ALU.is_ge))
        A("vector", lambda e: e.tensor_scalar_mul(fbm[:], fbm[:], inact[:]))
        A("vector", lambda e: e.tensor_max(selm[:], maskt[:], fbm[:]))
        A("vector", lambda e: e.tensor_tensor(ml[:], gated[:], selm[:], op=ALU.mult))
        A("vector", lambda e: e.tensor_scalar(tmpa[:], selm[:], 1e30, -1e30,
                                              op0=ALU.mult, op1=ALU.add))
        A("vector", lambda e: e.tensor_add(ml[:], ml[:], tmpa[:]))
        A("vector", lambda e: e.reduce_max(rmax[:], ml[:], axis=AX.X))
        A("vector", lambda e: e.tensor_scalar_mul(nrmax[:], rmax[:], -1.0))
        A("scalar", lambda e: e.activation(out=esf[:], in_=ml[:], func=AF.Exp,
                                           bias=nrmax[:], accum_out=essum[:]))
        A("vector", lambda e: e.reciprocal(rsum[:], essum[:]))
        A("vector", lambda e: e.tensor_scalar_mul(routing[:], esf[:], rsum[:]))
        A("vector", lambda e: e.tensor_tensor(routm[:], routing[:], selm[:], op=ALU.mult))
        A("tensor", lambda e: e.transpose(pb[2][0:32, 0:2], routm[:], idf_sb[0:2, 0:2]))
        A("vector", lambda e: e.tensor_copy(rt_bf[:], pb[2][0:32, 0:2]))
        A("tensor", lambda e: e.transpose(pb[2][0:32, 0:2], routing[:], idf_sb[0:2, 0:2]))
        A("vector", lambda e: e.tensor_copy(ro_bf[:], pb[2][0:32, 0:2]))
        for wsrc, rt, wdst in ((w_v, rt_bf, wv_dyn), (o_w, ro_bf, wo_dyn)):
            for ch in range(64):
                b = (ch % 2) * D
                P.dma("gpsimd", out=wvf[:, b:b + D],
                      in_=wsrc[:, ch * D:(ch + 1) * D])
                for j in range(4):
                    A("tensor", lambda e, rt=rt, b=b, j=j: e.matmul(
                        pb[j][0:2, :], rt[:], wvf[:, b + j * 512:b + (j + 1) * 512],
                        start=True, stop=True))
                for j in range(4):
                    A("vector", lambda e, b=b, j=j: e.tensor_copy(
                        dyn_sb[0:2, b + j * 512:b + (j + 1) * 512], pb[j][0:2, :]))
                P.dma("sync", out=wdst[:, ch * D:(ch + 1) * D], in_=dyn_sb[0:2, b:b + D])
        for i in range(128):
            t, ch = divmod(i, 16)
            A("tensor", lambda e, t=t, ch=ch, i=i: e.transpose(
                pbb[i % 2][0:128, 0:128],
                ln1_sb[:, t * D + ch * 128:t * D + (ch + 1) * 128], idb_sb[:]))
            A("vector", lambda e, t=t, ch=ch, i=i: e.tensor_copy(
                lnfm[:, (ch * 8 + t) * 128:(ch * 8 + t + 1) * 128], pbb[i % 2][0:128, 0:128]))
        A("tensor", lambda e: e.matmul(pb[2][0:128, 0:2], ones_r[:], wsel_sb[:],
                                       start=True, stop=True))
        A("vector", lambda e: e.tensor_copy(ws0[:], pb[2][0:128, 0:1]))
        A("vector", lambda e: e.tensor_copy(ws1[:], pb[2][0:128, 1:2]))
        for b in range(2):
            for ch in range(16):
                idx = b * 16 + ch
                A("sync", lambda e, b=b, ch=ch, idx=idx: e.dma_start_transpose(
                    wvT[:, idx * 64:(idx + 1) * 64],
                    wv3[b][:, ch * 128:(ch + 1) * 128]), inc=16)
        A("vector", lambda e: e.tensor_scalar_mul(wvtmp[:], wvT[:, 1024:2048], ws1[:]))
        A("vector", lambda e: e.scalar_tensor_tensor(
            wvsel[:], wvT[:, 0:1024], ws0[:], wvtmp[:], op0=ALU.mult, op1=ALU.add))
        P.dma("sync", out=wo_sb[0:64, 0:D], in_=wo3[0])
        P.dma("sync", out=wo_sb[0:64, D:2 * D], in_=wo3[1])
        A("vector", lambda e: e.tensor_scalar_mul(wo_tmp[:], wo_sb[0:64, D:2 * D], ws1[0:64, :]))
        A("vector", lambda e: e.scalar_tensor_tensor(
            wo_sel[:], wo_sb[0:64, 0:D], ws0[0:64, :], wo_tmp[:], op0=ALU.mult, op1=ALU.add))
        for t in range(NTILE):
            for ch in range(16):
                A("tensor", lambda e, t=t, ch=ch: e.matmul(
                    pb[2][0:128, 0:64], lnfm[:, (ch * 8 + t) * 128:(ch * 8 + t + 1) * 128],
                    wvsel[:, ch * 64:(ch + 1) * 64],
                    start=(ch == 0), stop=(ch == 15)))
            A("vector", lambda e, t=t: e.tensor_copy(
                cb_sb[:, t * 64:(t + 1) * 64], pb[2][0:128, 0:64]))
        for t in range(NTILE):
            A("tensor", lambda e, t=t: e.transpose(
                pbb[t % 2][0:64, 0:128], cb_sb[:, t * 64:(t + 1) * 64], idb_sb[:]))
            A("vector", lambda e, t=t: e.tensor_copy(
                cbT[:, t * 128:(t + 1) * 128], pbb[t % 2][0:64, 0:128]))
        for t in range(NTILE):
            P.dma("sync", out=x1t[:], in_=x13[t])
            for j in range(4):
                A("tensor", lambda e, t=t, j=j: e.matmul(
                    pb[3][0:128, :], cbT[:, t * 128:(t + 1) * 128],
                    wo_sel[:, j * 512:(j + 1) * 512], start=True, stop=True))
                A("vector", lambda e, j=j: e.scalar_tensor_tensor(
                    x2t[:, j * 512:(j + 1) * 512], pb[3][0:128, :], 1.0,
                    x1t[:, j * 512:(j + 1) * 512], op0=ALU.mult, op1=ALU.add))
            P.dma("sync", out=x23[t], in_=x2t[:])
        P.emit(block, nc, gs)
    return nc


def run_ffn0(x1, ln1, ss, d):
    nc = build_ffn0()
    bsel = np.zeros((8, 2), dtype=np.float32)
    bsel[0:4, 0] = 1.0
    bsel[4:8, 1] = 1.0
    cns = _attn_consts()
    in_maps = []
    for c in range(NCORES):
        wsel = np.zeros((1, 2), dtype=np.float32)
        wsel[0, c // 4] = 1.0
        in_maps.append({
            "x1": np.ascontiguousarray(x1[c * PTOK:(c + 1) * PTOK]),
            "ln1": np.ascontiguousarray(ln1[c * PTOK:(c + 1) * PTOK]),
            "ssall": ss.astype(np.float32),
            "sim": d["L0_ffn_sim_matrix"].astype(np.float32),
            "gates": d["L0_ffn_gates"].reshape(1, 32).astype(np.float32),
            "w_v": d["L0_ffn_w_v"].reshape(32, 64 * D).astype(np.float32),
            "o_w": d["L0_ffn_o_weights"].reshape(32, 64 * D).astype(np.float32),
            "bsel": bsel, "wsel": wsel,
            "ident_f32": cns["ident_f32"], "ident_bf": cns["ident_bf"],
            "ones_row_f32": cns["ones_row_f32"],
            "ones_col_f32": np.ones((128, 1), dtype=np.float32),
        })
    res = run_bass_kernel_spmd(nc, in_maps, core_ids=list(range(NCORES)))
    x2 = np.concatenate([res.results[c]["x2"] for c in range(NCORES)], 0)
    return x2



NU_F = 768


def build_l1k():
    nc = bass.Bass("TRN2", target_bir_lowering=False)
    x2 = nc.dram_tensor("x2", [PTOK, D], F32, kind="ExternalInput").ap()
    Wk = nc.dram_tensor("Wk", [512, D], F32, kind="ExternalInput").ap()
    bk = nc.dram_tensor("bk", [1, 512], F32, kind="ExternalInput").ap()
    Wv = nc.dram_tensor("Wv", [512, D], F32, kind="ExternalInput").ap()
    bv = nc.dram_tensor("bv", [1, 512], F32, kind="ExternalInput").ap()
    Wmb = nc.dram_tensor("Wmb", [D, D], F32, kind="ExternalInput").ap()
    bmb = nc.dram_tensor("bmb", [1, D], F32, kind="ExternalInput").ap()
    memory = nc.dram_tensor("memory", [16 * 768, 128], F32, kind="ExternalInput").ap()
    norm = nc.dram_tensor("norm", [16, 768], F32, kind="ExternalInput").ap()
    idb = nc.dram_tensor("ident_bf", [128, 128], BF16, kind="ExternalInput").ap()
    idf = nc.dram_tensor("ident_f32", [128, 128], F32, kind="ExternalInput").ap()
    onesrb = nc.dram_tensor("ones_row_bf", [1, 128], BF16, kind="ExternalInput").ap()
    ln2o = nc.dram_tensor("ln2", [PTOK, D], BF16, kind="ExternalOutput").ap()
    dmemo = nc.dram_tensor("dmem", [16 * 768, 128], F32, kind="ExternalOutput").ap()
    dnormo = nc.dram_tensor("dnorm", [16, 768], F32, kind="ExternalOutput").ap()

    x23 = x2.rearrange("(t p) d -> t p d", p=128)
    ln23 = ln2o.rearrange("(t p) d -> t p d", p=128)
    mem3 = memory.rearrange("(h c p) d -> h p c d", p=128, c=6)

    with contextlib.ExitStack() as es:
        sb = lambda n, s, d: es.enter_context(nc.sbuf_tensor(n, s, d))
        ps = lambda n, s, d: es.enter_context(nc.psum_tensor(n, s, d))
        sem = lambda n: es.enter_context(nc.semaphore(n))
        xt = sb("xt", [128, D], F32)
        xm = sb("xm", [128, D], F32)
        m_ = sb("m_", [128, 1], F32)
        v2 = sb("v2", [128, 1], F32)
        sd = sb("sd", [128, 1], F32)
        rstd = sb("rstd", [128, 1], F32)
        epsb = sb("epsb", [128, 1], F32)
        multiuse = sb("multiuse", [128, NTILE * D], BF16)
        ln2fm = sb("ln2fm", [128, NTILE * D], BF16)
        wrow = sb("wrow", [128, 2 * D], BF16)
        wT = sb("wT", [128, 16 * 512], BF16)
        bias_sb = sb("bias_sb", [1, D], BF16)
        ones_rb = sb("ones_rb", [1, 128], BF16)
        idb_sb = sb("idb_sb", [128, 128], BF16)
        idf_sb = sb("idf_sb", [128, 128], F32)
        ktile = sb("ktile", [128, 512], F32)
        a_sb = sb("a_sb", [128, 256], F32)
        mkraw = sb("mkraw", [128, NU_F], F32)
        ss2 = sb("ss2", [128, 1], F32)
        nrm = sb("nrm", [128, 1], F32)
        nmx = sb("nmx", [128, 1], F32)
        rmu = sb("rmu", [128, 1], F32)
        v_sb = sb("v_sb", [128, NTILE * 512], BF16)
        mk_tm = [sb(f"mk{g}", [128, NTILE * NU_F], BF16) for g in range(4)]
        mkfm = sb("mkfm", [128, 6 * PTOK], BF16)
        normsb = sb("normsb", [16, 768], F32)
        normT = sb("normT", [128, 96], BF16)
        mem_sb = sb("mem_sb", [128, 6 * 128], BF16)
        den_sb = sb("den_sb", [1, PTOK], F32)
        den_cols = sb("den_cols", [128, NTILE], F32)
        rden_cols = sb("rden_cols", [128, NTILE], F32)
        coef_cols = sb("coef_cols", [128, NTILE], F32)
        coefb = sb("coefb", [128, NTILE], BF16)
        numsb = sb("numsb", [128, PTOK], F32)
        wmv_sb = sb("wmv_sb", [128, NTILE * 128], BF16)
        dm_sb = sb("dm_sb", [128, 128], F32)
        dn_sb = sb("dn_sb", [1, 768], F32)
        pbf = [ps(f"pbf{i}", [128, 512], F32) for i in range(4)]
        pbb = [ps(f"pbb{i}", [128, 1024], BF16) for i in range(2)]
        gs = sem("gs")
        block = es.enter_context(nc.Block())

        P = Plan()
        A = P.add
        P.dma("sync", out=idb_sb[:], in_=idb)
        P.dma("sync", out=idf_sb[:], in_=idf)
        P.dma("sync", out=ones_rb[:], in_=onesrb)
        P.dma("sync", out=normsb[:], in_=norm)
        A("vector", lambda e: e.memset(epsb[:], EPS_LN))
        for i in range(NTILE):
            P.dma("sync", out=xt[:], in_=x23[i])
            A("vector", lambda e: e.reduce_sum(m_[:], xt[:], axis=AX.X))
            A("vector", lambda e: e.tensor_scalar_mul(m_[:], m_[:], 1.0 / D))
            A("vector", lambda e: e.tensor_scalar_sub(xm[:], xt[:], m_[:]))
            A("scalar", lambda e: e.activation(out=ln2fm[:, 0:D], in_=xm[:],
                                               func=AF.Square, accum_out=v2[:]))
            A("scalar", lambda e: e.activation(out=sd[:], in_=v2[:], func=AF.Sqrt,
                                               scale=1.0 / D, bias=epsb[:, 0:1]))
            A("vector", lambda e: e.reciprocal(rstd[:], sd[:]))
            A("scalar", lambda e, i=i: e.activation(out=multiuse[:, i * D:(i + 1) * D],
                                                    in_=xm[:], func=AF.Identity,
                                                    scale=rstd[:], bias=0.0))
            P.dma("sync", out=ln23[i], in_=multiuse[:, i * D:(i + 1) * D])
        for i in range(128):
            t, ch = divmod(i, 16)
            A("tensor", lambda e, t=t, ch=ch, i=i: e.transpose(
                pbb[i % 2][0:128, 0:128],
                multiuse[:, t * D + ch * 128:t * D + (ch + 1) * 128], idb_sb[:]))
            A("vector", lambda e, t=t, ch=ch, i=i: e.tensor_copy(
                ln2fm[:, (ch * 8 + t) * 128:(ch * 8 + t + 1) * 128],
                pbb[i % 2][0:128, 0:128]))

        def load_wt(Wsrc, j0, npt):
            for jt in range(npt):
                slot = (jt % 2) * D
                P.dma("gpsimd", out=wrow[:, slot:slot + D],
                      in_=Wsrc.rearrange("(t p) d -> t p d", p=128)[j0 + jt])
                for ch in range(16):
                    i = ch
                    A("tensor", lambda e, ch=ch, slot=slot, i=i: e.transpose(
                        pbb[i % 2][0:128, 0:128],
                        wrow[:, slot + ch * 128:slot + (ch + 1) * 128], idb_sb[:]))
                    A("vector", lambda e, ch=ch, jt=jt, i=i: e.tensor_copy(
                        wT[:, ch * 512 + jt * 128:ch * 512 + (jt + 1) * 128],
                        pbb[i % 2][0:128, 0:128]))

        def proj_tile(t, bias_off, jw=512):
            A("tensor", lambda e: e.matmul(pbf[0][0:128, 0:jw], ones_rb[:],
                                           bias_sb[0:1, bias_off:bias_off + jw],
                                           start=True, stop=False))
            for ch in range(16):
                A("tensor", lambda e, ch=ch, t=t: e.matmul(
                    pbf[0][0:128, 0:jw],
                    ln2fm[:, (ch * 8 + t) * 128:(ch * 8 + t + 1) * 128],
                    wT[:, ch * 512:ch * 512 + jw],
                    start=False, stop=(ch == 15)))

        def dpfp_mk(t, g):
            A("scalar", lambda e, g=g: e.activation(out=a_sb[:, 0:128],
                                                    in_=ktile[:, g * 128:(g + 1) * 128],
                                                    func=AF.Relu))
            A("scalar", lambda e, g=g: e.activation(out=a_sb[:, 128:256],
                                                    in_=ktile[:, g * 128:(g + 1) * 128],
                                                    func=AF.Relu, scale=-1.0))
            for u in range(3):
                j = u + 1
                A("vector", lambda e, u=u, j=j: e.tensor_tensor(
                    mkraw[:, u * 256:u * 256 + j], a_sb[:, 0:j],
                    a_sb[:, 256 - j:256], op=ALU.mult))
                A("vector", lambda e, u=u, j=j: e.tensor_tensor(
                    mkraw[:, u * 256 + j:(u + 1) * 256], a_sb[:, j:256],
                    a_sb[:, 0:256 - j], op=ALU.mult))
            A("scalar", lambda e: e.activation(out=numsb[:, 0:NU_F], in_=mkraw[:],
                                               func=AF.Square, accum_out=ss2[:]))
            A("scalar", lambda e: e.activation(out=nrm[:], in_=ss2[:], func=AF.Sqrt))
            A("vector", lambda e: e.tensor_scalar_max(nmx[:], nrm[:], 1e-12))
            A("vector", lambda e: e.reciprocal(rmu[:], nmx[:]))
            A("vector", lambda e, t=t, g=g: e.tensor_scalar_mul(
                mk_tm[g][:, t * NU_F:(t + 1) * NU_F], mkraw[:], rmu[:]))

        P.dma("gpsimd", out=bias_sb[0:1, 0:512], in_=bk)
        load_wt(Wk, 0, 4)
        for t in range(NTILE):
            proj_tile(t, 0)
            A("vector", lambda e: e.tensor_copy(ktile[:], pbf[0][0:128, 0:512]))
            for g in range(4):
                dpfp_mk(t, g)
        P.dma("gpsimd", out=bias_sb[0:1, 0:512], in_=bv)
        load_wt(Wv, 0, 4)
        for t in range(NTILE):
            proj_tile(t, 0)
            A("vector", lambda e, t=t: e.tensor_copy(v_sb[:, t * 512:(t + 1) * 512],
                                                     pbf[0][0:128, 0:512]))
        P.dma("gpsimd", out=bias_sb[:], in_=bmb)
        for jg in range(4):
            load_wt(Wmb, 4 * jg, 4)
            for t in range(NTILE):
                proj_tile(t, jg * 512)
                A("scalar", lambda e, t=t, jg=jg: e.activation(
                    out=multiuse[:, t * D + jg * 512:t * D + (jg + 1) * 512],
                    in_=pbf[0][0:128, 0:512], func=AF.Sigmoid))
        for fc in range(6):
            A("tensor", lambda e, fc=fc: e.transpose(
                pbf[1][0:128, 0:16], normsb[:, fc * 128:(fc + 1) * 128],
                idf_sb[0:16, 0:16]))
            A("vector", lambda e, fc=fc: e.tensor_copy(
                normT[:, fc * 16:(fc + 1) * 16], pbf[1][0:128, 0:16]))
        for g in range(4):
            for i in range(48):
                fc, t = divmod(i, 8)
                A("tensor", lambda e, fc=fc, t=t, g=g, i=i: e.transpose(
                    pbb[i % 2][0:128, 0:128],
                    mk_tm[g][:, t * NU_F + fc * 128:t * NU_F + (fc + 1) * 128],
                    idb_sb[:]))
                A("vector", lambda e, fc=fc, t=t, i=i: e.tensor_copy(
                    mkfm[:, fc * PTOK + t * 128:fc * PTOK + (t + 1) * 128],
                    pbb[i % 2][0:128, 0:128]))
            for h in range(4 * g, 4 * g + 4):
                for half in range(2):
                    for fc in range(6):
                        A("tensor", lambda e, h=h, fc=fc, half=half: e.matmul(
                            pbf[half][0:1, 0:512],
                            normT[:, fc * 16 + h:fc * 16 + h + 1],
                            mkfm[:, fc * PTOK + half * 512:fc * PTOK + (half + 1) * 512],
                            start=(fc == 0), stop=(fc == 5)))
                for half in range(2):
                    A("vector", lambda e, half=half: e.tensor_scalar_add(
                        den_sb[0:1, half * 512:(half + 1) * 512],
                        pbf[half][0:1, 0:512], 1e-8))
                for t in range(NTILE):
                    A("tensor", lambda e, t=t: e.transpose(
                        pbf[2][0:128, 0:1], den_sb[0:1, t * 128:(t + 1) * 128],
                        idf_sb[0:1, 0:1]))
                    A("vector", lambda e, t=t: e.tensor_copy(
                        den_cols[:, t:t + 1], pbf[2][0:128, 0:1]))
                A("vector", lambda e: e.reciprocal(rden_cols[:], den_cols[:]))
                A("vector", lambda e: e.tensor_scalar(coef_cols[:], den_cols[:], -1.0, 1.0,
                                                      op0=ALU.mult, op1=ALU.add))
                A("vector", lambda e: e.tensor_scalar(coef_cols[:], coef_cols[:], 0.0, 1.0,
                                                      op0=ALU.max, op1=ALU.min))
                A("vector", lambda e: e.tensor_copy(coefb[:], coef_cols[:]))
                P.dma("gpsimd", out=mem_sb[:].rearrange("p (c d) -> p c d", d=128),
                      in_=mem3[h])
                for half in range(2):
                    for fc in range(6):
                        A("tensor", lambda e, fc=fc, half=half: e.matmul(
                            pbf[half][0:128, 0:512],
                            mem_sb[:, fc * 128:(fc + 1) * 128],
                            mkfm[:, fc * PTOK + half * 512:fc * PTOK + (half + 1) * 512],
                            start=(fc == 0), stop=(fc == 5)))
                for half in range(2):
                    A("vector", lambda e, half=half: e.tensor_copy(
                        numsb[:, half * 512:(half + 1) * 512], pbf[half][0:128, 0:512]))
                for t in range(NTILE):
                    A("tensor", lambda e, t=t: e.transpose(
                        pbf[2][0:128, 0:128], numsb[:, t * 128:(t + 1) * 128],
                        idf_sb[:]))
                    A("vector", lambda e, t=t: e.tensor_scalar_mul(
                        dm_sb[:], pbf[2][0:128, 0:128], rden_cols[:, t:t + 1]))
                    A("vector", lambda e, t=t, g=g: e.tensor_sub(
                        dm_sb[:], v_sb[:, t * 512 + g * 128:t * 512 + (g + 1) * 128],
                        dm_sb[:]))
                    A("vector", lambda e, t=t, h=h: e.tensor_tensor(
                        wmv_sb[:, t * 128:(t + 1) * 128], dm_sb[:],
                        multiuse[:, t * D + h * 128:t * D + (h + 1) * 128], op=ALU.mult))
                for fc in range(6):
                    for t in range(NTILE):
                        A("tensor", lambda e, fc=fc, t=t, g=g: e.matmul(
                            pbf[2][0:128, 0:128],
                            mk_tm[g][:, t * NU_F + fc * 128:t * NU_F + (fc + 1) * 128],
                            wmv_sb[:, t * 128:(t + 1) * 128],
                            start=(t == 0), stop=(t == NTILE - 1)))
                    A("vector", lambda e: e.tensor_copy(dm_sb[:], pbf[2][0:128, 0:128]))
                    P.dma("sync", out=dmemo[h * 768 + fc * 128:h * 768 + (fc + 1) * 128, :],
                          in_=dm_sb[:])
                for t in range(NTILE):
                    A("tensor", lambda e, t=t, g=g: e.matmul(
                        pbf[3][0:1, 0:512], coefb[:, t:t + 1],
                        mk_tm[g][:, t * NU_F:t * NU_F + 512],
                        start=(t == 0), stop=(t == NTILE - 1)))
                    A("tensor", lambda e, t=t, g=g: e.matmul(
                        pbf[2][0:1, 0:256], coefb[:, t:t + 1],
                        mk_tm[g][:, t * NU_F + 512:t * NU_F + NU_F],
                        start=(t == 0), stop=(t == NTILE - 1)))
                A("vector", lambda e: e.tensor_copy(dn_sb[0:1, 0:512], pbf[3][0:1, 0:512]))
                A("vector", lambda e: e.tensor_copy(dn_sb[0:1, 512:768],
                                                    pbf[2][0:1, 0:256]))
                P.dma("sync", out=dnormo[h:h + 1, :], in_=dn_sb[:])
        P.emit(block, nc, gs)
    return nc


def run_l1k(x2, d):
    nc = build_l1k()
    cns = _attn_consts()
    in_maps = []
    for c in range(NCORES):
        in_maps.append({
            "x2": np.ascontiguousarray(x2[c * PTOK:(c + 1) * PTOK]),
            "Wk": d["L1_attn_Wk"].astype(np.float32),
            "bk": d["L1_attn_bk"].reshape(1, 512).astype(np.float32),
            "Wv": d["L1_attn_Wv"].astype(np.float32),
            "bv": d["L1_attn_bv"].reshape(1, 512).astype(np.float32),
            "Wmb": d["L1_attn_Wmb"].astype(np.float32),
            "bmb": d["L1_attn_bmb"].reshape(1, D).astype(np.float32),
            "memory": d["L1_attn_memory"].reshape(16 * 768, 128).astype(np.float32),
            "norm": d["L1_attn_norm"].astype(np.float32),
            "ident_bf": cns["ident_bf"], "ident_f32": cns["ident_f32"],
            "ones_row_bf": np.ones((1, 128), dtype=BF),
        })
    res = run_bass_kernel_spmd(nc, in_maps, core_ids=list(range(NCORES)))
    ln2 = np.concatenate([res.results[c]["ln2"] for c in range(NCORES)], 0)
    dmem = np.stack([res.results[c]["dmem"] for c in range(NCORES)], 0)
    dnorm = np.stack([res.results[c]["dnorm"] for c in range(NCORES)], 0)
    return ln2, dmem, dnorm



def build_l1q():
    nc = bass.Bass("TRN2", target_bir_lowering=False)
    x2 = nc.dram_tensor("x2", [PTOK, D], F32, kind="ExternalInput").ap()
    ln2 = nc.dram_tensor("ln2", [PTOK, D], BF16, kind="ExternalInput").ap()
    dmem = nc.dram_tensor("dmem", [8, 16 * 768, 128], F32, kind="ExternalInput").ap()
    dnorm = nc.dram_tensor("dnorm", [8, 16, 768], F32, kind="ExternalInput").ap()
    memory = nc.dram_tensor("memory", [16 * 768, 128], F32, kind="ExternalInput").ap()
    norm = nc.dram_tensor("norm", [16, 768], F32, kind="ExternalInput").ap()
    Wq = nc.dram_tensor("Wq", [D, D], F32, kind="ExternalInput").ap()
    bq = nc.dram_tensor("bq", [1, D], F32, kind="ExternalInput").ap()
    Wg = nc.dram_tensor("Wg", [D, D], F32, kind="ExternalInput").ap()
    bg = nc.dram_tensor("bg", [1, D], F32, kind="ExternalInput").ap()
    idb = nc.dram_tensor("ident_bf", [128, 128], BF16, kind="ExternalInput").ap()
    idf = nc.dram_tensor("ident_f32", [128, 128], F32, kind="ExternalInput").ap()
    onesrb = nc.dram_tensor("ones_row_bf", [1, 128], BF16, kind="ExternalInput").ap()
    x3o = nc.dram_tensor("x3", [PTOK, D], F32, kind="ExternalOutput").ap()

    x23 = x2.rearrange("(t p) d -> t p d", p=128)
    ln23 = ln2.rearrange("(t p) d -> t p d", p=128)
    x33 = x3o.rearrange("(t p) d -> t p d", p=128)
    dm4 = dmem.rearrange("r (h c p) d -> r h p c d", p=128, c=6)
    mem3 = memory.rearrange("(h c p) d -> h p c d", p=128, c=6)

    with contextlib.ExitStack() as es:
        sb = lambda n, s, d: es.enter_context(nc.sbuf_tensor(n, s, d))
        ps = lambda n, s, d: es.enter_context(nc.psum_tensor(n, s, d))
        sem = lambda n: es.enter_context(nc.semaphore(n))
        ln2sb = sb("ln2sb", [128, NTILE * D], BF16)
        ln2fm = sb("ln2fm", [128, NTILE * D], BF16)
        q_sb = sb("q_sb", [128, NTILE * D], BF16)
        gate_sb = sb("gate_sb", [128, NTILE * D], BF16)
        newmem = sb("newmem", [128, 16 * 6 * 128], BF16)
        dacc = sb("dacc", [128, 768], F32)
        nrm_sb = sb("nrm_sb", [16, 768], F32)
        nrm_acc = sb("nrm_acc", [16, 768], F32)
        newnormT = sb("newnormT", [128, 96], BF16)
        wrow = sb("wrow", [128, 2 * D], BF16)
        wT = sb("wT", [128, 16 * 512], BF16)
        bias_sb = sb("bias_sb", [1, D], BF16)
        ones_rb = sb("ones_rb", [1, 128], BF16)
        idb_sb = sb("idb_sb", [128, 128], BF16)
        idf_sb = sb("idf_sb", [128, 128], F32)
        a_sb = sb("a_sb", [128, 256], F32)
        mkraw = sb("mkraw", [128, NU_F], F32)
        mkb = sb("mkb", [128, NU_F], BF16)
        mqfm = sb("mqfm", [128, NU_F], BF16)
        ss2 = sb("ss2", [128, 1], F32)
        nrm1 = sb("nrm1", [128, 1], F32)
        nmx = sb("nmx", [128, 1], F32)
        rmu = sb("rmu", [128, 1], F32)
        dena = sb("dena", [1, 128], F32)
        dena_col = sb("dena_col", [128, 1], F32)
        rden = sb("rden", [128, 1], F32)
        numsb = sb("numsb", [128, 128], F32)
        mh_acc = sb("mh_acc", [128, D], F32)
        scr = ln2fm[:, 0:2 * 768].bitcast(F32)
        x2t = ln2fm[:, 2 * D:4 * D].bitcast(F32)
        pbf = [ps(f"pbf{i}", [128, 512], F32) for i in range(4)]
        pbb = [ps(f"pbb{i}", [128, 1024], BF16) for i in range(2)]
        gs = sem("gs")
        block = es.enter_context(nc.Block())

        P = Plan()
        A = P.add
        P.dma("sync", out=idb_sb[:], in_=idb)
        P.dma("sync", out=idf_sb[:], in_=idf)
        P.dma("sync", out=ones_rb[:], in_=onesrb)
        P.dma("sync", out=ln2sb[:].rearrange("p (t d) -> p t d", d=D),
              in_=ln2.rearrange("(t p) d -> p t d", p=128))
        for h in range(16):
            P.dma("sync", out=dacc[:].rearrange("p (c d) -> p c d", d=128), in_=dm4[0][h])
            for r in range(1, 8):
                P.dma("sync", out=scr[:, 0:768].rearrange("p (c d) -> p c d", d=128),
                      in_=dm4[r][h])
                A("vector", lambda e: e.tensor_add(dacc[:], dacc[:], scr[:, 0:768]))
            P.dma("gpsimd", out=scr[:, 0:768].rearrange("p (c d) -> p c d", d=128),
                  in_=mem3[h])
            A("vector", lambda e: e.tensor_add(dacc[:], dacc[:], scr[:, 0:768]))
            A("vector", lambda e, h=h: e.tensor_copy(
                newmem[:, h * 768:(h + 1) * 768], dacc[:]))
        P.dma("sync", out=nrm_acc[:], in_=dnorm[0])
        for r in range(1, 8):
            P.dma("sync", out=nrm_sb[:], in_=dnorm[r])
            A("vector", lambda e: e.tensor_add(nrm_acc[:], nrm_acc[:], nrm_sb[:]))
        P.dma("sync", out=nrm_sb[:], in_=norm)
        A("vector", lambda e: e.tensor_add(nrm_acc[:], nrm_acc[:], nrm_sb[:]))
        for fc in range(6):
            A("tensor", lambda e, fc=fc: e.transpose(
                pbf[1][0:128, 0:16], nrm_acc[:, fc * 128:(fc + 1) * 128],
                idf_sb[0:16, 0:16]))
            A("vector", lambda e, fc=fc: e.tensor_copy(
                newnormT[:, fc * 16:(fc + 1) * 16], pbf[1][0:128, 0:16]))
        for i in range(128):
            t, ch = divmod(i, 16)
            A("tensor", lambda e, t=t, ch=ch, i=i: e.transpose(
                pbb[i % 2][0:128, 0:128],
                ln2sb[:, t * D + ch * 128:t * D + (ch + 1) * 128], idb_sb[:]))
            A("vector", lambda e, t=t, ch=ch, i=i: e.tensor_copy(
                ln2fm[:, (ch * 8 + t) * 128:(ch * 8 + t + 1) * 128],
                pbb[i % 2][0:128, 0:128]))

        def load_wt(Wsrc, j0, npt):
            for jt in range(npt):
                slot = (jt % 2) * D
                P.dma("gpsimd", out=wrow[:, slot:slot + D],
                      in_=Wsrc.rearrange("(t p) d -> t p d", p=128)[j0 + jt])
                for ch in range(16):
                    i = ch
                    A("tensor", lambda e, ch=ch, slot=slot, i=i: e.transpose(
                        pbb[i % 2][0:128, 0:128],
                        wrow[:, slot + ch * 128:slot + (ch + 1) * 128], idb_sb[:]))
                    A("vector", lambda e, ch=ch, jt=jt, i=i: e.tensor_copy(
                        wT[:, ch * 512 + jt * 128:ch * 512 + (jt + 1) * 128],
                        pbb[i % 2][0:128, 0:128]))

        def proj_tile(t, bias_off, jw=512):
            A("tensor", lambda e, bias_off=bias_off, jw=jw: e.matmul(
                pbf[0][0:128, 0:jw], ones_rb[:],
                bias_sb[0:1, bias_off:bias_off + jw], start=True, stop=False))
            for ch in range(16):
                A("tensor", lambda e, ch=ch, t=t, jw=jw: e.matmul(
                    pbf[0][0:128, 0:jw],
                    ln2fm[:, (ch * 8 + t) * 128:(ch * 8 + t + 1) * 128],
                    wT[:, ch * 512:ch * 512 + jw],
                    start=False, stop=(ch == 15)))

        P.dma("gpsimd", out=bias_sb[:], in_=bq)
        for jg in range(4):
            load_wt(Wq, 4 * jg, 4)
            for t in range(NTILE):
                proj_tile(t, jg * 512)
                A("vector", lambda e, t=t, jg=jg: e.tensor_copy(
                    q_sb[:, t * D + jg * 512:t * D + (jg + 1) * 512],
                    pbf[0][0:128, 0:512]))
        P.dma("gpsimd", out=bias_sb[:], in_=bg)
        for jg in range(4):
            load_wt(Wg, 4 * jg, 4)
            for t in range(NTILE):
                proj_tile(t, jg * 512)
                A("scalar", lambda e, t=t, jg=jg: e.activation(
                    out=gate_sb[:, t * D + jg * 512:t * D + (jg + 1) * 512],
                    in_=pbf[0][0:128, 0:512], func=AF.Sigmoid))
        for t in range(NTILE):
            for h in range(16):
                A("scalar", lambda e, t=t, h=h: e.activation(
                    out=a_sb[:, 0:128], in_=q_sb[:, t * D + h * 128:t * D + (h + 1) * 128],
                    func=AF.Relu))
                A("scalar", lambda e, t=t, h=h: e.activation(
                    out=a_sb[:, 128:256], in_=q_sb[:, t * D + h * 128:t * D + (h + 1) * 128],
                    func=AF.Relu, scale=-1.0))
                for u in range(3):
                    j = u + 1
                    A("vector", lambda e, u=u, j=j: e.tensor_tensor(
                        mkraw[:, u * 256:u * 256 + j], a_sb[:, 0:j],
                        a_sb[:, 256 - j:256], op=ALU.mult))
                    A("vector", lambda e, u=u, j=j: e.tensor_tensor(
                        mkraw[:, u * 256 + j:(u + 1) * 256], a_sb[:, j:256],
                        a_sb[:, 0:256 - j], op=ALU.mult))
                A("scalar", lambda e: e.activation(out=scr[:], in_=mkraw[:],
                                                   func=AF.Square, accum_out=ss2[:]))
                A("scalar", lambda e: e.activation(out=nrm1[:], in_=ss2[:], func=AF.Sqrt))
                A("vector", lambda e: e.tensor_scalar_max(nmx[:], nrm1[:], 1e-12))
                A("vector", lambda e: e.reciprocal(rmu[:], nmx[:]))
                A("vector", lambda e: e.tensor_scalar_mul(mkb[:], mkraw[:], rmu[:]))
                for fc in range(6):
                    A("tensor", lambda e, fc=fc: e.transpose(
                        pbb[fc % 2][0:128, 0:128], mkb[:, fc * 128:(fc + 1) * 128],
                        idb_sb[:]))
                    A("vector", lambda e, fc=fc: e.tensor_copy(
                        mqfm[:, fc * 128:(fc + 1) * 128], pbb[fc % 2][0:128, 0:128]))
                for fc in range(6):
                    A("tensor", lambda e, fc=fc, h=h: e.matmul(
                        pbf[0][0:128, 0:128], newmem[:, (h * 6 + fc) * 128:(h * 6 + fc + 1) * 128],
                        mqfm[:, fc * 128:(fc + 1) * 128],
                        start=(fc == 0), stop=(fc == 5)))
                for fc in range(6):
                    A("tensor", lambda e, fc=fc, h=h: e.matmul(
                        pbf[1][0:1, 0:128], newnormT[:, fc * 16 + h:fc * 16 + h + 1],
                        mqfm[:, fc * 128:(fc + 1) * 128],
                        start=(fc == 0), stop=(fc == 5)))
                A("vector", lambda e: e.tensor_scalar_add(dena[:], pbf[1][0:1, 0:128], 1e-8))
                A("tensor", lambda e: e.transpose(pbf[2][0:128, 0:1], dena[:],
                                                  idf_sb[0:1, 0:1]))
                A("vector", lambda e: e.tensor_copy(dena_col[:], pbf[2][0:128, 0:1]))
                A("vector", lambda e: e.reciprocal(rden[:], dena_col[:]))
                A("vector", lambda e: e.tensor_copy(numsb[:], pbf[0][0:128, 0:128]))
                A("tensor", lambda e: e.transpose(pbf[3][0:128, 0:128], numsb[:],
                                                  idf_sb[:]))
                A("vector", lambda e, t=t, h=h: e.tensor_scalar_mul(
                    mh_acc[:, h * 128:(h + 1) * 128], pbf[3][0:128, 0:128], rden[:]))
            P.dma("sync", out=x2t[:], in_=x23[t])
            A("vector", lambda e, t=t: e.tensor_tensor(
                mh_acc[:], gate_sb[:, t * D:(t + 1) * D], mh_acc[:], op=ALU.mult))
            A("vector", lambda e: e.tensor_add(x2t[:], x2t[:], mh_acc[:]))
            A("vector", lambda e, t=t: e.tensor_add(x2t[:], x2t[:],
                                                    ln2sb[:, t * D:(t + 1) * D]))
            P.dma("sync", out=x33[t], in_=x2t[:])
        P.emit(block, nc, gs)
    return nc


def run_l1q(x2, ln2, dmem, dnorm, d):
    nc = build_l1q()
    cns = _attn_consts()
    in_maps = []
    for c in range(NCORES):
        in_maps.append({
            "x2": np.ascontiguousarray(x2[c * PTOK:(c + 1) * PTOK]),
            "ln2": np.ascontiguousarray(ln2[c * PTOK:(c + 1) * PTOK]),
            "dmem": dmem.astype(np.float32),
            "dnorm": dnorm.astype(np.float32),
            "memory": d["L1_attn_memory"].reshape(16 * 768, 128).astype(np.float32),
            "norm": d["L1_attn_norm"].astype(np.float32),
            "Wq": d["L1_attn_Wq"].astype(np.float32),
            "bq": d["L1_attn_bq"].reshape(1, D).astype(np.float32),
            "Wg": d["L1_attn_Wg"].astype(np.float32),
            "bg": d["L1_attn_bg"].reshape(1, D).astype(np.float32),
            "ident_bf": cns["ident_bf"], "ident_f32": cns["ident_f32"],
            "ones_row_bf": np.ones((1, 128), dtype=BF),
        })
    res = run_bass_kernel_spmd(nc, in_maps, core_ids=list(range(NCORES)))
    x3 = np.concatenate([res.results[c]["x3"] for c in range(NCORES)], 0)
    return x3



def build_moe(NTG=8):
    TOK = NTG * 512
    nc = bass.Bass("TRN2", target_bir_lowering=False)
    x3 = nc.dram_tensor("x3", [TOK, D], F32, kind="ExternalInput").ap()
    rw = nc.dram_tensor("rw", [16, D], F32, kind="ExternalInput").ap()
    ebias = nc.dram_tensor("ebias", [1, 16], F32, kind="ExternalInput").ap()
    gp2 = nc.dram_tensor("gp2", [2, D, 1792], F32, kind="ExternalInput").ap()
    up2 = nc.dram_tensor("up2", [2, D, 1792], F32, kind="ExternalInput").ap()
    dp2 = nc.dram_tensor("dp2", [2, 1792, D], F32, kind="ExternalInput").ap()
    esel = nc.dram_tensor("esel", [16, 256], F32, kind="ExternalInput").ap()
    idb = nc.dram_tensor("ident_bf", [128, 128], BF16, kind="ExternalInput").ap()
    idf = nc.dram_tensor("ident_f32", [128, 128], F32, kind="ExternalInput").ap()
    onesrb = nc.dram_tensor("ones_row_bf", [1, 128], BF16, kind="ExternalInput").ap()
    pout = nc.dram_tensor("pout", [TOK, D], F32, kind="ExternalOutput").ap()

    ln3d = nc.dram_tensor("ln3d", [TOK, D], BF16)
    x33 = x3.rearrange("(t p) d -> t p d", p=128)
    ln33 = ln3d.rearrange("(t p) d -> t p d", p=128)

    with contextlib.ExitStack() as es:
        sb = lambda n, s, d: es.enter_context(nc.sbuf_tensor(n, s, d))
        ps = lambda n, s, d: es.enter_context(nc.psum_tensor(n, s, d))
        sem = lambda n: es.enter_context(nc.semaphore(n))
        xt = sb("xt", [128, D], F32)
        xm = sb("xm", [128, D], F32)
        lnt = sb("lnt", [128, D], BF16)
        m_ = sb("m_", [128, 1], F32)
        v2 = sb("v2", [128, 1], F32)
        sd = sb("sd", [128, 1], F32)
        rstd = sb("rstd", [128, 1], F32)
        epsb = sb("epsb", [128, 1], F32)
        rwsb = sb("rwsb", [16, D], F32)
        routerT = sb("routerT", [128, 256], BF16)
        ebias_sb = sb("ebias_sb", [1, 16], F32)
        esel_sb = sb("esel_sb", [16, 256], F32)
        ones_rb = sb("ones_rb", [1, 128], BF16)
        ones_rf512 = sb("ones_rf512", [1, 512], F32)
        idb_sb = sb("idb_sb", [128, 128], BF16)
        idf_sb = sb("idf_sb", [128, 128], F32)
        x4fm = sb("x4fm", [128, 16 * 512], BF16)
        lg_sb = sb("lg_sb", [16, 512], F32)
        lgt = sb("lgt", [128, 16], F32)
        esc = sb("esc", [128, 16], F32)
        rmax = sb("rmax", [128, 1], F32)
        nrmax = sb("nrmax", [128, 1], F32)
        top8 = sb("top8", [128, 8], F32)
        m4 = sb("m4", [128, 1], F32)
        keep = sb("keep", [128, 16], F32)
        wsum = sb("wsum", [128, 1], F32)
        rsum = sb("rsum", [128, 1], F32)
        w_tm = sb("w_tm", [128, 64], F32)
        w_rows = sb("w_rows", [16, 512], F32)
        wb = sb("wb", [128, 2 * 512], F32)
        gw = sb("gw", [128, 16 * 128], BF16)
        uw = sb("uw", [128, 16 * 128], BF16)
        sg = sb("sg", [128, 512], BF16)
        abuf = sb("abuf", [128, 512], F32)
        act = sb("act", [128, 2 * 14 * 512], BF16)
        dwc = sb("dwc", [128, 14 * 128], BF16)
        po_sb = sb("po_sb", [128, 512], F32)
        po_t = sb("po_t", [128, 512], F32)
        pA = ps("pA", [128, 512], F32)
        pB = ps("pB", [128, 512], F32)
        pC = ps("pC", [128, 512], F32)
        pD = ps("pD", [128, 512], F32)
        gs = sem("gs")
        block = es.enter_context(nc.Block())

        P = Plan()
        A = P.add
        CH = P.ch
        P.dma("sync", out=idb_sb[:], in_=idb)
        P.dma("sync", out=idf_sb[:], in_=idf)
        P.dma("sync", out=ones_rb[:], in_=onesrb)
        A("vector", lambda e: e.memset(epsb[:], EPS_LN))
        A("vector", lambda e: e.memset(ones_rf512[:], 1.0))
        P.dma("sync", out=rwsb[:], in_=rw)
        P.dma("sync", out=ebias_sb[:], in_=ebias)
        P.dma("sync", out=esel_sb[:], in_=esel)
        for ch in range(16):
            A("tensor", lambda e, ch=ch: e.transpose(
                pA[0:128, 0:16], rwsb[:, ch * 128:(ch + 1) * 128], idf_sb[0:16, 0:16]))
            A("vector", lambda e, ch=ch: e.tensor_copy(
                routerT[:, ch * 16:(ch + 1) * 16], pA[0:128, 0:16]))
        for i in range(TOK // 128):
            P.dma("sync", out=xt[:], in_=x33[i])
            A("vector", lambda e: e.reduce_sum(m_[:], xt[:], axis=AX.X))
            A("vector", lambda e: e.tensor_scalar_mul(m_[:], m_[:], 1.0 / D))
            A("vector", lambda e: e.tensor_scalar_sub(xm[:], xt[:], m_[:]))
            A("scalar", lambda e: e.activation(out=lnt[:], in_=xm[:], func=AF.Square,
                                               accum_out=v2[:]))
            A("scalar", lambda e: e.activation(out=sd[:], in_=v2[:], func=AF.Sqrt,
                                               scale=1.0 / D, bias=epsb[:, 0:1]))
            A("vector", lambda e: e.reciprocal(rstd[:], sd[:]))
            A("scalar", lambda e: e.activation(out=lnt[:], in_=xm[:], func=AF.Identity,
                                               scale=rstd[:], bias=0.0))
            P.dma("sync", out=ln33[i], in_=lnt[:])
        for tg in range(NTG):
            for ch in range(16):
                P.add("sync", (lambda e, tg=tg, ch=ch: e.dma_start_transpose(
                    x4fm[:, ch * 512:(ch + 1) * 512],
                    ln3d[tg * 512:(tg + 1) * 512, ch * 128:(ch + 1) * 128])), 16,
                    chain=(ch > 0))
            A("tensor", lambda e: e.matmul(pA[0:16, 0:512], ebias_sb[:], ones_rf512[:],
                                           start=True, stop=False))
            for ch in range(16):
                P.add("tensor", (lambda e, ch=ch: e.matmul(
                    pA[0:16, 0:512], routerT[:, ch * 16:(ch + 1) * 16],
                    x4fm[:, ch * 512:(ch + 1) * 512],
                    start=False, stop=(ch == 15))), 1 if ch == 15 else 0, chain=(ch < 15))
            A("vector", lambda e: e.tensor_copy(lg_sb[:], pA[0:16, 0:512]))
            for j in range(4):
                A("tensor", lambda e, j=j: e.transpose(
                    pB[0:128, 0:16], lg_sb[:, j * 128:(j + 1) * 128], idf_sb[0:16, 0:16]))
                A("vector", lambda e: e.tensor_copy(lgt[:], pB[0:128, 0:16]))
                A("vector", lambda e: e.reduce_max(rmax[:], lgt[:], axis=AX.X))
                A("vector", lambda e: e.tensor_scalar_mul(nrmax[:], rmax[:], -1.0))
                A("scalar", lambda e: e.activation(out=esc[:], in_=lgt[:], func=AF.Exp,
                                                   bias=nrmax[:]))
                A("vector", lambda e: e.max(top8[:], esc[:]))
                A("vector", lambda e: e.tensor_copy(m4[:], top8[:, 3:4]))
                A("vector", lambda e: e.tensor_scalar(keep[:], esc[:], m4[:], None,
                                                      op0=ALU.is_ge))
                A("vector", lambda e: e.tensor_tensor(keep[:], keep[:], esc[:],
                                                      op=ALU.mult))
                A("vector", lambda e: e.reduce_sum(wsum[:], keep[:], axis=AX.X))
                A("vector", lambda e: e.reciprocal(rsum[:], wsum[:]))
                A("vector", lambda e, j=j: e.tensor_scalar_mul(
                    w_tm[:, j * 16:(j + 1) * 16], keep[:], rsum[:]))
                A("tensor", lambda e, j=j: e.transpose(
                    pB[0:16, 0:128], w_tm[:, j * 16:(j + 1) * 16], idf_sb[:]))
                A("vector", lambda e, j=j: e.tensor_copy(
                    w_rows[:, j * 128:(j + 1) * 128], pB[0:16, 0:128]))
            for ep in range(2):
                A("tensor", lambda e, ep=ep: e.matmul(
                    pB[0:128, 0:512], esel_sb[:, ep * 128:(ep + 1) * 128], w_rows[:],
                    start=True, stop=True))
                A("vector", lambda e, ep=ep: e.tensor_copy(
                    wb[:, ep * 512:(ep + 1) * 512], pB[0:128, 0:512]))
            for ep in range(2):
                for ic in range(14):
                    P.dma("gpsimd", out=gw[:].rearrange("p (c i) -> p c i", i=128),
                          in_=gp2[ep][:, ic * 128:(ic + 1) * 128]
                          .rearrange("(c p) i -> p c i", p=128))
                    P.dma("gpsimd", chain=True,
                          out=uw[:].rearrange("p (c i) -> p c i", i=128),
                          in_=up2[ep][:, ic * 128:(ic + 1) * 128]
                          .rearrange("(c p) i -> p c i", p=128))
                    for ch in range(16):
                        P.add("tensor", (lambda e, ch=ch: e.matmul(
                            pA[:], gw[:, ch * 128:(ch + 1) * 128],
                            x4fm[:, ch * 512:(ch + 1) * 512],
                            start=(ch == 0), stop=(ch == 15))), 0, chain=(ch > 0))
                    for ch in range(16):
                        P.add("tensor", (lambda e, ch=ch: e.matmul(
                            pB[:], uw[:, ch * 128:(ch + 1) * 128],
                            x4fm[:, ch * 512:(ch + 1) * 512],
                            start=(ch == 0), stop=(ch == 15))),
                            1 if ch == 15 else 0, chain=(ch > 0))
                    A("scalar", lambda e: e.activation(out=sg[:], in_=pA[:], func=AF.Silu))
                    A("vector", lambda e: e.tensor_tensor(abuf[:], pB[:], sg[:],
                                                          op=ALU.mult))
                    A("vector", lambda e, ep=ep, ic=ic: e.tensor_tensor(
                        act[:, (ep * 14 + ic) * 512:(ep * 14 + ic + 1) * 512],
                        abuf[:], wb[:, ep * 512:(ep + 1) * 512], op=ALU.mult))
            for hc in range(16):
                first = True
                for ep in range(2):
                    P.dma("gpsimd", out=dwc[:].rearrange("p (c h) -> p c h", h=128),
                          in_=dp2[ep][:, hc * 128:(hc + 1) * 128]
                          .rearrange("(c p) h -> p c h", p=128))
                    for ic in range(14):
                        P.add("tensor", (lambda e, ep=ep, ic=ic: e.matmul(
                            pC[:], dwc[:, ic * 128:(ic + 1) * 128],
                            act[:, (ep * 14 + ic) * 512:(ep * 14 + ic + 1) * 512],
                            start=(ep == 0 and ic == 0),
                            stop=(ep == 1 and ic == 13))),
                            1 if ic == 13 else 0, chain=(ic > 0))
                A("vector", lambda e: e.tensor_copy(po_sb[:], pC[:]))
                for tj in range(4):
                    A("tensor", lambda e, tj=tj: e.transpose(
                        pD[0:128, 0:128], po_sb[:, tj * 128:(tj + 1) * 128], idf_sb[:]))
                    A("vector", lambda e, tj=tj: e.tensor_copy(
                        po_t[:, tj * 128:(tj + 1) * 128], pD[0:128, 0:128]))
                P.dma("sync", out=pout.rearrange("(g tj p) d -> g p tj d", p=128, tj=4)
                      [tg][:, :, hc * 128:(hc + 1) * 128],
                      in_=po_t[:].rearrange("p (tj d) -> p tj d", d=128))
        P.emit(block, nc, gs)
    return nc


def run_moe(x3, d):
    NTG = 8
    nc = build_moe(NTG)
    cns = _attn_consts()
    parts = []
    gp = d["L1_ffn_gate_proj"].astype(np.float32)
    up = d["L1_ffn_up_proj"].astype(np.float32)
    dp = d["L1_ffn_down_proj"].astype(np.float32)
    for half in range(2):
        sl = slice(half * NTG * 512, (half + 1) * NTG * 512)
        in_maps = []
        for c in range(NCORES):
            esel = np.zeros((16, 256), dtype=np.float32)
            esel[2 * c, 0:128] = 1.0
            esel[2 * c + 1, 128:256] = 1.0
            in_maps.append({
                "x3": np.ascontiguousarray(x3[sl]),
                "rw": d["L1_ffn_router_w"].astype(np.float32),
                "ebias": d["L1_ffn_expert_bias"].reshape(1, 16).astype(np.float32),
                "gp2": np.ascontiguousarray(gp[2 * c:2 * c + 2]),
                "up2": np.ascontiguousarray(up[2 * c:2 * c + 2]),
                "dp2": np.ascontiguousarray(dp[2 * c:2 * c + 2]),
                "esel": esel,
                "ident_bf": cns["ident_bf"], "ident_f32": cns["ident_f32"],
                "ones_row_bf": np.ones((1, 128), dtype=BF),
            })
        res = run_bass_kernel_spmd(nc, in_maps, core_ids=list(range(NCORES)))
        parts.append(np.stack([res.results[c]["pout"] for c in range(NCORES)], 0))
    return np.concatenate(parts, 1)



def build_combine():
    nc = bass.Bass("TRN2", target_bir_lowering=False)
    x3 = nc.dram_tensor("x3", [PTOK, D], F32, kind="ExternalInput").ap()
    parts = nc.dram_tensor("parts", [8, PTOK, D], F32, kind="ExternalInput").ap()
    x4o = nc.dram_tensor("x4", [PTOK, D], F32, kind="ExternalOutput").ap()
    x33 = x3.rearrange("(t p) d -> t p d", p=128)
    x43 = x4o.rearrange("(t p) d -> t p d", p=128)
    p4 = parts.rearrange("r (t p) d -> r t p d", p=128)
    with contextlib.ExitStack() as es:
        sb = lambda n, s, d: es.enter_context(nc.sbuf_tensor(n, s, d))
        sem = lambda n: es.enter_context(nc.semaphore(n))
        acc = sb("acc", [128, D], F32)
        tmp = sb("tmp", [128, D], F32)
        gs = sem("gs")
        block = es.enter_context(nc.Block())
        P = Plan()
        A = P.add
        for i in range(NTILE):
            P.dma("sync", out=acc[:], in_=x33[i])
            for r in range(8):
                P.dma("sync", out=tmp[:], in_=p4[r][i])
                A("vector", lambda e: e.tensor_add(acc[:], acc[:], tmp[:]))
            P.dma("sync", out=x43[i], in_=acc[:])
        P.emit(block, nc, gs)
    return nc


def run_combine(x3, parts):
    nc = build_combine()
    in_maps = [{"x3": np.ascontiguousarray(x3[c * PTOK:(c + 1) * PTOK]),
                "parts": np.ascontiguousarray(parts[:, c * PTOK:(c + 1) * PTOK])}
               for c in range(NCORES)]
    res = run_bass_kernel_spmd(nc, in_maps, core_ids=list(range(NCORES)))
    return np.concatenate([res.results[c]["x4"] for c in range(NCORES)], 0)


def _attn_consts():
    masks = np.zeros((4, 128, 512), dtype=BF)
    for j in range(4):
        for kk in range(128):
            masks[j, kk, j * 128 + kk:] = 1.0
    return {
        "masks": masks,
        "ident_bf": np.eye(128, dtype=BF),
        "ident_f32": np.eye(128, dtype=np.float32),
        "ones_col_bf": np.ones((128, 1), dtype=BF),
        "ones_row_f32": np.ones((1, 128), dtype=np.float32),
    }


def run_attn0(ln, hn):
    nc = build_attn0()
    cns = _attn_consts()
    ln4 = ln.reshape(2, 4096, 16, 128)
    in_maps = []
    for c in range(NCORES):
        lnh = np.stack([np.ascontiguousarray(ln4[(4 * c + j) // 16, :, (4 * c + j) % 16])
                        for j in range(4)])
        hn_sel = np.stack([np.ascontiguousarray(hn[4 * ((4 * c + j) // 16):
                                                   4 * ((4 * c + j) // 16) + 4,
                                                   (4 * c + j) % 16])
                           for j in range(4)])
        in_maps.append(dict(lnh=lnh, hn_sel=hn_sel, **cns))
    res = run_bass_kernel_spmd(nc, in_maps, core_ids=list(range(NCORES)))
    att = np.zeros((2, 4096, 16, 128), dtype=np.float32)
    for c in range(NCORES):
        for j in range(4):
            bh = 4 * c + j
            att[bh // 16, :, bh % 16] = res.results[c]["attn"][j]
    return att.reshape(NTOK, D)





G4 = [[0, 1, 2, 3], [4, 5, 6, 7]]
G8 = [[0, 1, 2, 3, 4, 5, 6, 7]]
SCATT = 1.0 / float(np.sqrt(128.0))
NU_F2 = 768


class Arena:

    def __init__(self, ap, cols):
        self.ap = ap
        self.cols = cols
        self.off = 0

    def b(self, cols):
        self.off = (self.off + 15) & ~15
        assert self.off + cols <= self.cols, f"arena overflow {self.off}+{cols}"
        v = self.ap[:, self.off:self.off + cols]
        self.off += cols
        return v

    def f(self, cols_f32):
        v = self.b(2 * cols_f32)
        return v.bitcast(F32)

    def reset(self, off=0):
        self.off = off


def build_fused(upto="x4"):
    nc = bass.Bass("TRN2", target_bir_lowering=False)
    x_in = nc.dram_tensor("x", [PTOK, D], F32, kind="ExternalInput").ap()
    masks = nc.dram_tensor("masks", [32, 128, 1024], BF16, kind="ExternalInput").ap()
    sim = nc.dram_tensor("sim", [D, 32], F32, kind="ExternalInput").ap()
    gates = nc.dram_tensor("gates", [1, 32], F32, kind="ExternalInput").ap()
    w_v = nc.dram_tensor("w_v", [32, 64 * D], F32, kind="ExternalInput").ap()
    o_w = nc.dram_tensor("o_w", [32, 64 * D], F32, kind="ExternalInput").ap()
    Wk = nc.dram_tensor("Wk", [512, D], F32, kind="ExternalInput").ap()
    bk = nc.dram_tensor("bk", [1, 512], F32, kind="ExternalInput").ap()
    Wv = nc.dram_tensor("Wv", [512, D], F32, kind="ExternalInput").ap()
    bv = nc.dram_tensor("bv", [1, 512], F32, kind="ExternalInput").ap()
    Wmb = nc.dram_tensor("Wmb", [D, D], F32, kind="ExternalInput").ap()
    bmb = nc.dram_tensor("bmb", [1, D], F32, kind="ExternalInput").ap()
    Wq = nc.dram_tensor("Wq", [D, D], F32, kind="ExternalInput").ap()
    bq = nc.dram_tensor("bq", [1, D], F32, kind="ExternalInput").ap()
    Wg = nc.dram_tensor("Wg", [D, D], F32, kind="ExternalInput").ap()
    bg = nc.dram_tensor("bg", [1, D], F32, kind="ExternalInput").ap()
    memory = nc.dram_tensor("memory", [16 * 768, 128], F32, kind="ExternalInput").ap()
    norm = nc.dram_tensor("norm", [16, 768], F32, kind="ExternalInput").ap()
    if upto == "x4":
        rw = nc.dram_tensor("rw", [16, D], F32, kind="ExternalInput").ap()
        ebias = nc.dram_tensor("ebias", [1, 16], F32, kind="ExternalInput").ap()
        gp = nc.dram_tensor("gp", [16, D, 1792], F32, kind="ExternalInput").ap()
        upj = nc.dram_tensor("up", [16, D, 1792], F32, kind="ExternalInput").ap()
        dpj = nc.dram_tensor("dp", [16, 1792, D], F32, kind="ExternalInput").ap()
        esel = nc.dram_tensor("esel", [16, 16 * 128], F32, kind="ExternalInput").ap()
    idb = nc.dram_tensor("ident_bf", [128, 128], BF16, kind="ExternalInput").ap()
    idf = nc.dram_tensor("ident_f32", [128, 128], F32, kind="ExternalInput").ap()
    onesrb = nc.dram_tensor("ones_row_bf", [1, 128], BF16, kind="ExternalInput").ap()
    onesrf = nc.dram_tensor("ones_row_f32", [1, 128], F32, kind="ExternalInput").ap()
    onescb = nc.dram_tensor("ones_col_bf", [128, 1], BF16, kind="ExternalInput").ap()
    onescf = nc.dram_tensor("ones_col_f32", [128, 1], F32, kind="ExternalInput").ap()
    out = nc.dram_tensor("out", [PTOK, D], F32, kind="ExternalOutput").ap()

    ag1_in = nc.dram_tensor("ag1_in", [PTOK, D], BF16)
    ag1_out = nc.dram_tensor("ag1_out", [4 * PTOK, D], BF16)
    ag2_in = nc.dram_tensor("ag2_in", [D, PTOK], BF16)
    ag2_out = nc.dram_tensor("ag2_out", [4 * D, PTOK], BF16)
    arm_in = nc.dram_tensor("arm_in", [16, 1], F32)
    arm_out = nc.dram_tensor("arm_out", [16, 1], F32)
    ass_in = nc.dram_tensor("ass_in", [1, D], F32)
    ass_out = nc.dram_tensor("ass_out", [1, D], F32)
    dar_in = nc.dram_tensor("dar_in", [12384, 128], F32)
    dar_out = nc.dram_tensor("dar_out", [12384, 128], F32)
    wv_dyn = nc.dram_tensor("wv_dyn", [64, D], BF16)
    wo_dyn = nc.dram_tensor("wo_dyn", [64, D], BF16)
    xspill = nc.dram_tensor("xspill", [PTOK, D], F32)
    ln2d = nc.dram_tensor("ln2d", [PTOK, D], BF16)
    ln3d = nc.dram_tensor("ln3d", [PTOK, D], BF16)

    xin3 = x_in.rearrange("(t p) d -> t p d", p=128)
    out3 = out.rearrange("(t p) d -> t p d", p=128)
    mem3 = memory.rearrange("(h c p) d -> h p c d", p=128, c=6)
    ln2d3 = ln2d.rearrange("(t p) d -> t p d", p=128)
    xsp3 = xspill.rearrange("(t p) d -> t p d", p=128)
    ln3d3 = ln3d.rearrange("(t p) d -> t p d", p=128)

    ACOLS = 100352
    with contextlib.ExitStack() as es:
        sb = lambda n, s, d: es.enter_context(nc.sbuf_tensor(n, s, d))
        ps = lambda n, s, d: es.enter_context(nc.psum_tensor(n, s, d))
        sem = lambda n: es.enter_context(nc.semaphore(n))
        AR = Arena(sb("arena", [128, ACOLS], BF16)[:], ACOLS)
        m_ = sb("m_", [128, 1], F32)
        v2 = sb("v2", [128, 1], F32)
        sd = sb("sd", [128, 1], F32)
        rstd = sb("rstd", [128, 1], F32)
        epsb = sb("epsb", [128, 1], F32)
        ss2 = sb("ss2", [128, 1], F32)
        nrm1 = sb("nrm1", [128, 1], F32)
        nmx = sb("nmx", [128, 1], F32)
        rmu = sb("rmu", [128, 1], F32)
        negMb = [sb(f"negMb{h}", [128, 1], F32) for h in range(16)]
        rmax = sb("rmax", [128, 1], F32)
        nrmax = sb("nrmax", [128, 1], F32)
        wsum = sb("wsum", [128, 1], F32)
        rsum = sb("rsum", [128, 1], F32)
        rden = sb("rden", [128, 1], F32)
        g_s2 = sb("g_s2", [1, 1], F32)
        g_sn = sb("g_sn", [1, 1], F32)
        g_rn = sb("g_rn", [1, 1], F32)
        g_nm = sb("g_nm", [1, 1], F32)
        g_es = sb("g_es", [1, 1], F32)
        g_rs = sb("g_rs", [1, 1], F32)
        idb_sb = sb("idb_sb", [128, 128], BF16)
        idf_sb = sb("idf_sb", [128, 128], F32)
        ones_rb = sb("ones_rb", [1, 128], BF16)
        ones_rf = sb("ones_rf", [1, 128], F32)
        ones_cb = sb("ones_cb", [128, 1], BF16)
        ones_cf = sb("ones_cf", [128, 1], F32)
        ones_rf512 = sb("ones_rf512", [1, 512], F32)
        pbf = [ps(f"pbf{i}", [128, 512], F32) for i in range(6)]
        pbb = [ps(f"pbb{i}", [128, 1024], BF16) for i in range(2)]
        gs = sem("gs")
        block = es.enter_context(nc.Block())

        P = Plan()
        A = P.add

        P.dma("sync", out=idb_sb[:], in_=idb)
        P.dma("sync", out=idf_sb[:], in_=idf)
        P.dma("sync", out=ones_rb[:], in_=onesrb)
        P.dma("sync", out=ones_rf[:], in_=onesrf)
        P.dma("sync", out=ones_cb[:], in_=onescb)
        P.dma("sync", out=ones_cf[:], in_=onescf)
        A("vector", lambda e: e.memset(epsb[:], EPS_LN))
        A("vector", lambda e: e.memset(ones_rf512[:], 1.0))

        xres = AR.f(16384)
        ln0tm = AR.b(16384)
        ln0fm = AR.b(16384)
        xm = AR.f(2048)
        sqt = AR.f(2048)
        hs = AR.f(16)
        hmax = AR.f(16)
        hrow = AR.f(16)
        ld16 = AR.f(16)

        for i in range(NTILE):
            P.dma("sync", out=xres[:, i * D:(i + 1) * D], in_=xin3[i],
                  chain=(i > 0))
        A("vector", lambda e: e.memset(hmax[:], 0.0))

        def emit_ln(src_view, dst_view, hn=False):
            A("vector", lambda e: e.reduce_sum(m_[:], src_view, axis=AX.X))
            A("vector", lambda e: e.tensor_scalar_mul(m_[:], m_[:], 1.0 / D))
            A("vector", lambda e: e.tensor_scalar_sub(xm[:], src_view, m_[:]))
            A("scalar", lambda e: e.activation(out=sqt[:], in_=xm[:], func=AF.Square,
                                               accum_out=v2[:]))
            A("scalar", lambda e: e.activation(out=sd[:], in_=v2[:], func=AF.Sqrt,
                                               scale=1.0 / D, bias=epsb[:, 0:1]))
            A("vector", lambda e: e.reciprocal(rstd[:], sd[:]))
            if hn:
                A("vector", lambda e: e.reduce_sum(
                    hs[:], sqt[:].rearrange("p (h k) -> p h k", k=128), axis=AX.X))
                A("vector", lambda e: e.tensor_scalar(hrow[:, 0:1], rstd[:], rstd[:],
                                                      None, op0=ALU.mult))
                A("vector", lambda e: e.tensor_scalar_mul(hs[:], hs[:], hrow[:, 0:1]))
                A("vector", lambda e: e.tensor_max(hmax[:], hmax[:], hs[:]))
            A("scalar", lambda e: e.activation(out=dst_view, in_=xm[:],
                                               func=AF.Identity, scale=rstd[:],
                                               bias=0.0))

        for i in range(NTILE):
            emit_ln(xres[:, i * D:(i + 1) * D], ln0tm[:, i * D:(i + 1) * D], hn=True)
        for i in range(128):
            t, ch = divmod(i, 16)
            A("tensor", lambda e, t=t, ch=ch, i=i: e.transpose(
                pbb[i % 2][0:128, 0:128],
                ln0tm[:, t * D + ch * 128:t * D + (ch + 1) * 128], idb_sb[:]))
            A("vector", lambda e, t=t, ch=ch, i=i: e.tensor_copy(
                ln0fm[:, (ch * 8 + t) * 128:(ch * 8 + t + 1) * 128],
                pbb[i % 2][0:128, 0:128]))
        A("tensor", lambda e: e.transpose(pbf[0][0:16, 0:128], hmax[:, 0:16],
                                          idf_sb[:]))
        A("vector", lambda e: e.reduce_max(hrow[0:16, 0:1], pbf[0][0:16, 0:128],
                                           axis=AX.X))

        P.dma("sync", out=ag1_in.ap().rearrange("(t p) d -> p t d", p=128),
              in_=ln0tm[:].rearrange("p (t d) -> p t d", d=D))
        P.dma("sync", chain=True,
              out=ag2_in.ap().rearrange("(c p) (t j) -> p c t j", p=128, j=128),
              in_=ln0fm[:].rearrange("p (c t j) -> p c t j", t=8, j=128))
        P.dma("sync", chain=True, out=arm_in.ap(), in_=hrow[0:16, 0:1])
        A("gpsimd", lambda e: e.collective_compute(
            "AllGather", ALU.bypass, replica_groups=G4,
            ins=[ag1_in.ap().opt()], outs=[ag1_out.ap().opt()]))
        A("gpsimd", lambda e: e.collective_compute(
            "AllGather", ALU.bypass, replica_groups=G4,
            ins=[ag2_in.ap().opt()], outs=[ag2_out.ap().opt()]))
        A("gpsimd", lambda e: e.collective_compute(
            "AllReduce", ALU.max, replica_groups=G4,
            ins=[arm_in.ap().opt()], outs=[arm_out.ap().opt()]))
        P.dma("sync", out=ld16[0:16, 0:1], in_=arm_out.ap())
        A("vector", lambda e: e.tensor_scalar_mul(ld16[0:16, 0:1], ld16[0:16, 0:1],
                                                  -SCATT))
        A("tensor", lambda e: e.transpose(pbf[0][0:1, 0:16], ld16[0:16, 0:1],
                                          idf_sb[0:16, 0:16]))
        A("vector", lambda e: e.tensor_copy(ld16[0:1, 0:16], pbf[0][0:1, 0:16]))
        A("tensor", lambda e: e.matmul(pbf[1][0:128, 0:16], ones_rf[:],
                                       ld16[0:1, 0:16], start=True, stop=True))
        for h in range(16):
            A("vector", lambda e, h=h: e.tensor_copy(negMb[h][:], pbf[1][:, h:h + 1]))

        kbuf = AR.b(2 * 128)
        vbuf = AR.b(2 * 128)
        mbuf = AR.b(2 * 1024)
        ptb = AR.b(2 * 512)
        sums_sb = AR.f(512)
        recip_sb = AR.f(512)
        sums_cols = AR.f(8)
        acc_sb = AR.f(512)
        attc = AR.f(512)
        ag1r = ag1_out.ap()
        ag2r = ag2_out.ap()
        for h in range(16):
            for kt in range(32):
                sl = kt % 2
                P.dma("sync", out=kbuf[:, sl * 128:(sl + 1) * 128],
                      in_=ag2r[(kt // 8) * D + h * 128:(kt // 8) * D + (h + 1) * 128,
                               (kt % 8) * 128:(kt % 8 + 1) * 128])
                P.dma("sync", chain=True, out=vbuf[:, sl * 128:(sl + 1) * 128],
                      in_=ag1r[kt * 128:(kt + 1) * 128, h * 128:(h + 1) * 128])
                P.dma("sync", chain=True, out=mbuf[:, sl * 1024:(sl + 1) * 1024],
                      in_=masks[kt])
                A("tensor", lambda e, h=h, sl=sl: e.matmul(
                    pbf[4][:, 0:512], kbuf[:, sl * 128:(sl + 1) * 128],
                    ln0fm[:, (h * 8) * 128:(h * 8 + 4) * 128],
                    start=True, stop=True))
                P.add("tensor", (lambda e, h=h, sl=sl: e.matmul(
                    pbf[5][:, 0:512], kbuf[:, sl * 128:(sl + 1) * 128],
                    ln0fm[:, (h * 8 + 4) * 128:(h * 8 + 8) * 128],
                    start=True, stop=True)), 1, chain=True)
                A("scalar", lambda e, h=h: e.activation(
                    out=ptb[:, 0:512], in_=pbf[4][:, 0:512],
                    func=AF.Exp, scale=SCATT, bias=negMb[h][:]))
                P.add("scalar", (lambda e, h=h: e.activation(
                    out=ptb[:, 512:1024], in_=pbf[5][:, 0:512],
                    func=AF.Exp, scale=SCATT, bias=negMb[h][:])), 1, chain=True)
                A("vector", lambda e, sl=sl: e.tensor_tensor(
                    ptb[:, 0:512], ptb[:, 0:512],
                    mbuf[:, sl * 1024:sl * 1024 + 512], op=ALU.mult))
                P.add("vector", (lambda e, sl=sl: e.tensor_tensor(
                    ptb[:, 512:1024], ptb[:, 512:1024],
                    mbuf[:, sl * 1024 + 512:sl * 1024 + 1024], op=ALU.mult)),
                    1, chain=True)
                A("tensor", lambda e, sl=sl, kt=kt: e.matmul(
                    pbf[0][:, 0:512], vbuf[:, sl * 128:(sl + 1) * 128],
                    ptb[:, 0:512], start=(kt == 0), stop=(kt == 31)))
                P.add("tensor", (lambda e, kt=kt: e.matmul(
                    pbf[2][0:1, 0:512], ones_cb[:], ptb[:, 0:512],
                    start=(kt == 0), stop=(kt == 31))), 0, chain=True)
                P.add("tensor", (lambda e, sl=sl, kt=kt: e.matmul(
                    pbf[1][:, 0:512], vbuf[:, sl * 128:(sl + 1) * 128],
                    ptb[:, 512:1024], start=(kt == 0), stop=(kt == 31))),
                    0, chain=True)
                P.add("tensor", (lambda e, kt=kt: e.matmul(
                    pbf[3][0:1, 0:512], ones_cb[:], ptb[:, 512:1024],
                    start=(kt == 0), stop=(kt == 31))), 1, chain=True)
            for qg in range(2):
                A("vector", lambda e, qg=qg: e.tensor_copy(sums_sb[0:1, :],
                                                           pbf[2 + qg][0:1, 0:512]))
                A("vector", lambda e: e.reciprocal(recip_sb[0:1, :], sums_sb[0:1, :]))
                for j in range(4):
                    A("tensor", lambda e, qg=qg, j=j: e.transpose(
                        pbf[2 + qg][0:128, j:j + 1],
                        recip_sb[0:1, j * 128:(j + 1) * 128], idf_sb[0:1, 0:1]))
                A("vector", lambda e, qg=qg: e.tensor_copy(sums_cols[:, 0:4],
                                                           pbf[2 + qg][0:128, 0:4]))
                A("vector", lambda e, qg=qg: e.tensor_copy(acc_sb[:],
                                                           pbf[qg][:, 0:512]))
                for j in range(4):
                    A("tensor", lambda e, qg=qg, j=j: e.transpose(
                        pbf[2 + qg][0:128, j * 128:(j + 1) * 128],
                        acc_sb[:, j * 128:(j + 1) * 128], idf_sb[:]))
                for j in range(4):
                    A("vector", lambda e, qg=qg, j=j: e.tensor_scalar_mul(
                        attc[:, j * 128:(j + 1) * 128],
                        pbf[2 + qg][0:128, j * 128:(j + 1) * 128],
                        sums_cols[:, j:j + 1]))
                for j in range(4):
                    o = (qg * 4 + j) * D + h * 128
                    A("vector", lambda e, o=o, j=j: e.tensor_add(
                        xres[:, o:o + 128], xres[:, o:o + 128],
                        attc[:, j * 128:(j + 1) * 128]))
        if upto == "x1":
            for i in range(NTILE):
                P.dma("sync", out=out3[i], in_=xres[:, i * D:(i + 1) * D])
            P.emit(block, nc, gs)
            return nc

        AR.reset(32768)
        ln1fm = AR.b(16384)
        lntile = AR.b(2048)
        xm4v = AR.f(2048)
        scr4 = AR.f(2048)

        def emit_ln_s(src_view, dst_view, xmv, scrv):
            A("vector", lambda e: e.reduce_sum(m_[:], src_view, axis=AX.X))
            A("vector", lambda e: e.tensor_scalar_mul(m_[:], m_[:], 1.0 / D))
            A("vector", lambda e: e.tensor_scalar_sub(xmv[:], src_view, m_[:]))
            A("scalar", lambda e: e.activation(out=scrv[:], in_=xmv[:],
                                               func=AF.Square, accum_out=v2[:]))
            A("scalar", lambda e: e.activation(out=sd[:], in_=v2[:], func=AF.Sqrt,
                                               scale=1.0 / D, bias=epsb[:, 0:1]))
            A("vector", lambda e: e.reciprocal(rstd[:], sd[:]))
            A("scalar", lambda e: e.activation(out=dst_view, in_=xmv[:],
                                               func=AF.Identity, scale=rstd[:],
                                               bias=0.0))

        for i in range(NTILE):
            emit_ln_s(xres[:, i * D:(i + 1) * D], lntile[:], xm4v, scr4)
            for j in range(4):
                A("tensor", lambda e, i=i, j=j: e.matmul(
                    pbf[2 + j][0:1, 0:512], rstd[:], xm4v[:, j * 512:(j + 1) * 512],
                    start=(i == 0), stop=(i == NTILE - 1)))
            for ch in range(16):
                ii = i * 16 + ch
                A("tensor", lambda e, ch=ch, ii=ii: e.transpose(
                    pbb[ii % 2][0:128, 0:128], lntile[:, ch * 128:(ch + 1) * 128],
                    idb_sb[:]))
                A("vector", lambda e, i=i, ch=ch, ii=ii: e.tensor_copy(
                    ln1fm[:, (ch * 8 + i) * 128:(ch * 8 + i + 1) * 128],
                    pbb[ii % 2][0:128, 0:128]))
        ssrow = AR.f(2048)
        for j in range(4):
            A("vector", lambda e, j=j: e.tensor_copy(
                ssrow[0:1, j * 512:(j + 1) * 512], pbf[2 + j][0:1, 0:512]))
        P.dma("sync", out=ass_in.ap(), in_=ssrow[0:1, 0:2048])
        A("gpsimd", lambda e: e.collective_compute(
            "AllReduce", ALU.add, replica_groups=G4,
            ins=[ass_in.ap().opt()], outs=[ass_out.ap().opt()]))
        srep = AR.f(2048)
        simf = AR.f(512)
        srf4 = AR.f(16)
        aff = AR.f(32)
        t32a = AR.f(32)
        t32b = AR.f(32)
        t8 = AR.f(8)
        row32a = AR.f(32)
        row32b = AR.f(32)
        rt_bf = AR.b(2)
        ro_bf = AR.b(2)
        P.dma("sync", out=srep[0:1, :], in_=ass_out.ap())
        A("vector", lambda e: e.tensor_scalar_mul(srep[0:1, :], srep[0:1, :],
                                                  1.0 / 4096.0))
        P.dma("sync", out=simf[:].rearrange("p (c e) -> p c e", e=32),
              in_=sim.rearrange("(c p) e -> p c e", p=128))
        A("scalar", lambda e: e.activation(out=xm4v[0:1, :], in_=srep[0:1, :],
                                           func=AF.Square, accum_out=g_s2[0:1, :]))
        A("scalar", lambda e: e.activation(out=g_sn[0:1, :], in_=g_s2[0:1, :],
                                           func=AF.Sqrt))
        A("vector", lambda e: e.reciprocal(g_rn[0:1, :], g_sn[0:1, :]))
        for ch in range(16):
            A("tensor", lambda e, ch=ch: e.transpose(
                pbf[3][0:128, 0:1], srep[0:1, ch * 128:(ch + 1) * 128],
                idf_sb[0:1, 0:1]))
            A("vector", lambda e, ch=ch: e.tensor_copy(srf4[:, ch:ch + 1],
                                                       pbf[3][0:128, 0:1]))
        for ch in range(16):
            A("tensor", lambda e, ch=ch: e.matmul(
                pbf[3][0:1, 0:32], srf4[:, ch:ch + 1],
                simf[:, ch * 32:(ch + 1) * 32], start=(ch == 0), stop=(ch == 15)))
        A("vector", lambda e: e.tensor_copy(aff[0:1, :], pbf[3][0:1, 0:32]))
        A("vector", lambda e: e.tensor_scalar_mul(aff[0:1, :], aff[0:1, :],
                                                  g_rn[0:1, 0:1]))
        A("scalar", lambda e: e.activation(out=simf[:], in_=simf[:], func=AF.Square))
        A("tensor", lambda e: e.matmul(pbf[3][0:1, 0:512], ones_cf[:], simf[:],
                                       start=True, stop=True))
        A("vector", lambda e: e.reduce_sum(
            row32a[0:1, :], pbf[3][0:1, 0:512].rearrange("p (c e) -> p e c", e=32),
            axis=AX.X))
        A("scalar", lambda e: e.activation(out=row32a[0:1, :], in_=row32a[0:1, :],
                                           func=AF.Sqrt))
        A("vector", lambda e: e.reciprocal(row32b[0:1, :], row32a[0:1, :]))
        A("vector", lambda e: e.tensor_tensor(aff[0:1, :], aff[0:1, :],
                                              row32b[0:1, :], op=ALU.mult))
        gz = AR.f(32)
        P.dma("sync", out=gz[0:1, :], in_=gates)
        A("scalar", lambda e: e.activation(out=t32a[0:1, :], in_=gz[0:1, 0:32],
                                           func=AF.Sigmoid))
        logit = aff
        A("vector", lambda e: e.tensor_sub(logit[0:1, :], aff[0:1, :], t32a[0:1, :]))
        gated = t32a
        A("vector", lambda e: e.tensor_scalar_max(gated[0:1, :], logit[0:1, :], 0.0))
        maskt = t32b
        A("vector", lambda e: e.tensor_scalar(maskt[0:1, :], logit[0:1, :], 0.0,
                                              None, op0=ALU.is_gt))
        A("vector", lambda e: e.reduce_sum(g_nm[0:1, :], maskt[0:1, :], axis=AX.X))
        A("vector", lambda e: e.tensor_scalar(g_nm[0:1, :], g_nm[0:1, :], 0.0, None,
                                              op0=ALU.is_equal))
        A("vector", lambda e: e.max(t8[0:1, :], logit[0:1, :]))
        m8v = g_es
        A("vector", lambda e: e.tensor_copy(m8v[0:1, :], t8[0:1, 7:8]))
        pen = row32a
        A("vector", lambda e: e.tensor_scalar(pen[0:1, :], logit[0:1, :],
                                              m8v[0:1, 0:1], -1e30,
                                              op0=ALU.is_ge, op1=ALU.mult))
        A("vector", lambda e: e.tensor_add(pen[0:1, :], logit[0:1, :], pen[0:1, :]))
        A("vector", lambda e: e.max(t8[0:1, :], pen[0:1, :]))
        A("vector", lambda e: e.tensor_copy(m8v[0:1, :], t8[0:1, 7:8]))
        fbm = row32a
        A("vector", lambda e: e.tensor_scalar(fbm[0:1, :], logit[0:1, :],
                                              m8v[0:1, 0:1], None, op0=ALU.is_ge))
        A("vector", lambda e: e.tensor_scalar_mul(fbm[0:1, :], fbm[0:1, :],
                                                  g_nm[0:1, 0:1]))
        selm = row32b
        A("vector", lambda e: e.tensor_max(selm[0:1, :], maskt[0:1, :], fbm[0:1, :]))
        ml = aff
        A("vector", lambda e: e.tensor_tensor(ml[0:1, :], gated[0:1, :],
                                              selm[0:1, :], op=ALU.mult))
        pen2 = t32a
        A("vector", lambda e: e.tensor_scalar(pen2[0:1, :], selm[0:1, :], 1e30,
                                              -1e30, op0=ALU.mult, op1=ALU.add))
        A("vector", lambda e: e.tensor_add(ml[0:1, :], ml[0:1, :], pen2[0:1, :]))
        A("vector", lambda e: e.reduce_max(g_nm[0:1, :], ml[0:1, :], axis=AX.X))
        A("vector", lambda e: e.tensor_scalar_mul(g_nm[0:1, :], g_nm[0:1, :], -1.0))
        A("scalar", lambda e: e.activation(out=pen2[0:1, :], in_=ml[0:1, :],
                                           func=AF.Exp, bias=g_nm[0:1, 0:1],
                                           accum_out=g_es[0:1, :]))
        A("vector", lambda e: e.reciprocal(g_rs[0:1, :], g_es[0:1, :]))
        routing = ml
        A("vector", lambda e: e.tensor_scalar_mul(routing[0:1, :], pen2[0:1, :],
                                                  g_rs[0:1, 0:1]))
        routm = pen2
        A("vector", lambda e: e.tensor_tensor(routm[0:1, :], routing[0:1, :],
                                              selm[0:1, :], op=ALU.mult))
        A("tensor", lambda e: e.transpose(pbf[3][0:32, 0:1], routm[0:1, 0:32],
                                          idf_sb[0:1, 0:1]))
        A("vector", lambda e: e.tensor_copy(rt_bf[0:32, 0:1], pbf[3][0:32, 0:1]))
        A("tensor", lambda e: e.transpose(pbf[3][0:32, 0:1], routing[0:1, 0:32],
                                          idf_sb[0:1, 0:1]))
        A("vector", lambda e: e.tensor_copy(ro_bf[0:32, 0:1], pbf[3][0:32, 0:1]))
        if upto == "gat":
            dbg = AR.f(2048)
            P.dma("sync", out=out[0:1, :], in_=srep[0:1, :])
            P.dma("sync", out=out[1:2, 0:32], in_=logit[0:1, :])
            P.dma("sync", out=out[2:3, 0:32], in_=maskt[0:1, :])
            P.dma("sync", out=out[3:4, 0:32], in_=selm[0:1, :])
            P.dma("sync", out=out[4:5, 0:32], in_=routing[0:1, :])
            P.dma("sync", out=out[5:6, 0:32], in_=routm[0:1, :])
        wvf = AR.b(2 * D)
        dynsb = AR.b(2 * D)
        for wsrc, rt, wdst in ((w_v, rt_bf, wv_dyn), (o_w, ro_bf, wo_dyn)):
            for chk in range(64):
                b0 = (chk % 2) * D
                P.dma("gpsimd", out=wvf[0:32, b0:b0 + D],
                      in_=wsrc[:, chk * D:(chk + 1) * D])
                for j in range(4):
                    P.add("tensor", (lambda e, rt=rt, b0=b0, j=j: e.matmul(
                        pbf[j][0:1, 0:512], rt[0:32, 0:1],
                        wvf[0:32, b0 + j * 512:b0 + (j + 1) * 512],
                        start=True, stop=True)), 1 if j == 3 else 0, chain=(j > 0))
                for j in range(4):
                    P.add("vector", (lambda e, b0=b0, j=j: e.tensor_copy(
                        dynsb[0:1, b0 + j * 512:b0 + (j + 1) * 512],
                        pbf[j][0:1, 0:512])), 1 if j == 3 else 0, chain=(j > 0))
                P.dma("sync", out=wdst.ap().rearrange("h c -> (h c)")
                      [chk * D:(chk + 1) * D].rearrange("(a b) -> a b", a=1),
                      in_=dynsb[0:1, b0:b0 + D])
        wvT = AR.b(16 * 64)
        wo_sel = AR.b(2048)
        for ch in range(16):
            P.add("sync", (lambda e, ch=ch: e.dma_start_transpose(
                wvT[:, ch * 64:(ch + 1) * 64],
                wv_dyn.ap()[:, ch * 128:(ch + 1) * 128])), 16, chain=(ch > 0))
        P.dma("sync", out=wo_sel[0:64, :], in_=wo_dyn.ap())
        if upto == "gat":
            A("vector", lambda e: e.tensor_copy(dbg[0:64, 0:64], wvT[0:64, 0:64]))
            P.dma("sync", out=out[6:7, 0:512], in_=wvT[0:1, 0:1024].bitcast(F32))
            P.dma("sync", out=out[7:8, 0:512], in_=wo_sel[0:1, 0:1024].bitcast(F32))
            for i in range(8, NTILE):
                pass
            P.emit(block, nc, gs)
            return nc
        cbT = AR.b(1024)
        for t in range(NTILE):
            for ch in range(16):
                P.add("tensor", (lambda e, t=t, ch=ch: e.matmul(
                    pbf[2][0:128, 0:64],
                    ln1fm[:, (ch * 8 + t) * 128:(ch * 8 + t + 1) * 128],
                    wvT[:, ch * 64:(ch + 1) * 64],
                    start=(ch == 0), stop=(ch == 15))), 1 if ch == 15 else 0,
                    chain=(ch > 0))
            A("vector", lambda e, t=t: e.tensor_copy(lntile[:, 0:64],
                                                     pbf[2][0:128, 0:64]))
            A("tensor", lambda e, t=t: e.transpose(pbb[0][0:64, 0:128],
                                                   lntile[:, 0:64], idb_sb[:]))
            A("vector", lambda e, t=t: e.tensor_copy(cbT[0:64, t * 128:(t + 1) * 128],
                                                     pbb[0][0:64, 0:128]))
        if upto == "cmb":
            P.dma("sync", out=out[0:64, 0:512], in_=cbT[0:64, 0:1024].bitcast(F32))
            P.emit(block, nc, gs)
            return nc
        ffn_t = AR.f(512)
        for t in range(NTILE):
            for j in range(4):
                A("tensor", lambda e, t=t, j=j: e.matmul(
                    pbf[3][0:128, :], cbT[0:64, t * 128:(t + 1) * 128],
                    wo_sel[0:64, j * 512:(j + 1) * 512], start=True, stop=True))
                A("vector", lambda e: e.tensor_copy(ffn_t[:], pbf[3][0:128, :]))
                A("vector", lambda e, t=t, j=j: e.tensor_add(
                    xres[:, t * D + j * 512:t * D + (j + 1) * 512],
                    xres[:, t * D + j * 512:t * D + (j + 1) * 512], ffn_t[:]))
        if upto == "x2":
            for i in range(NTILE):
                P.dma("sync", out=out3[i], in_=xres[:, i * D:(i + 1) * D])
            P.emit(block, nc, gs)
            return nc

        AR.reset(32768)
        multiuse = AR.b(16384)
        ln2fm = AR.b(16384)
        wT = AR.b(8192)
        wrow = AR.b(4096)
        v_sb = AR.b(4096)
        ktile = AR.f(512)
        a_sb = AR.f(256)
        mkraw = AR.f(768)
        numsb = AR.f(1024)
        den_sb = AR.f(1024)
        den_cols = AR.f(8)
        rden_cols = AR.f(8)
        coef_cols = AR.f(8)
        coefb = AR.b(8)
        normT = AR.b(96)
        mem_sb = AR.b(768)
        dm_sb = AR.f(128)
        dn_sb = AR.f(768)
        wmv_sb = AR.b(1024)
        biasr = AR.b(2048)
        xm5 = wT[:, 0:4096].bitcast(F32)
        scr5 = wT[:, 4096:6144]
        for i in range(NTILE):
            emit_ln_s(xres[:, i * D:(i + 1) * D], multiuse[:, i * D:(i + 1) * D],
                      xm5, scr5)
            P.dma("sync", out=ln2d3[i], in_=multiuse[:, i * D:(i + 1) * D])
        for i in range(NTILE):
            P.dma("sync", out=xsp3[i], in_=xres[:, i * D:(i + 1) * D],
                  chain=(i > 0))
        xresb = AR.ap[:, 0:32768]
        mkfm_a = [xresb[:, g * 6144:(g + 1) * 6144] for g in range(4)]
        mktm_g = xresb[:, 24576:30720]
        for i in range(128):
            t, ch = divmod(i, 16)
            A("tensor", lambda e, t=t, ch=ch, i=i: e.transpose(
                pbb[i % 2][0:128, 0:128],
                multiuse[:, t * D + ch * 128:t * D + (ch + 1) * 128], idb_sb[:]))
            A("vector", lambda e, t=t, ch=ch, i=i: e.tensor_copy(
                ln2fm[:, (ch * 8 + t) * 128:(ch * 8 + t + 1) * 128],
                pbb[i % 2][0:128, 0:128]))

        def load_wt(Wsrc, j0, npt):
            for jt in range(npt):
                slot = (jt % 2) * D
                P.dma("gpsimd", out=wrow[:, slot:slot + D],
                      in_=Wsrc.rearrange("(t p) d -> t p d", p=128)[j0 + jt])
                for ch in range(16):
                    i = ch
                    A("tensor", lambda e, ch=ch, slot=slot, i=i: e.transpose(
                        pbb[i % 2][0:128, 0:128],
                        wrow[:, slot + ch * 128:slot + (ch + 1) * 128], idb_sb[:]))
                    A("vector", lambda e, ch=ch, jt=jt, i=i: e.tensor_copy(
                        wT[:, ch * 512 + jt * 128:ch * 512 + (jt + 1) * 128],
                        pbb[i % 2][0:128, 0:128]))

        def proj_tile(t, bias_off, fmv, jw=512):
            A("tensor", lambda e, bias_off=bias_off, jw=jw: e.matmul(
                pbf[0][0:128, 0:jw], ones_rb[:],
                biasr[0:1, bias_off:bias_off + jw], start=True, stop=False))
            for ch in range(16):
                P.add("tensor", (lambda e, ch=ch, t=t, jw=jw, fmv=fmv: e.matmul(
                    pbf[0][0:128, 0:jw],
                    fmv[:, (ch * 8 + t) * 128:(ch * 8 + t + 1) * 128],
                    wT[:, ch * 512:ch * 512 + jw],
                    start=False, stop=(ch == 15))), 1 if ch == 15 else 0,
                    chain=(ch > 0))

        def dpfp(srcv, dstv):
            A("scalar", lambda e: e.activation(out=a_sb[:, 0:128], in_=srcv,
                                               func=AF.Relu))
            A("scalar", lambda e: e.activation(out=a_sb[:, 128:256], in_=srcv,
                                               func=AF.Relu, scale=-1.0))
            for u in range(3):
                j = u + 1
                A("vector", lambda e, u=u, j=j: e.tensor_tensor(
                    mkraw[:, u * 256:u * 256 + j], a_sb[:, 0:j],
                    a_sb[:, 256 - j:256], op=ALU.mult))
                A("vector", lambda e, u=u, j=j: e.tensor_tensor(
                    mkraw[:, u * 256 + j:(u + 1) * 256], a_sb[:, j:256],
                    a_sb[:, 0:256 - j], op=ALU.mult))
            A("scalar", lambda e: e.activation(out=numsb[:, 0:768], in_=mkraw[:],
                                               func=AF.Square, accum_out=ss2[:]))
            A("scalar", lambda e: e.activation(out=nrm1[:], in_=ss2[:], func=AF.Sqrt))
            A("vector", lambda e: e.tensor_scalar_max(nmx[:], nrm1[:], 1e-12))
            A("vector", lambda e: e.reciprocal(rmu[:], nmx[:]))
            A("vector", lambda e: e.tensor_scalar_mul(dstv, mkraw[:], rmu[:]))

        P.dma("gpsimd", out=biasr[0:1, 0:512], in_=bk)
        load_wt(Wk, 0, 4)
        mkt_t = AR.b(768)
        for t in range(NTILE):
            proj_tile(t, 0, ln2fm)
            A("vector", lambda e: e.tensor_copy(ktile[:], pbf[0][0:128, 0:512]))
            for g in range(4):
                dpfp(ktile[:, g * 128:(g + 1) * 128], mkt_t[:])
                for fc in range(6):
                    A("tensor", lambda e, fc=fc: e.transpose(
                        pbb[fc % 2][0:128, 0:128],
                        mkt_t[:, fc * 128:(fc + 1) * 128], idb_sb[:]))
                    A("vector", lambda e, g=g, t=t, fc=fc: e.tensor_copy(
                        mkfm_a[g][:, fc * 1024 + t * 128:fc * 1024 + (t + 1) * 128],
                        pbb[fc % 2][0:128, 0:128]))
        P.dma("gpsimd", out=biasr[0:1, 0:512], in_=bv)
        load_wt(Wv, 0, 4)
        for t in range(NTILE):
            proj_tile(t, 0, ln2fm)
            A("vector", lambda e, t=t: e.tensor_copy(v_sb[:, t * 512:(t + 1) * 512],
                                                     pbf[0][0:128, 0:512]))
        P.dma("gpsimd", out=biasr[0:1, :], in_=bmb)
        for jg in range(4):
            load_wt(Wmb, 4 * jg, 4)
            for t in range(NTILE):
                proj_tile(t, jg * 512, ln2fm)
                A("scalar", lambda e, t=t, jg=jg: e.activation(
                    out=multiuse[:, t * D + jg * 512:t * D + (jg + 1) * 512],
                    in_=pbf[0][0:128, 0:512], func=AF.Sigmoid))
        nsb = numsb
        P.dma("sync", out=nsb[0:16, 0:768], in_=norm)
        for fc in range(6):
            A("tensor", lambda e, fc=fc: e.transpose(
                pbf[1][0:128, 0:16], nsb[0:16, fc * 128:(fc + 1) * 128],
                idf_sb[0:16, 0:16]))
            A("vector", lambda e, fc=fc: e.tensor_copy(
                normT[:, fc * 16:(fc + 1) * 16], pbf[1][0:128, 0:16]))
        for g in range(4):
            for i in range(48):
                fc, t = divmod(i, 8)
                A("tensor", lambda e, g=g, fc=fc, t=t, i=i: e.transpose(
                    pbb[i % 2][0:128, 0:128],
                    mkfm_a[g][:, fc * 1024 + t * 128:fc * 1024 + (t + 1) * 128],
                    idb_sb[:]))
                A("vector", lambda e, fc=fc, t=t, i=i: e.tensor_copy(
                    mktm_g[:, t * 768 + fc * 128:t * 768 + (fc + 1) * 128],
                    pbb[i % 2][0:128, 0:128]))
            for h in range(4 * g, 4 * g + 4):
                for half in range(2):
                    for fc in range(6):
                        P.add("tensor", (lambda e, h=h, fc=fc, half=half, g=g: e.matmul(
                            pbf[half][0:1, 0:512],
                            normT[:, fc * 16 + h:fc * 16 + h + 1],
                            mkfm_a[g][:, fc * 1024 + half * 512:fc * 1024 + (half + 1) * 512],
                            start=(fc == 0), stop=(fc == 5))),
                            1 if fc == 5 else 0, chain=(fc > 0))
                for half in range(2):
                    A("vector", lambda e, half=half: e.tensor_scalar_add(
                        den_sb[0:1, half * 512:(half + 1) * 512],
                        pbf[half][0:1, 0:512], 1e-8))
                for t in range(NTILE):
                    A("tensor", lambda e, t=t: e.transpose(
                        pbf[2][0:128, 0:1], den_sb[0:1, t * 128:(t + 1) * 128],
                        idf_sb[0:1, 0:1]))
                    A("vector", lambda e, t=t: e.tensor_copy(
                        den_cols[:, t:t + 1], pbf[2][0:128, 0:1]))
                A("vector", lambda e: e.reciprocal(rden_cols[:], den_cols[:]))
                A("vector", lambda e: e.tensor_scalar(coef_cols[:], den_cols[:],
                                                      -1.0, 1.0, op0=ALU.mult,
                                                      op1=ALU.add))
                A("vector", lambda e: e.tensor_scalar(coef_cols[:], coef_cols[:],
                                                      0.0, 1.0, op0=ALU.max,
                                                      op1=ALU.min))
                A("vector", lambda e: e.tensor_copy(coefb[:], coef_cols[:]))
                P.dma("gpsimd", out=mem_sb[:].rearrange("p (c d) -> p c d", d=128),
                      in_=mem3[h])
                for half in range(2):
                    for fc in range(6):
                        P.add("tensor", (lambda e, fc=fc, half=half, g=g: e.matmul(
                            pbf[half][0:128, 0:512],
                            mem_sb[:, fc * 128:(fc + 1) * 128],
                            mkfm_a[g][:, fc * 1024 + half * 512:fc * 1024 + (half + 1) * 512],
                            start=(fc == 0), stop=(fc == 5))),
                            1 if fc == 5 else 0, chain=(fc > 0))
                for half in range(2):
                    A("vector", lambda e, half=half: e.tensor_copy(
                        numsb[:, half * 512:(half + 1) * 512],
                        pbf[half][0:128, 0:512]))
                for t in range(NTILE):
                    A("tensor", lambda e, t=t: e.transpose(
                        pbf[2][0:128, 0:128], numsb[:, t * 128:(t + 1) * 128],
                        idf_sb[:]))
                    A("vector", lambda e, t=t: e.tensor_scalar_mul(
                        dm_sb[:], pbf[2][0:128, 0:128], rden_cols[:, t:t + 1]))
                    A("vector", lambda e, t=t, g=g: e.tensor_sub(
                        dm_sb[:], v_sb[:, t * 512 + g * 128:t * 512 + (g + 1) * 128],
                        dm_sb[:]))
                    A("vector", lambda e, t=t, h=h: e.tensor_tensor(
                        wmv_sb[:, t * 128:(t + 1) * 128], dm_sb[:],
                        multiuse[:, t * D + h * 128:t * D + (h + 1) * 128],
                        op=ALU.mult))
                for fc in range(6):
                    for t in range(NTILE):
                        P.add("tensor", (lambda e, fc=fc, t=t: e.matmul(
                            pbf[2][0:128, 0:128],
                            mktm_g[:, t * 768 + fc * 128:t * 768 + (fc + 1) * 128],
                            wmv_sb[:, t * 128:(t + 1) * 128],
                            start=(t == 0), stop=(t == NTILE - 1))),
                            1 if t == NTILE - 1 else 0, chain=(t > 0))
                    A("vector", lambda e: e.tensor_copy(dm_sb[:],
                                                        pbf[2][0:128, 0:128]))
                    P.dma("sync", out=dar_in.ap()
                          [h * 768 + fc * 128:h * 768 + (fc + 1) * 128, :],
                          in_=dm_sb[:])
                for t in range(NTILE):
                    P.add("tensor", (lambda e, t=t: e.matmul(
                        pbf[3][0:1, 0:512], coefb[:, t:t + 1],
                        mktm_g[:, t * 768:t * 768 + 512],
                        start=(t == 0), stop=(t == NTILE - 1))), 0, chain=(t > 0))
                    P.add("tensor", (lambda e, t=t: e.matmul(
                        pbf[4][0:1, 0:256], coefb[:, t:t + 1],
                        mktm_g[:, t * 768 + 512:t * 768 + 768],
                        start=(t == 0), stop=(t == NTILE - 1))),
                        1 if t == NTILE - 1 else 0, chain=(t > 0))
                A("vector", lambda e: e.tensor_copy(dn_sb[0:1, 0:512],
                                                    pbf[3][0:1, 0:512]))
                A("vector", lambda e: e.tensor_copy(dn_sb[0:1, 512:768],
                                                    pbf[4][0:1, 0:256]))
                P.dma("sync",
                      out=dar_in.ap()[12288 + h * 6:12288 + (h + 1) * 6, :]
                      .rearrange("r d -> (r d)").rearrange("(a b) -> a b", a=1),
                      in_=dn_sb[0:1, 0:768])
        A("gpsimd", lambda e: e.collective_compute(
            "AllReduce", ALU.add, replica_groups=G8,
            ins=[dar_in.ap().opt()], outs=[dar_out.ap().opt()]))

        AR.reset(0)
        newmem = AR.b(12288)
        ln2fm6 = AR.b(16384)
        q_sb = AR.b(16384)
        gate_sb = AR.b(16384)
        wT6 = AR.b(8192)
        wrow6 = AR.b(4096)
        mqfm = AR.b(768)
        mkb6 = AR.b(768)
        newnormT = AR.b(96)
        biasr6 = AR.b(2048)
        dacc = AR.f(768)
        scrq = AR.f(768)
        a6 = AR.f(256)
        mkraw6 = AR.f(768)
        dena = AR.f(128)
        numq = AR.f(128)
        mh_acc = AR.f(2048)
        x2t6 = AR.f(2048)
        ln2t6 = AR.b(2048)
        nrm6 = AR.f(768)
        ntmp = AR.f(768)
        dar3 = dar_out.ap()[0:12288, :].rearrange("(h c p) d -> h p c d", p=128, c=6)

        def load_wt6(Wsrc, j0, npt):
            for jt in range(npt):
                slot = (jt % 2) * D
                P.dma("gpsimd", out=wrow6[:, slot:slot + D],
                      in_=Wsrc.rearrange("(t p) d -> t p d", p=128)[j0 + jt])
                for ch in range(16):
                    i = ch
                    A("tensor", lambda e, ch=ch, slot=slot, i=i: e.transpose(
                        pbb[i % 2][0:128, 0:128],
                        wrow6[:, slot + ch * 128:slot + (ch + 1) * 128], idb_sb[:]))
                    A("vector", lambda e, ch=ch, jt=jt, i=i: e.tensor_copy(
                        wT6[:, ch * 512 + jt * 128:ch * 512 + (jt + 1) * 128],
                        pbb[i % 2][0:128, 0:128]))

        def proj_tile6(t, bias_off, fmv, jw=512):
            A("tensor", lambda e, bias_off=bias_off, jw=jw: e.matmul(
                pbf[0][0:128, 0:jw], ones_rb[:],
                biasr6[0:1, bias_off:bias_off + jw], start=True, stop=False))
            for ch in range(16):
                P.add("tensor", (lambda e, ch=ch, t=t, jw=jw, fmv=fmv: e.matmul(
                    pbf[0][0:128, 0:jw],
                    fmv[:, (ch * 8 + t) * 128:(ch * 8 + t + 1) * 128],
                    wT6[:, ch * 512:ch * 512 + jw],
                    start=False, stop=(ch == 15))), 1 if ch == 15 else 0,
                    chain=(ch > 0))

        for h in range(16):
            P.dma("sync", out=dacc[:].rearrange("p (c d) -> p c d", d=128),
                  in_=dar3[h])
            P.dma("gpsimd", chain=True,
                  out=scrq[:].rearrange("p (c d) -> p c d", d=128), in_=mem3[h])
            A("vector", lambda e: e.tensor_add(dacc[:], dacc[:], scrq[:]))
            A("vector", lambda e, h=h: e.tensor_copy(newmem[:, h * 768:(h + 1) * 768],
                                                     dacc[:]))
        P.dma("sync", out=nrm6[0:16, 0:768],
              in_=dar_out.ap()[12288:12384, :].rearrange("(h s) d -> h (s d)", h=16))
        P.dma("sync", out=ntmp[0:16, 0:768], in_=norm)
        A("vector", lambda e: e.tensor_add(nrm6[0:16, :], nrm6[0:16, :],
                                           ntmp[0:16, :]))
        for fc in range(6):
            A("tensor", lambda e, fc=fc: e.transpose(
                pbf[1][0:128, 0:16], nrm6[0:16, fc * 128:(fc + 1) * 128],
                idf_sb[0:16, 0:16]))
            A("vector", lambda e, fc=fc: e.tensor_copy(
                newnormT[:, fc * 16:(fc + 1) * 16], pbf[1][0:128, 0:16]))
        for t in range(NTILE):
            P.dma("sync", out=ln2t6[:], in_=ln2d3[t])
            for ch in range(16):
                i = t * 16 + ch
                A("tensor", lambda e, ch=ch, i=i: e.transpose(
                    pbb[i % 2][0:128, 0:128], ln2t6[:, ch * 128:(ch + 1) * 128],
                    idb_sb[:]))
                A("vector", lambda e, t=t, ch=ch, i=i: e.tensor_copy(
                    ln2fm6[:, (ch * 8 + t) * 128:(ch * 8 + t + 1) * 128],
                    pbb[i % 2][0:128, 0:128]))
        P.dma("gpsimd", out=biasr6[0:1, :], in_=bq)
        for jg in range(4):
            load_wt6(Wq, 4 * jg, 4)
            for t in range(NTILE):
                proj_tile6(t, jg * 512, ln2fm6)
                A("vector", lambda e, t=t, jg=jg: e.tensor_copy(
                    q_sb[:, t * D + jg * 512:t * D + (jg + 1) * 512],
                    pbf[0][0:128, 0:512]))
        P.dma("gpsimd", out=biasr6[0:1, :], in_=bg)
        for jg in range(4):
            load_wt6(Wg, 4 * jg, 4)
            for t in range(NTILE):
                proj_tile6(t, jg * 512, ln2fm6)
                A("scalar", lambda e, t=t, jg=jg: e.activation(
                    out=gate_sb[:, t * D + jg * 512:t * D + (jg + 1) * 512],
                    in_=pbf[0][0:128, 0:512], func=AF.Sigmoid))
        for t in range(NTILE):
            for h in range(16):
                A("scalar", lambda e, t=t, h=h: e.activation(
                    out=a6[:, 0:128],
                    in_=q_sb[:, t * D + h * 128:t * D + (h + 1) * 128],
                    func=AF.Relu))
                A("scalar", lambda e, t=t, h=h: e.activation(
                    out=a6[:, 128:256],
                    in_=q_sb[:, t * D + h * 128:t * D + (h + 1) * 128],
                    func=AF.Relu, scale=-1.0))
                for u in range(3):
                    j = u + 1
                    A("vector", lambda e, u=u, j=j: e.tensor_tensor(
                        mkraw6[:, u * 256:u * 256 + j], a6[:, 0:j],
                        a6[:, 256 - j:256], op=ALU.mult))
                    A("vector", lambda e, u=u, j=j: e.tensor_tensor(
                        mkraw6[:, u * 256 + j:(u + 1) * 256], a6[:, j:256],
                        a6[:, 0:256 - j], op=ALU.mult))
                A("scalar", lambda e: e.activation(out=scrq[:], in_=mkraw6[:],
                                                   func=AF.Square, accum_out=ss2[:]))
                A("scalar", lambda e: e.activation(out=nrm1[:], in_=ss2[:],
                                                   func=AF.Sqrt))
                A("vector", lambda e: e.tensor_scalar_max(nmx[:], nrm1[:], 1e-12))
                A("vector", lambda e: e.reciprocal(rmu[:], nmx[:]))
                A("vector", lambda e: e.tensor_scalar_mul(mkb6[:], mkraw6[:], rmu[:]))
                for fc in range(6):
                    A("tensor", lambda e, fc=fc: e.transpose(
                        pbb[fc % 2][0:128, 0:128], mkb6[:, fc * 128:(fc + 1) * 128],
                        idb_sb[:]))
                    A("vector", lambda e, fc=fc: e.tensor_copy(
                        mqfm[:, fc * 128:(fc + 1) * 128], pbb[fc % 2][0:128, 0:128]))
                for fc in range(6):
                    P.add("tensor", (lambda e, fc=fc, h=h: e.matmul(
                        pbf[0][0:128, 0:128],
                        newmem[:, (h * 6 + fc) * 128:(h * 6 + fc + 1) * 128],
                        mqfm[:, fc * 128:(fc + 1) * 128],
                        start=(fc == 0), stop=(fc == 5))), 0, chain=(fc > 0))
                for fc in range(6):
                    P.add("tensor", (lambda e, fc=fc, h=h: e.matmul(
                        pbf[1][0:1, 0:128],
                        newnormT[:, fc * 16 + h:fc * 16 + h + 1],
                        mqfm[:, fc * 128:(fc + 1) * 128],
                        start=(fc == 0), stop=(fc == 5))),
                        1 if fc == 5 else 0, chain=(fc > 0))
                A("vector", lambda e: e.tensor_scalar_add(dena[0:1, 0:128],
                                                          pbf[1][0:1, 0:128], 1e-8))
                A("tensor", lambda e: e.transpose(pbf[2][0:128, 0:1],
                                                  dena[0:1, 0:128],
                                                  idf_sb[0:1, 0:1]))
                A("vector", lambda e: e.tensor_copy(nmx[:], pbf[2][0:128, 0:1]))
                A("vector", lambda e: e.reciprocal(rden[:], nmx[:]))
                A("vector", lambda e: e.tensor_copy(numq[:], pbf[0][0:128, 0:128]))
                A("tensor", lambda e: e.transpose(pbf[3][0:128, 0:128],
                                                  numq[:, 0:128], idf_sb[:]))
                A("vector", lambda e, h=h: e.tensor_scalar_mul(
                    mh_acc[:, h * 128:(h + 1) * 128], pbf[3][0:128, 0:128],
                    rden[:]))
            P.dma("sync", out=x2t6[:], in_=xsp3[t])
            P.dma("sync", chain=True, out=ln2t6[:], in_=ln2d3[t])
            A("vector", lambda e, t=t: e.tensor_tensor(
                mh_acc[:], gate_sb[:, t * D:(t + 1) * D], mh_acc[:], op=ALU.mult))
            A("vector", lambda e: e.tensor_add(x2t6[:], x2t6[:], mh_acc[:]))
            A("vector", lambda e: e.tensor_add(x2t6[:], x2t6[:], ln2t6[:]))
            P.dma("sync", out=xsp3[t], in_=x2t6[:])
        if upto == "x3":
            for i in range(NTILE):
                P.dma("sync", out=x2t6[:], in_=xsp3[i])
                P.dma("sync", out=out3[i], in_=x2t6[:])
            P.emit(block, nc, gs)
            return nc

        AR.reset(0)
        x4fm = AR.b(2 * 8192)
        act = AR.b(14 * 512)
        gw7 = AR.b(2048)
        uw7 = AR.b(2048)
        dwc7 = AR.b(1792)
        routerT = AR.b(256)
        eselb = AR.f(16 * 128)
        lg_sb = AR.f(512)
        lgt = AR.f(16)
        escm = AR.f(16)
        keep = AR.f(16)
        w_tm = AR.f(64)
        w_rows = AR.f(512)
        wb7 = AR.f(16 * 512)
        sg7 = AR.b(512)
        abuf = AR.f(512)
        po_sb = AR.f(512)
        po_t = AR.f(512)
        xt7 = AR.f(2048)
        lnt7 = AR.b(2048)
        xm7 = AR.f(2048)
        scr7 = AR.b(2048)
        res7 = AR.f(2048)
        rw7 = AR.f(2048)
        eb7 = AR.f(16)
        t8b = AR.f(8)
        for i in range(NTILE):
            P.dma("sync", out=xt7[:], in_=xsp3[i])
            emit_ln_s(xt7[:], lnt7[:], xm7, scr7)
            P.dma("sync", out=ln3d3[i], in_=lnt7[:])
        P.dma("sync", out=rw7[0:16, :], in_=rw)
        P.dma("sync", out=eb7[0:1, 0:16], in_=ebias)
        P.dma("sync", out=eselb[0:16, :], in_=esel)
        for ch in range(16):
            A("tensor", lambda e, ch=ch: e.transpose(
                pbf[0][0:128, 0:16], rw7[0:16, ch * 128:(ch + 1) * 128],
                idf_sb[0:16, 0:16]))
            A("vector", lambda e, ch=ch: e.tensor_copy(
                routerT[:, ch * 16:(ch + 1) * 16], pbf[0][0:128, 0:16]))
        for tg in range(2):
            for ch in range(16):
                P.add("sync", (lambda e, tg=tg, ch=ch: e.dma_start_transpose(
                    x4fm[:, tg * 8192 + ch * 512:tg * 8192 + (ch + 1) * 512],
                    ln3d[tg * 512:(tg + 1) * 512, ch * 128:(ch + 1) * 128])), 16,
                    chain=(ch > 0))
            A("tensor", lambda e: e.matmul(pbf[0][0:16, 0:512], eb7[0:1, 0:16],
                                           ones_rf512[:], start=True, stop=False))
            for ch in range(16):
                P.add("tensor", (lambda e, tg=tg, ch=ch: e.matmul(
                    pbf[0][0:16, 0:512], routerT[:, ch * 16:(ch + 1) * 16],
                    x4fm[:, tg * 8192 + ch * 512:tg * 8192 + (ch + 1) * 512],
                    start=False, stop=(ch == 15))), 1 if ch == 15 else 0,
                    chain=(ch > 0))
            A("vector", lambda e: e.tensor_copy(lg_sb[0:16, :], pbf[0][0:16, 0:512]))
            for j in range(4):
                A("tensor", lambda e, j=j: e.transpose(
                    pbf[1][0:128, 0:16], lg_sb[0:16, j * 128:(j + 1) * 128],
                    idf_sb[0:16, 0:16]))
                A("vector", lambda e: e.tensor_copy(lgt[:], pbf[1][0:128, 0:16]))
                A("vector", lambda e: e.reduce_max(rmax[:], lgt[:], axis=AX.X))
                A("vector", lambda e: e.tensor_scalar_mul(nrmax[:], rmax[:], -1.0))
                A("scalar", lambda e: e.activation(out=escm[:], in_=lgt[:],
                                                   func=AF.Exp, bias=nrmax[:]))
                A("vector", lambda e: e.max(t8b[:], escm[:]))
                A("vector", lambda e: e.tensor_copy(wsum[:], t8b[:, 3:4]))
                A("vector", lambda e: e.tensor_scalar(keep[:], escm[:], wsum[:],
                                                      None, op0=ALU.is_ge))
                A("vector", lambda e: e.tensor_tensor(keep[:], keep[:], escm[:],
                                                      op=ALU.mult))
                A("vector", lambda e: e.reduce_sum(wsum[:], keep[:], axis=AX.X))
                A("vector", lambda e: e.reciprocal(rsum[:], wsum[:]))
                A("vector", lambda e, j=j: e.tensor_scalar_mul(
                    w_tm[:, j * 16:(j + 1) * 16], keep[:], rsum[:]))
                A("tensor", lambda e, j=j: e.transpose(
                    pbf[1][0:16, 0:128], w_tm[:, j * 16:(j + 1) * 16], idf_sb[:]))
                A("vector", lambda e, j=j: e.tensor_copy(
                    w_rows[0:16, j * 128:(j + 1) * 128], pbf[1][0:16, 0:128]))
            for ep in range(16):
                A("tensor", lambda e, ep=ep: e.matmul(
                    pbf[1][0:128, 0:512], eselb[0:16, ep * 128:(ep + 1) * 128],
                    w_rows[0:16, :], start=True, stop=True))
                A("vector", lambda e, ep=ep: e.tensor_copy(
                    wb7[:, ep * 512:(ep + 1) * 512], pbf[1][0:128, 0:512]))
            for ep in range(16):
                for ic in range(14):
                    P.dma("gpsimd", out=gw7[:].rearrange("p (c i) -> p c i", i=128),
                          in_=gp[ep][:, ic * 128:(ic + 1) * 128]
                          .rearrange("(c p) i -> p c i", p=128))
                    P.dma("gpsimd", chain=True,
                          out=uw7[:].rearrange("p (c i) -> p c i", i=128),
                          in_=upj[ep][:, ic * 128:(ic + 1) * 128]
                          .rearrange("(c p) i -> p c i", p=128))
                    for ch in range(16):
                        P.add("tensor", (lambda e, tg=tg, ch=ch: e.matmul(
                            pbf[2][:, 0:512], gw7[:, ch * 128:(ch + 1) * 128],
                            x4fm[:, tg * 8192 + ch * 512:tg * 8192 + (ch + 1) * 512],
                            start=(ch == 0), stop=(ch == 15))), 0, chain=(ch > 0))
                    for ch in range(16):
                        P.add("tensor", (lambda e, tg=tg, ch=ch: e.matmul(
                            pbf[3][:, 0:512], uw7[:, ch * 128:(ch + 1) * 128],
                            x4fm[:, tg * 8192 + ch * 512:tg * 8192 + (ch + 1) * 512],
                            start=(ch == 0), stop=(ch == 15))),
                            1 if ch == 15 else 0, chain=(ch > 0))
                    A("scalar", lambda e: e.activation(out=sg7[:], in_=pbf[2][:, 0:512],
                                                       func=AF.Silu))
                    A("vector", lambda e: e.tensor_tensor(abuf[:], pbf[3][:, 0:512],
                                                          sg7[:], op=ALU.mult))
                    A("vector", lambda e, ep=ep, ic=ic: e.tensor_tensor(
                        act[:, ic * 512:(ic + 1) * 512],
                        abuf[:], wb7[:, ep * 512:(ep + 1) * 512], op=ALU.mult))
                for hc in range(16):
                    P.dma("gpsimd", out=dwc7[:].rearrange("p (c h) -> p c h", h=128),
                          in_=dpj[ep][:, hc * 128:(hc + 1) * 128]
                          .rearrange("(c p) h -> p c h", p=128))
                    for ic in range(14):
                        P.add("tensor", (lambda e, ic=ic: e.matmul(
                            pbf[4][:, 0:512], dwc7[:, ic * 128:(ic + 1) * 128],
                            act[:, ic * 512:(ic + 1) * 512],
                            start=(ic == 0), stop=(ic == 13))),
                            1 if ic == 13 else 0, chain=(ic > 0))
                    A("vector", lambda e: e.tensor_copy(po_sb[:], pbf[4][:, 0:512]))
                    for tj in range(4):
                        A("tensor", lambda e, tj=tj: e.transpose(
                            pbf[5][0:128, 0:128], po_sb[:, tj * 128:(tj + 1) * 128],
                            idf_sb[:]))
                        A("vector", lambda e, tj=tj: e.tensor_copy(
                            po_t[:, tj * 128:(tj + 1) * 128], pbf[5][0:128, 0:128]))
                    P.add("gpsimd", (lambda e, tg=tg, ep=ep, hc=hc: e.dma_start(
                        out=out.rearrange("(g tj p) d -> g p tj d", p=128, tj=4)
                        [tg][:, :, hc * 128:(hc + 1) * 128],
                        in_=po_t[:].rearrange("p (tj d) -> p tj d", d=128),
                        **({} if ep == 0 else {"accum_op": ALU.add}))), 16)
        for i in range(NTILE):
            P.dma("sync", out=xt7[:], in_=xsp3[i])
            P.dma("sync", out=res7[:], in_=out3[i])
            A("vector", lambda e: e.tensor_add(res7[:], res7[:], xt7[:]))
            P.dma("sync", out=out3[i], in_=res7[:])
        P.emit(block, nc, gs)
        return nc



def fused_in_maps(d, upto="x4"):
    cns = _attn_consts()
    x = np.ascontiguousarray(
        d["L0_attn_x"].astype(np.float32, copy=False).reshape(NTOK, D))
    base = {
        "sim": d["L0_ffn_sim_matrix"].astype(np.float32),
        "gates": d["L0_ffn_gates"].reshape(1, 32).astype(np.float32),
        "w_v": d["L0_ffn_w_v"].reshape(32, 64 * D).astype(np.float32),
        "o_w": d["L0_ffn_o_weights"].reshape(32, 64 * D).astype(np.float32),
        "Wk": d["L1_attn_Wk"].astype(np.float32),
        "bk": d["L1_attn_bk"].reshape(1, 512).astype(np.float32),
        "Wv": d["L1_attn_Wv"].astype(np.float32),
        "bv": d["L1_attn_bv"].reshape(1, 512).astype(np.float32),
        "Wmb": d["L1_attn_Wmb"].astype(np.float32),
        "bmb": d["L1_attn_bmb"].reshape(1, D).astype(np.float32),
        "Wq": d["L1_attn_Wq"].astype(np.float32),
        "bq": d["L1_attn_bq"].reshape(1, D).astype(np.float32),
        "Wg": d["L1_attn_Wg"].astype(np.float32),
        "bg": d["L1_attn_bg"].reshape(1, D).astype(np.float32),
        "memory": d["L1_attn_memory"].reshape(16 * 768, 128).astype(np.float32),
        "norm": d["L1_attn_norm"].astype(np.float32),
        "ident_bf": cns["ident_bf"], "ident_f32": cns["ident_f32"],
        "ones_row_bf": np.ones((1, 128), dtype=BF),
        "ones_row_f32": np.ones((1, 128), dtype=np.float32),
        "ones_col_bf": np.ones((128, 1), dtype=BF),
        "ones_col_f32": np.ones((128, 1), dtype=np.float32),
    }
    if upto == "x4":
        es = np.zeros((16, 16 * 128), dtype=np.float32)
        for e in range(16):
            es[e, e * 128:(e + 1) * 128] = 1.0
        base.update({
            "rw": d["L1_ffn_router_w"].astype(np.float32),
            "ebias": d["L1_ffn_expert_bias"].reshape(1, 16).astype(np.float32),
            "gp": d["L1_ffn_gate_proj"].astype(np.float32),
            "up": d["L1_ffn_up_proj"].astype(np.float32),
            "dp": d["L1_ffn_down_proj"].astype(np.float32),
            "esel": es,
        })
    in_maps = []
    for c in range(NCORES):
        qbase = (c % 4) * 1024
        mk = np.zeros((32, 128, 1024), dtype=BF)
        for kt in range(32):
            kpos = kt * 128 + np.arange(128)[:, None]
            qpos = qbase + np.arange(1024)[None, :]
            mk[kt] = (kpos <= qpos).astype(BF)
        m = dict(base)
        m["x"] = np.ascontiguousarray(x[c * PTOK:(c + 1) * PTOK])
        m["masks"] = mk
        in_maps.append(m)
    return in_maps


def run_fused(d, upto="x4"):
    nc = build_fused(upto)
    in_maps = fused_in_maps(d, upto)
    res = run_bass_kernel_spmd(nc, in_maps, core_ids=list(range(NCORES)))
    return np.concatenate([res.results[c]["out"] for c in range(NCORES)], 0)



def kernel(**inputs):
    x4 = run_fused(inputs, "x4")
    return x4.reshape(2, 4096, D)


# version 03
# speedup vs baseline: 1.0426x; 1.0426x over previous
import contextlib
import numpy as np
import ml_dtypes
import concourse.bass as bass
import concourse.mybir as mybir
from concourse.bass_utils import run_bass_kernel_spmd

F32 = mybir.dt.float32
BF16 = mybir.dt.bfloat16
U32 = mybir.dt.uint32
AF = mybir.ActivationFunctionType
AX = mybir.AxisListType
ALU = mybir.AluOpType
BF = ml_dtypes.bfloat16

NCORES = 8
NTOK = 8192
PTOK = NTOK // NCORES
D = 2048
NTILE = PTOK // 128
EPS_LN = 1e-5


class Ctr:
    def __init__(self):
        self.c = {}

    def inc(self, instr, sem, n=1):
        assert instr is not None, "tried to attach semaphore to a None instruction"
        instr.then_inc(sem, n)
        self.c[id(sem)] = self.c.get(id(sem), 0) + n
        return self.c[id(sem)]

    def val(self, sem):
        return self.c.get(id(sem), 0)

    def fence(self, eng, instr, sem):
        n = self.inc(instr, sem)
        eng.wait_ge(sem, n)


def _consts():
    return {
        "ident_bf": np.eye(128, dtype=BF),
        "ident_f32": np.eye(128, dtype=np.float32),
    }



def build_ln0():
    nc = bass.Bass("TRN2", target_bir_lowering=False)
    x = nc.dram_tensor("x", [PTOK, D], F32, kind="ExternalInput").ap()
    idf = nc.dram_tensor("ident_f32", [128, 128], F32, kind="ExternalInput").ap()
    ln = nc.dram_tensor("ln", [PTOK, D], BF16, kind="ExternalOutput").ap()
    hn = nc.dram_tensor("hn", [16, 1], F32, kind="ExternalOutput").ap()

    x3 = x.rearrange("(t p) d -> t p d", p=128)
    ln3 = ln.rearrange("(t p) d -> t p d", p=128)
    C = Ctr()
    with contextlib.ExitStack() as es:
        sb = lambda n, s, d: es.enter_context(nc.sbuf_tensor(n, s, d))
        ps = lambda n, s, d: es.enter_context(nc.psum_tensor(n, s, d))
        sem = lambda n: es.enter_context(nc.semaphore(n))
        xt = sb("xt", [128, 2 * D], F32)
        xm = sb("xm", [128, D], F32)
        sq = sb("sq", [128, D], F32)
        lnb = sb("lnb", [128, 2 * D], BF16)
        m = sb("lm", [128, 1], F32)
        v2 = sb("lv2", [128, 1], F32)
        sd = sb("lsd", [128, 1], F32)
        rstd = sb("lrstd", [128, 1], F32)
        rstd2 = sb("lrstd2", [128, 1], F32)
        epsb = sb("epsb", [128, 1], F32)
        hs = sb("hs", [128, 16], F32)
        hmax = sb("hmax", [128, 16], F32)
        idf_sb = sb("idf_sb", [128, 128], F32)
        hn_sb = sb("hn_sb", [16, 1], F32)
        tp_ps = ps("tp_ps", [16, 128], F32)
        ld = sem("ld"); vs = sem("vs"); vf = sem("vf")
        sc = sem("sc"); tt = sem("tt"); st = sem("st")
        block = es.enter_context(nc.Block())
        @block.sync
        def _(s):
            C.inc(s.dma_start(out=idf_sb[:], in_=idf), ld, 16)
            for i in range(NTILE):
                b = (i % 2) * D
                if i >= 2:
                    s.wait_ge(vs, 3 * (i - 2) + 1)
                C.inc(s.dma_start(out=xt[:, b:b + D], in_=x3[i]), ld, 16)

        @block.vector
        def _(v):
            v.memset(hmax[:], 0.0)
            v.memset(epsb[:], EPS_LN)
            for i in range(NTILE):
                b = (i % 2) * D
                v.wait_ge(ld, 16 * (i + 2))
                if i >= 1:
                    v.wait_ge(sc, 3 * (i - 1) + 3)
                C.fence(v, v.reduce_sum(m[:], xt[:, b:b + D], axis=AX.X), vf)
                C.fence(v, v.tensor_scalar_mul(m[:], m[:], 1.0 / D), vf)
                C.inc(v.tensor_scalar_sub(xm[:], xt[:, b:b + D], m[:]), vs, 1)
                v.wait_ge(sc, 3 * i + 2)
                C.fence(v, v.reciprocal(rstd[:], sd[:]), vf)
                i_r2 = v.tensor_scalar(rstd2[:], rstd[:], rstd[:], None, op0=ALU.mult)
                C.inc(i_r2, vs, 1)
                v.wait_ge(vs, C.val(vs))
                C.fence(v, v.reduce_sum(hs[:], sq[:].rearrange("p (h k) -> p h k", k=128), axis=AX.X), vf)
                C.fence(v, v.tensor_scalar_mul(hs[:], hs[:], rstd2[:]), vf)
                i_mx = v.tensor_max(hmax[:], hmax[:], hs[:])
                C.inc(i_mx, vs, 1)
                v.wait_ge(vs, C.val(vs))
            v.wait_ge(tt, 1)
            C.inc(v.reduce_max(hn_sb[:], tp_ps[:], axis=AX.X), vs, 1)

        @block.scalar
        def _(s):
            for i in range(NTILE):
                b = (i % 2) * D
                s.wait_ge(vs, 3 * i + 1)
                C.inc(s.activation(out=sq[:], in_=xm[:], func=AF.Square, accum_out=v2[:]), sc, 1)
                s.wait_ge(sc, C.val(sc))
                C.inc(s.activation(out=sd[:], in_=v2[:], func=AF.Sqrt, scale=1.0 / D, bias=epsb[:, 0:1]), sc, 1)
                s.wait_ge(vs, 3 * i + 2)
                if i >= 2:
                    s.wait_ge(st, 16 * (i - 1))
                C.inc(s.activation(out=lnb[:, b:b + D], in_=xm[:], func=AF.Identity, scale=rstd[:], bias=0.0), sc, 1)

        @block.tensor
        def _(t):
            t.wait_ge(vs, 3 * NTILE)
            t.wait_ge(ld, 16)
            C.inc(t.transpose(tp_ps[:], hmax[:, 0:16], idf_sb[:]), tt, 1)

        @block.gpsimd
        def _(g):
            for i in range(NTILE):
                g.wait_ge(sc, 3 * i + 3)
                b = (i % 2) * D
                C.inc(g.dma_start(out=ln3[i], in_=lnb[:, b:b + D]), st, 16)
            g.wait_ge(vs, 3 * NTILE + 1)
            C.inc(g.dma_start(out=hn, in_=hn_sb[:]), st, 16)
            g.wait_ge(st, 16 * (NTILE + 1))
    return nc


def run_ln0(x_full):
    nc = build_ln0()
    cns = _consts()
    in_maps = [{"x": np.ascontiguousarray(x_full[c * PTOK:(c + 1) * PTOK]),
                "ident_f32": cns["ident_f32"]} for c in range(NCORES)]
    res = run_bass_kernel_spmd(nc, in_maps, core_ids=list(range(NCORES)))
    ln = np.concatenate([res.results[c]["ln"] for c in range(NCORES)], 0)
    hn = np.stack([res.results[c]["hn"][:, 0] for c in range(NCORES)], 0)
    return ln, hn



def build_attn0():
    nc = bass.Bass("TRN2", target_bir_lowering=False)
    lnh = nc.dram_tensor("lnh", [4, 4096, 128], BF16, kind="ExternalInput").ap()
    hn_sel = nc.dram_tensor("hn_sel", [4, 4], F32, kind="ExternalInput").ap()
    masks = nc.dram_tensor("masks", [4, 128, 512], BF16, kind="ExternalInput").ap()
    idb = nc.dram_tensor("ident_bf", [128, 128], BF16, kind="ExternalInput").ap()
    idf = nc.dram_tensor("ident_f32", [128, 128], F32, kind="ExternalInput").ap()
    onescol = nc.dram_tensor("ones_col_bf", [128, 1], BF16, kind="ExternalInput").ap()
    onesrow = nc.dram_tensor("ones_row_f32", [1, 128], F32, kind="ExternalInput").ap()
    attn = nc.dram_tensor("attn", [4, 4096, 128], F32, kind="ExternalOutput").ap()

    SC = 1.0 / float(np.sqrt(128.0))
    steps = []
    for p in range(4):
        for qg in range(8):
            nk = 4 * (qg + 1)
            for k in range(nk):
                steps.append((p, qg, k, k == 0, k == nk - 1,
                              (k - 4 * qg) if k >= 4 * qg else None))
    NSTEP = len(steps)
    nmask_upto = []
    nm = 0
    for s in steps:
        if s[5] is not None:
            nm += 1
        nmask_upto.append(nm)

    C = Ctr()
    with contextlib.ExitStack() as es:
        sb = lambda n, s, d: es.enter_context(nc.sbuf_tensor(n, s, d))
        ps = lambda n, s, d: es.enter_context(nc.psum_tensor(n, s, d))
        sem = lambda n: es.enter_context(nc.semaphore(n))
        lnh_sb = [sb(f"lnh{p}", [128, 4096], BF16) for p in range(4)]
        lnfm = [sb(f"lnfm{p}", [128, 4096], BF16) for p in range(4)]
        mask_sb = sb("mask_sb", [128, 4 * 512], BF16)
        idb_sb = sb("idb_sb", [128, 128], BF16)
        idf_sb = sb("idf_sb", [128, 128], F32)
        ones_c = sb("ones_c", [128, 1], BF16)
        ones_r = sb("ones_r", [1, 128], F32)
        hns_sb = sb("hns_sb", [4, 4], F32)
        mb4 = sb("mb4", [4, 1], F32)
        negm4 = sb("negm4", [4, 1], F32)
        negmrow = sb("negmrow", [1, 4], F32)
        negMb = [sb(f"negMb{p}", [128, 1], F32) for p in range(4)]
        pt_sb = sb("pt_sb", [128, 2 * 512], BF16)
        sums_sb = sb("sums_sb", [1, 512], F32)
        recip_sb = sb("recip_sb", [1, 512], F32)
        sums_cols = sb("sums_cols", [128, 4], F32)
        acc_sb = sb("acc_sb", [128, 512], F32)
        out_sb = sb("out_sb", [128, 512], F32)
        st_ps = ps("st_ps", [128, 2 * 512], F32)
        acc_ps = ps("acc_ps", [128, 512], F32)
        sum_ps = ps("sum_ps", [1, 512], F32)
        tp_ps = [ps("tp_ps0", [128, 128], BF16), ps("tp_ps1", [128, 128], BF16)]
        att_ps = ps("att_ps", [128, 512], F32)
        mp_ps = ps("mp_ps", [128, 8], F32)
        ld = sem("ld"); tt = sem("tt"); vc = sem("vc"); vf = sem("vf")
        stm = sem("stm"); se = sem("se"); vm = sem("vm"); pv = sem("pv")
        ve = sem("ve"); ta = sem("ta"); so = sem("so")
        block = es.enter_context(nc.Block())

        NLOAD = 10

        @block.sync
        def _(s):
            C.inc(s.dma_start(out=idb_sb[:], in_=idb), ld, 16)
            C.inc(s.dma_start(out=idf_sb[:], in_=idf), ld, 16)
            C.inc(s.dma_start(out=ones_c[:], in_=onescol), ld, 16)
            C.inc(s.dma_start(out=ones_r[:], in_=onesrow), ld, 16)
            C.inc(s.dma_start(out=hns_sb[:], in_=hn_sel), ld, 16)
            C.inc(s.dma_start(out=mask_sb[:].rearrange("p (m q) -> p m q", q=512),
                              in_=masks.rearrange("m p q -> p m q")), ld, 16)
            for p in range(4):
                C.inc(s.dma_start(out=lnh_sb[p][:].rearrange("p (t d) -> p t d", d=128),
                                  in_=lnh[p].rearrange("(t p) d -> p t d", p=128)), ld, 16)

        @block.tensor
        def _(t):
            t.wait_ge(ld, 16 * NLOAD)
            t.wait_ge(vc, 1)
            C.inc(t.transpose(mp_ps[0:1, 0:4], negm4[:], idf_sb[0:4, 0:4]), tt, 1)
            t.wait_ge(vc, 2)
            C.inc(t.matmul(mp_ps[:, 4:8], ones_r[:], negmrow[:], start=True, stop=True), tt, 1)
            for i in range(128):
                p, tl = divmod(i, 32)
                if i >= 2:
                    t.wait_ge(vc, 6 + (i - 1))
                C.inc(t.transpose(tp_ps[i % 2][:], lnh_sb[p][:, tl * 128:(tl + 1) * 128],
                                  idb_sb[:]), tt, 1)
            for si, (p, qg, k, kf, kl, mj) in enumerate(steps):
                e = p * 8 + qg
                if kf:
                    t.wait_ge(vc, 6 + 128)
                    if e >= 1:
                        t.wait_ge(ve, 5 * (e - 1) + 4)
                if si >= 2:
                    t.wait_ge(se, si - 1)
                C.inc(t.matmul(st_ps[:, (si % 2) * 512:(si % 2 + 1) * 512],
                               lnfm[p][:, k * 128:(k + 1) * 128],
                               lnfm[p][:, qg * 512:(qg + 1) * 512],
                               start=True, stop=True), stm, 1)
                if mj is not None:
                    t.wait_ge(vm, nmask_upto[si])
                else:
                    t.wait_ge(se, si + 1)
                t.matmul(acc_ps[:], lnh_sb[p][:, k * 128:(k + 1) * 128],
                         pt_sb[:, (si % 2) * 512:(si % 2 + 1) * 512],
                         start=kf, stop=kl)
                C.inc(t.matmul(sum_ps[:], ones_c[:],
                               pt_sb[:, (si % 2) * 512:(si % 2 + 1) * 512],
                               start=kf, stop=kl), pv, 1)
                if kl:
                    t.wait_ge(ve, 5 * e + 2)
                    for j in range(4):
                        C.inc(t.transpose(att_ps[:, j:j + 1],
                                          recip_sb[0:1, j * 128:(j + 1) * 128],
                                          idf_sb[0:1, 0:1]), ta, 1)
                    t.wait_ge(ve, 5 * e + 4)
                    for j in range(4):
                        C.inc(t.transpose(att_ps[:, j * 128:(j + 1) * 128],
                                          acc_sb[:, j * 128:(j + 1) * 128],
                                          idf_sb[:]), ta, 1)

        @block.vector
        def _(v):
            v.wait_ge(ld, 16 * NLOAD)
            C.fence(v, v.reduce_max(mb4[:], hns_sb[:], axis=AX.X), vf)
            C.inc(v.tensor_scalar_mul(negm4[:], mb4[:], -SC), vc, 1)
            v.wait_ge(tt, 1)
            C.inc(v.tensor_copy(negmrow[:], mp_ps[0:1, 0:4]), vc, 1)
            v.wait_ge(tt, 2)
            for j in range(4):
                C.inc(v.tensor_copy(negMb[j][:], mp_ps[:, 4 + j:5 + j]), vc, 1)
            for i in range(128):
                p, tl = divmod(i, 32)
                v.wait_ge(tt, 2 + i + 1)
                C.inc(v.tensor_copy(lnfm[p][:, tl * 128:(tl + 1) * 128],
                                    tp_ps[i % 2][:]), vc, 1)
            for si, (p, qg, k, kf, kl, mj) in enumerate(steps):
                e = p * 8 + qg
                if mj is not None:
                    v.wait_ge(se, si + 1)
                    C.inc(v.tensor_tensor(pt_sb[:, (si % 2) * 512:(si % 2 + 1) * 512],
                                          pt_sb[:, (si % 2) * 512:(si % 2 + 1) * 512],
                                          mask_sb[:, mj * 512:(mj + 1) * 512],
                                          op=ALU.mult), vm, 1)
                if kl:
                    v.wait_ge(pv, si + 1)
                    C.inc(v.tensor_copy(sums_sb[:], sum_ps[:]), ve, 1)
                    v.wait_ge(ve, C.val(ve))
                    C.inc(v.reciprocal(recip_sb[:], sums_sb[:]), ve, 1)
                    v.wait_ge(ta, 8 * e + 4)
                    C.inc(v.tensor_copy(sums_cols[:], att_ps[:, 0:4]), ve, 1)
                    v.wait_ge(ve, C.val(ve))
                    C.inc(v.tensor_copy(acc_sb[:], acc_ps[:]), ve, 1)
                    v.wait_ge(ta, 8 * e + 8)
                    if e >= 1:
                        v.wait_ge(so, 16 * e)
                    for j in range(4):
                        i_ts = v.tensor_scalar_mul(out_sb[:, j * 128:(j + 1) * 128],
                                                   att_ps[:, j * 128:(j + 1) * 128],
                                                   sums_cols[:, j:j + 1])
                        if j == 3:
                            C.inc(i_ts, ve, 1)

        @block.scalar
        def _(s):
            s.wait_ge(vc, 6)
            for si, (p, qg, k, kf, kl, mj) in enumerate(steps):
                s.wait_ge(stm, si + 1)
                if si >= 2:
                    s.wait_ge(pv, si - 1)
                C.inc(s.activation(out=pt_sb[:, (si % 2) * 512:(si % 2 + 1) * 512],
                                   in_=st_ps[:, (si % 2) * 512:(si % 2 + 1) * 512],
                                   func=AF.Exp, scale=SC, bias=negMb[p][:]), se, 1)

        @block.gpsimd
        def _(g):
            for e in range(32):
                p, qg = divmod(e, 8)
                g.wait_ge(ve, 5 * e + 5)
                C.inc(g.dma_start(
                    out=attn[p].rearrange("(g j p) d -> g p j d", p=128, j=4)[qg],
                    in_=out_sb[:].rearrange("p (j d) -> p j d", d=128)), so, 16)
            g.wait_ge(so, 16 * 32)
    return nc



def build_resid_ln(with_ssum=True):
    nc = bass.Bass("TRN2", target_bir_lowering=False)
    x = nc.dram_tensor("x", [PTOK, D], F32, kind="ExternalInput").ap()
    add = nc.dram_tensor("add", [PTOK, D], F32, kind="ExternalInput").ap()
    onescol = nc.dram_tensor("ones_col_bf", [128, 1], BF16, kind="ExternalInput").ap()
    x1o = nc.dram_tensor("x1", [PTOK, D], F32, kind="ExternalOutput").ap()
    lno = nc.dram_tensor("ln", [PTOK, D], BF16, kind="ExternalOutput").ap()
    sso = nc.dram_tensor("ssum", [1, D], F32, kind="ExternalOutput").ap()

    x3 = x.rearrange("(t p) d -> t p d", p=128)
    a3 = add.rearrange("(t p) d -> t p d", p=128)
    x13 = x1o.rearrange("(t p) d -> t p d", p=128)
    ln3 = lno.rearrange("(t p) d -> t p d", p=128)
    C = Ctr()
    with contextlib.ExitStack() as es:
        sb = lambda n, s, d: es.enter_context(nc.sbuf_tensor(n, s, d))
        ps = lambda n, s, d: es.enter_context(nc.psum_tensor(n, s, d))
        sem = lambda n: es.enter_context(nc.semaphore(n))
        xt = sb("xt", [128, 2 * D], F32)
        at = sb("at", [128, 2 * D], F32)
        x1b = sb("x1b", [128, 2 * D], F32)
        xm = sb("xm", [128, D], F32)
        sq = sb("sq", [128, D], F32)
        lnb = sb("lnb", [128, 2 * D], BF16)
        m = sb("lm", [128, 1], F32)
        v2 = sb("lv2", [128, 1], F32)
        sd = sb("lsd", [128, 1], F32)
        rstd = sb("lrstd", [128, 1], F32)
        epsb = sb("epsb", [128, 1], F32)
        ones_c = sb("ones_c", [128, 1], BF16)
        ss_sb = sb("ss_sb", [1, D], F32)
        ss_ps = ps("ss_ps", [1, D], F32)
        ld = sem("ld"); vs = sem("vs"); vx = sem("vx"); vf = sem("vf")
        sc = sem("sc"); tm = sem("tm"); st = sem("st")
        block = es.enter_context(nc.Block())

        @block.sync
        def _(s):
            C.inc(s.dma_start(out=ones_c[:], in_=onescol), ld, 16)
            for i in range(NTILE):
                b = (i % 2) * D
                if i >= 2:
                    s.wait_ge(vx, i - 1)
                C.inc(s.dma_start(out=xt[:, b:b + D], in_=x3[i]), ld, 16)
                C.inc(s.dma_start(out=at[:, b:b + D], in_=a3[i]), ld, 16)

        @block.vector
        def _(v):
            v.memset(epsb[:], EPS_LN)
            for i in range(NTILE):
                b = (i % 2) * D
                v.wait_ge(ld, 16 * (2 * i + 3))
                if i >= 2:
                    v.wait_ge(st, 16 * (2 * (i - 2) + 1))
                C.inc(v.tensor_add(x1b[:, b:b + D], xt[:, b:b + D], at[:, b:b + D]), vx, 1)
                C.fence(v, v.reduce_sum(m[:], x1b[:, b:b + D], axis=AX.X), vf)
                C.fence(v, v.tensor_scalar_mul(m[:], m[:], 1.0 / D), vf)
                if i >= 1:
                    v.wait_ge(sc, 3 * (i - 1) + 3)
                    if with_ssum:
                        v.wait_ge(tm, i)
                C.inc(v.tensor_scalar_sub(xm[:], x1b[:, b:b + D], m[:]), vs, 1)
                v.wait_ge(sc, 3 * i + 2)
                C.inc(v.reciprocal(rstd[:], sd[:]), vs, 1)
            if with_ssum:
                v.wait_ge(tm, NTILE)
                C.inc(v.tensor_copy(ss_sb[:], ss_ps[:]), vs, 1)

        @block.scalar
        def _(s):
            for i in range(NTILE):
                b = (i % 2) * D
                s.wait_ge(vs, 2 * i + 1)
                if i >= 2:
                    s.wait_ge(tm, i - 1)
                    s.wait_ge(st, 16 * (2 * (i - 2) + 2))
                C.inc(s.activation(out=sq[:], in_=xm[:], func=AF.Square, accum_out=v2[:]), sc, 1)
                s.wait_ge(sc, C.val(sc))
                C.inc(s.activation(out=sd[:], in_=v2[:], func=AF.Sqrt, scale=1.0 / D,
                                   bias=epsb[:, 0:1]), sc, 1)
                s.wait_ge(vs, 2 * i + 2)
                C.inc(s.activation(out=lnb[:, b:b + D], in_=xm[:], func=AF.Identity,
                                   scale=rstd[:], bias=0.0), sc, 1)

        @block.tensor
        def _(t):
            if with_ssum:
                for i in range(NTILE):
                    t.wait_ge(vs, 2 * i + 2)
                    ii = None
                    for j in range(4):
                        ii = t.matmul(ss_ps[0:1, j * 512:(j + 1) * 512], rstd[:],
                                      xm[:, j * 512:(j + 1) * 512],
                                      start=(i == 0), stop=(i == NTILE - 1))
                    C.inc(ii, tm, 1)

        @block.gpsimd
        def _(g):
            for i in range(NTILE):
                b = (i % 2) * D
                g.wait_ge(vx, i + 1)
                C.inc(g.dma_start(out=x13[i], in_=x1b[:, b:b + D]), st, 16)
                g.wait_ge(sc, 3 * i + 3)
                if with_ssum:
                    g.wait_ge(tm, i + 1)
                C.inc(g.dma_start(out=ln3[i], in_=lnb[:, b:b + D]), st, 16)
            if with_ssum:
                g.wait_ge(vs, 2 * NTILE + 1)
                C.inc(g.dma_start(out=sso, in_=ss_sb[:]), st, 16)
            g.wait_ge(st, C.val(st))
    return nc


def run_resid_ln(x_full, add_full, with_ssum=True):
    nc = build_resid_ln(with_ssum)
    ones = np.ones((128, 1), dtype=BF)
    in_maps = [{"x": np.ascontiguousarray(x_full[c * PTOK:(c + 1) * PTOK]),
                "add": np.ascontiguousarray(add_full[c * PTOK:(c + 1) * PTOK]),
                "ones_col_bf": ones} for c in range(NCORES)]
    res = run_bass_kernel_spmd(nc, in_maps, core_ids=list(range(NCORES)))
    x1 = np.concatenate([res.results[c]["x1"] for c in range(NCORES)], 0)
    ln = np.concatenate([res.results[c]["ln"] for c in range(NCORES)], 0)
    ss = np.stack([res.results[c]["ssum"][0] for c in range(NCORES)], 0)
    return x1, ln, ss



class Plan:

    def __init__(self):
        self.ops = []

    def add(self, eng, fn, inc=1, chain=False):
        self.ops.append((eng, fn, inc, chain))

    def ch(self, eng, fn):
        self.add(eng, fn, 0, chain=True)

    def dma(self, eng, chain=False, **kw):
        self.add(eng, lambda e: e.dma_start(**kw), 16, chain)

    def emit(self, block, nc, gs, engines=("sync", "tensor", "vector", "scalar", "gpsimd")):
        pre = [0]
        for (_, _, inc, _) in self.ops:
            pre.append(pre[-1] + inc)

        def mk(name):
            def closure(eng):
                for idx, (ename, fn, inc, chain) in enumerate(self.ops):
                    if ename == name:
                        if not chain:
                            if pre[idx] > 0:
                                eng.wait_ge(gs, pre[idx])
                            instr = fn(eng)
                            if inc:
                                instr.then_inc(gs, inc)
                        else:
                            instr = fn(eng)
                            if inc:
                                instr.then_inc(gs, inc)
                if name == "gpsimd":
                    eng.wait_ge(gs, pre[-1])
            return closure

        for name in engines:
            getattr(block, name)(mk(name))



def build_ffn0():
    nc = bass.Bass("TRN2", target_bir_lowering=False)
    x1 = nc.dram_tensor("x1", [PTOK, D], F32, kind="ExternalInput").ap()
    ln1 = nc.dram_tensor("ln1", [PTOK, D], BF16, kind="ExternalInput").ap()
    ssall = nc.dram_tensor("ssall", [8, D], F32, kind="ExternalInput").ap()
    sim = nc.dram_tensor("sim", [D, 32], F32, kind="ExternalInput").ap()
    gates = nc.dram_tensor("gates", [1, 32], F32, kind="ExternalInput").ap()
    w_v = nc.dram_tensor("w_v", [32, 64 * D], F32, kind="ExternalInput").ap()
    o_w = nc.dram_tensor("o_w", [32, 64 * D], F32, kind="ExternalInput").ap()
    bselt = nc.dram_tensor("bsel", [8, 2], F32, kind="ExternalInput").ap()
    wselt = nc.dram_tensor("wsel", [1, 2], F32, kind="ExternalInput").ap()
    idf = nc.dram_tensor("ident_f32", [128, 128], F32, kind="ExternalInput").ap()
    idb = nc.dram_tensor("ident_bf", [128, 128], BF16, kind="ExternalInput").ap()
    onesr = nc.dram_tensor("ones_row_f32", [1, 128], F32, kind="ExternalInput").ap()
    onesc32 = nc.dram_tensor("ones_col_f32", [128, 1], F32, kind="ExternalInput").ap()
    x2o = nc.dram_tensor("x2", [PTOK, D], F32, kind="ExternalOutput").ap()

    wv_dyn = nc.dram_tensor("wv_dyn", [2, 64 * D], BF16)
    wo_dyn = nc.dram_tensor("wo_dyn", [2, 64 * D], BF16)
    wv3 = wv_dyn.rearrange("b (h c) -> b h c", c=D)
    wo3 = wo_dyn.rearrange("b (h c) -> b h c", c=D)

    x13 = x1.rearrange("(t p) d -> t p d", p=128)
    x23 = x2o.rearrange("(t p) d -> t p d", p=128)

    with contextlib.ExitStack() as es:
        sb = lambda n, s, d: es.enter_context(nc.sbuf_tensor(n, s, d))
        ps = lambda n, s, d: es.enter_context(nc.psum_tensor(n, s, d))
        sem = lambda n: es.enter_context(nc.semaphore(n))
        ss_sb = sb("ss_sb", [8, D], F32)
        bsel_sb = sb("bsel_sb", [8, 2], F32)
        wsel_sb = sb("wsel_sb", [1, 2], F32)
        sim_sb = sb("sim_sb", [128, 16 * 32], F32)
        simsq = sb("simsq", [128, 16 * 32], F32)
        srep = sb("srep", [2, D], F32)
        srsq = sb("srsq", [2, D], F32)
        srf = sb("srf", [128, 32], F32)
        gsg = sb("gsg", [1, 32], F32)
        sgrow = sb("sgrow", [1, 32], F32)
        s2 = sb("s2", [2, 1], F32)
        snorm = sb("snorm", [2, 1], F32)
        rn = sb("rn", [2, 1], F32)
        cn2 = sb("cn2", [1, 32], F32)
        cnq = sb("cnq", [1, 32], F32)
        rcn = sb("rcn", [1, 32], F32)
        rcn2 = sb("rcn2", [2, 32], F32)
        sg2 = sb("sg2", [2, 32], F32)
        aff = sb("aff", [2, 32], F32)
        logit = sb("logit", [2, 32], F32)
        gated = sb("gated", [2, 32], F32)
        maskt = sb("maskt", [2, 32], F32)
        msum = sb("msum", [2, 1], F32)
        inact = sb("inact", [2, 1], F32)
        top8 = sb("top8", [2, 8], F32)
        m8 = sb("m8", [2, 1], F32)
        tmpa = sb("tmpa", [2, 32], F32)
        m16 = sb("m16", [2, 1], F32)
        fbm = sb("fbm", [2, 32], F32)
        selm = sb("selm", [2, 32], F32)
        ml = sb("ml", [2, 32], F32)
        rmax = sb("rmax", [2, 1], F32)
        nrmax = sb("nrmax", [2, 1], F32)
        esf = sb("esf", [2, 32], F32)
        essum = sb("essum", [2, 1], F32)
        rsum = sb("rsum", [2, 1], F32)
        routing = sb("routing", [2, 32], F32)
        routm = sb("routm", [2, 32], F32)
        rt_bf = sb("rt_bf", [32, 2], BF16)
        ro_bf = sb("ro_bf", [32, 2], BF16)
        wvf = sb("wvf", [32, 2 * D], BF16)
        dyn_sb = sb("dyn_sb", [2, 2 * D], BF16)
        ws0 = sb("ws0", [128, 1], F32)
        ws1 = sb("ws1", [128, 1], F32)
        wvT = sb("wvT", [128, 2 * 16 * 64], BF16)
        wvsel = sb("wvsel", [128, 16 * 64], BF16)
        wvtmp = sb("wvtmp", [128, 16 * 64], BF16)
        wo_sb = sb("wo_sb", [64, 2 * D], BF16)
        wo_sel = sb("wo_sel", [64, D], BF16)
        wo_tmp = sb("wo_tmp", [64, D], BF16)
        ln1_sb = sb("ln1_sb", [128, NTILE * D], BF16)
        lnfm = sb("lnfm", [128, NTILE * D], BF16)
        cb_sb = sb("cb_sb", [128, NTILE * 64], BF16)
        cbT = sb("cbT", [64, PTOK], BF16)
        x1t = sb("x1t", [128, D], F32)
        x2t = sb("x2t", [128, D], F32)
        idf_sb = sb("idf_sb", [128, 128], F32)
        idb_sb = sb("idb_sb", [128, 128], BF16)
        ones_r = sb("ones_r", [1, 128], F32)
        ones_c32 = sb("ones_c32", [128, 1], F32)
        pb = [ps(f"pb{i}", [128, 512], F32) for i in range(5)]
        pbb = [ps(f"pbb{i}", [128, 1024], BF16) for i in range(2)]
        gs = sem("gs")
        block = es.enter_context(nc.Block())

        P = Plan()
        A = P.add
        P.dma("sync", out=ss_sb[:], in_=ssall)
        P.dma("sync", out=sim_sb[:].rearrange("p (c e) -> p c e", e=32),
              in_=sim.rearrange("(c p) e -> p c e", p=128))
        P.dma("sync", out=gsg[:], in_=gates)
        P.dma("sync", out=idf_sb[:], in_=idf)
        P.dma("sync", out=idb_sb[:], in_=idb)
        P.dma("sync", out=ones_r[:], in_=onesr)
        P.dma("sync", out=ones_c32[:], in_=onesc32)
        P.dma("sync", out=bsel_sb[:], in_=bselt)
        P.dma("sync", out=wsel_sb[:], in_=wselt)
        P.dma("sync", out=ln1_sb[:].rearrange("p (t d) -> p t d", d=D),
              in_=ln1.rearrange("(t p) d -> p t d", p=128))
        for j in range(4):
            A("tensor", lambda e, j=j: e.matmul(pb[j][0:2, :], bsel_sb[:],
                                                ss_sb[:, j * 512:(j + 1) * 512],
                                                start=True, stop=True))
        for j in range(4):
            A("vector", lambda e, j=j: e.tensor_scalar_mul(
                srep[0:2, j * 512:(j + 1) * 512], pb[j][0:2, :], 1.0 / 4096.0))
        A("scalar", lambda e: e.activation(out=srsq[:], in_=srep[:], func=AF.Square,
                                           accum_out=s2[:]))
        A("scalar", lambda e: e.activation(out=snorm[:], in_=s2[:], func=AF.Sqrt))
        A("vector", lambda e: e.reciprocal(rn[:], snorm[:]))
        for ch in range(16):
            A("tensor", lambda e, ch=ch: e.transpose(
                pb[4][0:128, 0:2], srep[0:2, ch * 128:(ch + 1) * 128], idf_sb[0:2, 0:2]))
            A("vector", lambda e, ch=ch: e.tensor_copy(
                srf[:, 2 * ch:2 * ch + 2], pb[4][0:128, 0:2]))
        for ch in range(16):
            A("tensor", lambda e, ch=ch: e.matmul(
                pb[0][0:2, 0:32], srf[:, 2 * ch:2 * ch + 2],
                sim_sb[:, ch * 32:(ch + 1) * 32],
                start=(ch == 0), stop=(ch == 15)))
        A("scalar", lambda e: e.activation(out=simsq[:], in_=sim_sb[:], func=AF.Square))
        A("tensor", lambda e: e.matmul(pb[1][0:1, 0:512], ones_c32[:], simsq[:],
                                       start=True, stop=True))
        A("vector", lambda e: e.reduce_sum(
            cn2[:], pb[1][0:1, 0:512].rearrange("p (c e) -> p e c", e=32), axis=AX.X))
        A("scalar", lambda e: e.activation(out=cnq[:], in_=cn2[:], func=AF.Sqrt))
        A("vector", lambda e: e.reciprocal(rcn[:], cnq[:]))
        A("vector", lambda e: e.tensor_copy(aff[:], pb[0][0:2, 0:32]))
        A("vector", lambda e: e.tensor_scalar_mul(aff[:], aff[:], rn[:]))
        A("tensor", lambda e: e.matmul(pb[2][0:2, 0:32], ones_r[0:1, 0:2], rcn[:],
                                       start=True, stop=True))
        A("vector", lambda e: e.tensor_copy(rcn2[:], pb[2][0:2, 0:32]))
        A("vector", lambda e: e.tensor_tensor(aff[:], aff[:], rcn2[:], op=ALU.mult))
        A("scalar", lambda e: e.activation(out=sgrow[:], in_=gsg[:], func=AF.Sigmoid))
        A("tensor", lambda e: e.matmul(pb[2][0:2, 0:32], ones_r[0:1, 0:2], sgrow[:],
                                       start=True, stop=True))
        A("vector", lambda e: e.tensor_copy(sg2[:], pb[2][0:2, 0:32]))
        A("vector", lambda e: e.tensor_sub(logit[:], aff[:], sg2[:]))
        A("vector", lambda e: e.tensor_scalar_max(gated[:], logit[:], 0.0))
        A("vector", lambda e: e.tensor_scalar(maskt[:], logit[:], 0.0, None, op0=ALU.is_gt))
        A("vector", lambda e: e.reduce_sum(msum[:], maskt[:], axis=AX.X))
        A("vector", lambda e: e.tensor_scalar(inact[:], msum[:], 0.0, None, op0=ALU.is_equal))
        A("vector", lambda e: e.max(top8[:], logit[:]))
        A("vector", lambda e: e.tensor_copy(m8[:], top8[:, 7:8]))
        A("vector", lambda e: e.tensor_scalar(tmpa[:], logit[:], m8[:], -1e30,
                                              op0=ALU.is_ge, op1=ALU.mult))
        A("vector", lambda e: e.tensor_add(tmpa[:], logit[:], tmpa[:]))
        A("vector", lambda e: e.max(top8[:], tmpa[:]))
        A("vector", lambda e: e.tensor_copy(m16[:], top8[:, 7:8]))
        A("vector", lambda e: e.tensor_scalar(fbm[:], logit[:], m16[:], None, op0=ALU.is_ge))
        A("vector", lambda e: e.tensor_scalar_mul(fbm[:], fbm[:], inact[:]))
        A("vector", lambda e: e.tensor_max(selm[:], maskt[:], fbm[:]))
        A("vector", lambda e: e.tensor_tensor(ml[:], gated[:], selm[:], op=ALU.mult))
        A("vector", lambda e: e.tensor_scalar(tmpa[:], selm[:], 1e30, -1e30,
                                              op0=ALU.mult, op1=ALU.add))
        A("vector", lambda e: e.tensor_add(ml[:], ml[:], tmpa[:]))
        A("vector", lambda e: e.reduce_max(rmax[:], ml[:], axis=AX.X))
        A("vector", lambda e: e.tensor_scalar_mul(nrmax[:], rmax[:], -1.0))
        A("scalar", lambda e: e.activation(out=esf[:], in_=ml[:], func=AF.Exp,
                                           bias=nrmax[:], accum_out=essum[:]))
        A("vector", lambda e: e.reciprocal(rsum[:], essum[:]))
        A("vector", lambda e: e.tensor_scalar_mul(routing[:], esf[:], rsum[:]))
        A("vector", lambda e: e.tensor_tensor(routm[:], routing[:], selm[:], op=ALU.mult))
        A("tensor", lambda e: e.transpose(pb[2][0:32, 0:2], routm[:], idf_sb[0:2, 0:2]))
        A("vector", lambda e: e.tensor_copy(rt_bf[:], pb[2][0:32, 0:2]))
        A("tensor", lambda e: e.transpose(pb[2][0:32, 0:2], routing[:], idf_sb[0:2, 0:2]))
        A("vector", lambda e: e.tensor_copy(ro_bf[:], pb[2][0:32, 0:2]))
        for wsrc, rt, wdst in ((w_v, rt_bf, wv_dyn), (o_w, ro_bf, wo_dyn)):
            for ch in range(64):
                b = (ch % 2) * D
                P.dma("gpsimd", out=wvf[:, b:b + D],
                      in_=wsrc[:, ch * D:(ch + 1) * D])
                for j in range(4):
                    A("tensor", lambda e, rt=rt, b=b, j=j: e.matmul(
                        pb[j][0:2, :], rt[:], wvf[:, b + j * 512:b + (j + 1) * 512],
                        start=True, stop=True))
                for j in range(4):
                    A("vector", lambda e, b=b, j=j: e.tensor_copy(
                        dyn_sb[0:2, b + j * 512:b + (j + 1) * 512], pb[j][0:2, :]))
                P.dma("sync", out=wdst[:, ch * D:(ch + 1) * D], in_=dyn_sb[0:2, b:b + D])
        for i in range(128):
            t, ch = divmod(i, 16)
            A("tensor", lambda e, t=t, ch=ch, i=i: e.transpose(
                pbb[i % 2][0:128, 0:128],
                ln1_sb[:, t * D + ch * 128:t * D + (ch + 1) * 128], idb_sb[:]))
            A("vector", lambda e, t=t, ch=ch, i=i: e.tensor_copy(
                lnfm[:, (ch * 8 + t) * 128:(ch * 8 + t + 1) * 128], pbb[i % 2][0:128, 0:128]))
        A("tensor", lambda e: e.matmul(pb[2][0:128, 0:2], ones_r[:], wsel_sb[:],
                                       start=True, stop=True))
        A("vector", lambda e: e.tensor_copy(ws0[:], pb[2][0:128, 0:1]))
        A("vector", lambda e: e.tensor_copy(ws1[:], pb[2][0:128, 1:2]))
        for b in range(2):
            for ch in range(16):
                idx = b * 16 + ch
                A("sync", lambda e, b=b, ch=ch, idx=idx: e.dma_start_transpose(
                    wvT[:, idx * 64:(idx + 1) * 64],
                    wv3[b][:, ch * 128:(ch + 1) * 128]), inc=16)
        A("vector", lambda e: e.tensor_scalar_mul(wvtmp[:], wvT[:, 1024:2048], ws1[:]))
        A("vector", lambda e: e.scalar_tensor_tensor(
            wvsel[:], wvT[:, 0:1024], ws0[:], wvtmp[:], op0=ALU.mult, op1=ALU.add))
        P.dma("sync", out=wo_sb[0:64, 0:D], in_=wo3[0])
        P.dma("sync", out=wo_sb[0:64, D:2 * D], in_=wo3[1])
        A("vector", lambda e: e.tensor_scalar_mul(wo_tmp[:], wo_sb[0:64, D:2 * D], ws1[0:64, :]))
        A("vector", lambda e: e.scalar_tensor_tensor(
            wo_sel[:], wo_sb[0:64, 0:D], ws0[0:64, :], wo_tmp[:], op0=ALU.mult, op1=ALU.add))
        for t in range(NTILE):
            for ch in range(16):
                A("tensor", lambda e, t=t, ch=ch: e.matmul(
                    pb[2][0:128, 0:64], lnfm[:, (ch * 8 + t) * 128:(ch * 8 + t + 1) * 128],
                    wvsel[:, ch * 64:(ch + 1) * 64],
                    start=(ch == 0), stop=(ch == 15)))
            A("vector", lambda e, t=t: e.tensor_copy(
                cb_sb[:, t * 64:(t + 1) * 64], pb[2][0:128, 0:64]))
        for t in range(NTILE):
            A("tensor", lambda e, t=t: e.transpose(
                pbb[t % 2][0:64, 0:128], cb_sb[:, t * 64:(t + 1) * 64], idb_sb[:]))
            A("vector", lambda e, t=t: e.tensor_copy(
                cbT[:, t * 128:(t + 1) * 128], pbb[t % 2][0:64, 0:128]))
        for t in range(NTILE):
            P.dma("sync", out=x1t[:], in_=x13[t])
            for j in range(4):
                A("tensor", lambda e, t=t, j=j: e.matmul(
                    pb[3][0:128, :], cbT[:, t * 128:(t + 1) * 128],
                    wo_sel[:, j * 512:(j + 1) * 512], start=True, stop=True))
                A("vector", lambda e, j=j: e.scalar_tensor_tensor(
                    x2t[:, j * 512:(j + 1) * 512], pb[3][0:128, :], 1.0,
                    x1t[:, j * 512:(j + 1) * 512], op0=ALU.mult, op1=ALU.add))
            P.dma("sync", out=x23[t], in_=x2t[:])
        P.emit(block, nc, gs)
    return nc


def run_ffn0(x1, ln1, ss, d):
    nc = build_ffn0()
    bsel = np.zeros((8, 2), dtype=np.float32)
    bsel[0:4, 0] = 1.0
    bsel[4:8, 1] = 1.0
    cns = _attn_consts()
    in_maps = []
    for c in range(NCORES):
        wsel = np.zeros((1, 2), dtype=np.float32)
        wsel[0, c // 4] = 1.0
        in_maps.append({
            "x1": np.ascontiguousarray(x1[c * PTOK:(c + 1) * PTOK]),
            "ln1": np.ascontiguousarray(ln1[c * PTOK:(c + 1) * PTOK]),
            "ssall": ss.astype(np.float32),
            "sim": d["L0_ffn_sim_matrix"].astype(np.float32),
            "gates": d["L0_ffn_gates"].reshape(1, 32).astype(np.float32),
            "w_v": d["L0_ffn_w_v"].reshape(32, 64 * D).astype(np.float32),
            "o_w": d["L0_ffn_o_weights"].reshape(32, 64 * D).astype(np.float32),
            "bsel": bsel, "wsel": wsel,
            "ident_f32": cns["ident_f32"], "ident_bf": cns["ident_bf"],
            "ones_row_f32": cns["ones_row_f32"],
            "ones_col_f32": np.ones((128, 1), dtype=np.float32),
        })
    res = run_bass_kernel_spmd(nc, in_maps, core_ids=list(range(NCORES)))
    x2 = np.concatenate([res.results[c]["x2"] for c in range(NCORES)], 0)
    return x2



NU_F = 768


def build_l1k():
    nc = bass.Bass("TRN2", target_bir_lowering=False)
    x2 = nc.dram_tensor("x2", [PTOK, D], F32, kind="ExternalInput").ap()
    Wk = nc.dram_tensor("Wk", [512, D], F32, kind="ExternalInput").ap()
    bk = nc.dram_tensor("bk", [1, 512], F32, kind="ExternalInput").ap()
    Wv = nc.dram_tensor("Wv", [512, D], F32, kind="ExternalInput").ap()
    bv = nc.dram_tensor("bv", [1, 512], F32, kind="ExternalInput").ap()
    Wmb = nc.dram_tensor("Wmb", [D, D], F32, kind="ExternalInput").ap()
    bmb = nc.dram_tensor("bmb", [1, D], F32, kind="ExternalInput").ap()
    memory = nc.dram_tensor("memory", [16 * 768, 128], F32, kind="ExternalInput").ap()
    norm = nc.dram_tensor("norm", [16, 768], F32, kind="ExternalInput").ap()
    idb = nc.dram_tensor("ident_bf", [128, 128], BF16, kind="ExternalInput").ap()
    idf = nc.dram_tensor("ident_f32", [128, 128], F32, kind="ExternalInput").ap()
    onesrb = nc.dram_tensor("ones_row_bf", [1, 128], BF16, kind="ExternalInput").ap()
    ln2o = nc.dram_tensor("ln2", [PTOK, D], BF16, kind="ExternalOutput").ap()
    dmemo = nc.dram_tensor("dmem", [16 * 768, 128], F32, kind="ExternalOutput").ap()
    dnormo = nc.dram_tensor("dnorm", [16, 768], F32, kind="ExternalOutput").ap()

    x23 = x2.rearrange("(t p) d -> t p d", p=128)
    ln23 = ln2o.rearrange("(t p) d -> t p d", p=128)
    mem3 = memory.rearrange("(h c p) d -> h p c d", p=128, c=6)

    with contextlib.ExitStack() as es:
        sb = lambda n, s, d: es.enter_context(nc.sbuf_tensor(n, s, d))
        ps = lambda n, s, d: es.enter_context(nc.psum_tensor(n, s, d))
        sem = lambda n: es.enter_context(nc.semaphore(n))
        xt = sb("xt", [128, D], F32)
        xm = sb("xm", [128, D], F32)
        m_ = sb("m_", [128, 1], F32)
        v2 = sb("v2", [128, 1], F32)
        sd = sb("sd", [128, 1], F32)
        rstd = sb("rstd", [128, 1], F32)
        epsb = sb("epsb", [128, 1], F32)
        multiuse = sb("multiuse", [128, NTILE * D], BF16)
        ln2fm = sb("ln2fm", [128, NTILE * D], BF16)
        wrow = sb("wrow", [128, 2 * D], BF16)
        wT = sb("wT", [128, 16 * 512], BF16)
        bias_sb = sb("bias_sb", [1, D], BF16)
        ones_rb = sb("ones_rb", [1, 128], BF16)
        idb_sb = sb("idb_sb", [128, 128], BF16)
        idf_sb = sb("idf_sb", [128, 128], F32)
        ktile = sb("ktile", [128, 512], F32)
        a_sb = sb("a_sb", [128, 256], F32)
        mkraw = sb("mkraw", [128, NU_F], F32)
        ss2 = sb("ss2", [128, 1], F32)
        nrm = sb("nrm", [128, 1], F32)
        nmx = sb("nmx", [128, 1], F32)
        rmu = sb("rmu", [128, 1], F32)
        v_sb = sb("v_sb", [128, NTILE * 512], BF16)
        mk_tm = [sb(f"mk{g}", [128, NTILE * NU_F], BF16) for g in range(4)]
        mkfm = sb("mkfm", [128, 6 * PTOK], BF16)
        normsb = sb("normsb", [16, 768], F32)
        normT = sb("normT", [128, 96], BF16)
        mem_sb = sb("mem_sb", [128, 6 * 128], BF16)
        den_sb = sb("den_sb", [1, PTOK], F32)
        den_cols = sb("den_cols", [128, NTILE], F32)
        rden_cols = sb("rden_cols", [128, NTILE], F32)
        coef_cols = sb("coef_cols", [128, NTILE], F32)
        coefb = sb("coefb", [128, NTILE], BF16)
        numsb = sb("numsb", [128, PTOK], F32)
        wmv_sb = sb("wmv_sb", [128, NTILE * 128], BF16)
        dm_sb = sb("dm_sb", [128, 128], F32)
        dn_sb = sb("dn_sb", [1, 768], F32)
        pbf = [ps(f"pbf{i}", [128, 512], F32) for i in range(4)]
        pbb = [ps(f"pbb{i}", [128, 1024], BF16) for i in range(2)]
        gs = sem("gs")
        block = es.enter_context(nc.Block())

        P = Plan()
        A = P.add
        P.dma("sync", out=idb_sb[:], in_=idb)
        P.dma("sync", out=idf_sb[:], in_=idf)
        P.dma("sync", out=ones_rb[:], in_=onesrb)
        P.dma("sync", out=normsb[:], in_=norm)
        A("vector", lambda e: e.memset(epsb[:], EPS_LN))
        for i in range(NTILE):
            P.dma("sync", out=xt[:], in_=x23[i])
            A("vector", lambda e: e.reduce_sum(m_[:], xt[:], axis=AX.X))
            A("vector", lambda e: e.tensor_scalar_mul(m_[:], m_[:], 1.0 / D))
            A("vector", lambda e: e.tensor_scalar_sub(xm[:], xt[:], m_[:]))
            A("scalar", lambda e: e.activation(out=ln2fm[:, 0:D], in_=xm[:],
                                               func=AF.Square, accum_out=v2[:]))
            A("scalar", lambda e: e.activation(out=sd[:], in_=v2[:], func=AF.Sqrt,
                                               scale=1.0 / D, bias=epsb[:, 0:1]))
            A("vector", lambda e: e.reciprocal(rstd[:], sd[:]))
            A("scalar", lambda e, i=i: e.activation(out=multiuse[:, i * D:(i + 1) * D],
                                                    in_=xm[:], func=AF.Identity,
                                                    scale=rstd[:], bias=0.0))
            P.dma("sync", out=ln23[i], in_=multiuse[:, i * D:(i + 1) * D])
        for i in range(128):
            t, ch = divmod(i, 16)
            A("tensor", lambda e, t=t, ch=ch, i=i: e.transpose(
                pbb[i % 2][0:128, 0:128],
                multiuse[:, t * D + ch * 128:t * D + (ch + 1) * 128], idb_sb[:]))
            A("vector", lambda e, t=t, ch=ch, i=i: e.tensor_copy(
                ln2fm[:, (ch * 8 + t) * 128:(ch * 8 + t + 1) * 128],
                pbb[i % 2][0:128, 0:128]))

        def load_wt(Wsrc, j0, npt):
            for jt in range(npt):
                slot = (jt % 2) * D
                P.dma("gpsimd", out=wrow[:, slot:slot + D],
                      in_=Wsrc.rearrange("(t p) d -> t p d", p=128)[j0 + jt])
                for ch in range(16):
                    i = ch
                    A("tensor", lambda e, ch=ch, slot=slot, i=i: e.transpose(
                        pbb[i % 2][0:128, 0:128],
                        wrow[:, slot + ch * 128:slot + (ch + 1) * 128], idb_sb[:]))
                    A("vector", lambda e, ch=ch, jt=jt, i=i: e.tensor_copy(
                        wT[:, ch * 512 + jt * 128:ch * 512 + (jt + 1) * 128],
                        pbb[i % 2][0:128, 0:128]))

        def proj_tile(t, bias_off, jw=512):
            A("tensor", lambda e: e.matmul(pbf[0][0:128, 0:jw], ones_rb[:],
                                           bias_sb[0:1, bias_off:bias_off + jw],
                                           start=True, stop=False))
            for ch in range(16):
                A("tensor", lambda e, ch=ch, t=t: e.matmul(
                    pbf[0][0:128, 0:jw],
                    ln2fm[:, (ch * 8 + t) * 128:(ch * 8 + t + 1) * 128],
                    wT[:, ch * 512:ch * 512 + jw],
                    start=False, stop=(ch == 15)))

        def dpfp_mk(t, g):
            A("scalar", lambda e, g=g: e.activation(out=a_sb[:, 0:128],
                                                    in_=ktile[:, g * 128:(g + 1) * 128],
                                                    func=AF.Relu))
            A("scalar", lambda e, g=g: e.activation(out=a_sb[:, 128:256],
                                                    in_=ktile[:, g * 128:(g + 1) * 128],
                                                    func=AF.Relu, scale=-1.0))
            for u in range(3):
                j = u + 1
                A("vector", lambda e, u=u, j=j: e.tensor_tensor(
                    mkraw[:, u * 256:u * 256 + j], a_sb[:, 0:j],
                    a_sb[:, 256 - j:256], op=ALU.mult))
                A("vector", lambda e, u=u, j=j: e.tensor_tensor(
                    mkraw[:, u * 256 + j:(u + 1) * 256], a_sb[:, j:256],
                    a_sb[:, 0:256 - j], op=ALU.mult))
            A("scalar", lambda e: e.activation(out=numsb[:, 0:NU_F], in_=mkraw[:],
                                               func=AF.Square, accum_out=ss2[:]))
            A("scalar", lambda e: e.activation(out=nrm[:], in_=ss2[:], func=AF.Sqrt))
            A("vector", lambda e: e.tensor_scalar_max(nmx[:], nrm[:], 1e-12))
            A("vector", lambda e: e.reciprocal(rmu[:], nmx[:]))
            A("vector", lambda e, t=t, g=g: e.tensor_scalar_mul(
                mk_tm[g][:, t * NU_F:(t + 1) * NU_F], mkraw[:], rmu[:]))

        P.dma("gpsimd", out=bias_sb[0:1, 0:512], in_=bk)
        load_wt(Wk, 0, 4)
        for t in range(NTILE):
            proj_tile(t, 0)
            A("vector", lambda e: e.tensor_copy(ktile[:], pbf[0][0:128, 0:512]))
            for g in range(4):
                dpfp_mk(t, g)
        P.dma("gpsimd", out=bias_sb[0:1, 0:512], in_=bv)
        load_wt(Wv, 0, 4)
        for t in range(NTILE):
            proj_tile(t, 0)
            A("vector", lambda e, t=t: e.tensor_copy(v_sb[:, t * 512:(t + 1) * 512],
                                                     pbf[0][0:128, 0:512]))
        P.dma("gpsimd", out=bias_sb[:], in_=bmb)
        for jg in range(4):
            load_wt(Wmb, 4 * jg, 4)
            for t in range(NTILE):
                proj_tile(t, jg * 512)
                A("scalar", lambda e, t=t, jg=jg: e.activation(
                    out=multiuse[:, t * D + jg * 512:t * D + (jg + 1) * 512],
                    in_=pbf[0][0:128, 0:512], func=AF.Sigmoid))
        for fc in range(6):
            A("tensor", lambda e, fc=fc: e.transpose(
                pbf[1][0:128, 0:16], normsb[:, fc * 128:(fc + 1) * 128],
                idf_sb[0:16, 0:16]))
            A("vector", lambda e, fc=fc: e.tensor_copy(
                normT[:, fc * 16:(fc + 1) * 16], pbf[1][0:128, 0:16]))
        for g in range(4):
            for i in range(48):
                fc, t = divmod(i, 8)
                A("tensor", lambda e, fc=fc, t=t, g=g, i=i: e.transpose(
                    pbb[i % 2][0:128, 0:128],
                    mk_tm[g][:, t * NU_F + fc * 128:t * NU_F + (fc + 1) * 128],
                    idb_sb[:]))
                A("vector", lambda e, fc=fc, t=t, i=i: e.tensor_copy(
                    mkfm[:, fc * PTOK + t * 128:fc * PTOK + (t + 1) * 128],
                    pbb[i % 2][0:128, 0:128]))
            for h in range(4 * g, 4 * g + 4):
                for half in range(2):
                    for fc in range(6):
                        A("tensor", lambda e, h=h, fc=fc, half=half: e.matmul(
                            pbf[half][0:1, 0:512],
                            normT[:, fc * 16 + h:fc * 16 + h + 1],
                            mkfm[:, fc * PTOK + half * 512:fc * PTOK + (half + 1) * 512],
                            start=(fc == 0), stop=(fc == 5)))
                for half in range(2):
                    A("vector", lambda e, half=half: e.tensor_scalar_add(
                        den_sb[0:1, half * 512:(half + 1) * 512],
                        pbf[half][0:1, 0:512], 1e-8))
                for t in range(NTILE):
                    A("tensor", lambda e, t=t: e.transpose(
                        pbf[2][0:128, 0:1], den_sb[0:1, t * 128:(t + 1) * 128],
                        idf_sb[0:1, 0:1]))
                    A("vector", lambda e, t=t: e.tensor_copy(
                        den_cols[:, t:t + 1], pbf[2][0:128, 0:1]))
                A("vector", lambda e: e.reciprocal(rden_cols[:], den_cols[:]))
                A("vector", lambda e: e.tensor_scalar(coef_cols[:], den_cols[:], -1.0, 1.0,
                                                      op0=ALU.mult, op1=ALU.add))
                A("vector", lambda e: e.tensor_scalar(coef_cols[:], coef_cols[:], 0.0, 1.0,
                                                      op0=ALU.max, op1=ALU.min))
                A("vector", lambda e: e.tensor_copy(coefb[:], coef_cols[:]))
                P.dma("gpsimd", out=mem_sb[:].rearrange("p (c d) -> p c d", d=128),
                      in_=mem3[h])
                for half in range(2):
                    for fc in range(6):
                        A("tensor", lambda e, fc=fc, half=half: e.matmul(
                            pbf[half][0:128, 0:512],
                            mem_sb[:, fc * 128:(fc + 1) * 128],
                            mkfm[:, fc * PTOK + half * 512:fc * PTOK + (half + 1) * 512],
                            start=(fc == 0), stop=(fc == 5)))
                for half in range(2):
                    A("vector", lambda e, half=half: e.tensor_copy(
                        numsb[:, half * 512:(half + 1) * 512], pbf[half][0:128, 0:512]))
                for t in range(NTILE):
                    A("tensor", lambda e, t=t: e.transpose(
                        pbf[2][0:128, 0:128], numsb[:, t * 128:(t + 1) * 128],
                        idf_sb[:]))
                    A("vector", lambda e, t=t: e.tensor_scalar_mul(
                        dm_sb[:], pbf[2][0:128, 0:128], rden_cols[:, t:t + 1]))
                    A("vector", lambda e, t=t, g=g: e.tensor_sub(
                        dm_sb[:], v_sb[:, t * 512 + g * 128:t * 512 + (g + 1) * 128],
                        dm_sb[:]))
                    A("vector", lambda e, t=t, h=h: e.tensor_tensor(
                        wmv_sb[:, t * 128:(t + 1) * 128], dm_sb[:],
                        multiuse[:, t * D + h * 128:t * D + (h + 1) * 128], op=ALU.mult))
                for fc in range(6):
                    for t in range(NTILE):
                        A("tensor", lambda e, fc=fc, t=t, g=g: e.matmul(
                            pbf[2][0:128, 0:128],
                            mk_tm[g][:, t * NU_F + fc * 128:t * NU_F + (fc + 1) * 128],
                            wmv_sb[:, t * 128:(t + 1) * 128],
                            start=(t == 0), stop=(t == NTILE - 1)))
                    A("vector", lambda e: e.tensor_copy(dm_sb[:], pbf[2][0:128, 0:128]))
                    P.dma("sync", out=dmemo[h * 768 + fc * 128:h * 768 + (fc + 1) * 128, :],
                          in_=dm_sb[:])
                for t in range(NTILE):
                    A("tensor", lambda e, t=t, g=g: e.matmul(
                        pbf[3][0:1, 0:512], coefb[:, t:t + 1],
                        mk_tm[g][:, t * NU_F:t * NU_F + 512],
                        start=(t == 0), stop=(t == NTILE - 1)))
                    A("tensor", lambda e, t=t, g=g: e.matmul(
                        pbf[2][0:1, 0:256], coefb[:, t:t + 1],
                        mk_tm[g][:, t * NU_F + 512:t * NU_F + NU_F],
                        start=(t == 0), stop=(t == NTILE - 1)))
                A("vector", lambda e: e.tensor_copy(dn_sb[0:1, 0:512], pbf[3][0:1, 0:512]))
                A("vector", lambda e: e.tensor_copy(dn_sb[0:1, 512:768],
                                                    pbf[2][0:1, 0:256]))
                P.dma("sync", out=dnormo[h:h + 1, :], in_=dn_sb[:])
        P.emit(block, nc, gs)
    return nc


def run_l1k(x2, d):
    nc = build_l1k()
    cns = _attn_consts()
    in_maps = []
    for c in range(NCORES):
        in_maps.append({
            "x2": np.ascontiguousarray(x2[c * PTOK:(c + 1) * PTOK]),
            "Wk": d["L1_attn_Wk"].astype(np.float32),
            "bk": d["L1_attn_bk"].reshape(1, 512).astype(np.float32),
            "Wv": d["L1_attn_Wv"].astype(np.float32),
            "bv": d["L1_attn_bv"].reshape(1, 512).astype(np.float32),
            "Wmb": d["L1_attn_Wmb"].astype(np.float32),
            "bmb": d["L1_attn_bmb"].reshape(1, D).astype(np.float32),
            "memory": d["L1_attn_memory"].reshape(16 * 768, 128).astype(np.float32),
            "norm": d["L1_attn_norm"].astype(np.float32),
            "ident_bf": cns["ident_bf"], "ident_f32": cns["ident_f32"],
            "ones_row_bf": np.ones((1, 128), dtype=BF),
        })
    res = run_bass_kernel_spmd(nc, in_maps, core_ids=list(range(NCORES)))
    ln2 = np.concatenate([res.results[c]["ln2"] for c in range(NCORES)], 0)
    dmem = np.stack([res.results[c]["dmem"] for c in range(NCORES)], 0)
    dnorm = np.stack([res.results[c]["dnorm"] for c in range(NCORES)], 0)
    return ln2, dmem, dnorm



def build_l1q():
    nc = bass.Bass("TRN2", target_bir_lowering=False)
    x2 = nc.dram_tensor("x2", [PTOK, D], F32, kind="ExternalInput").ap()
    ln2 = nc.dram_tensor("ln2", [PTOK, D], BF16, kind="ExternalInput").ap()
    dmem = nc.dram_tensor("dmem", [8, 16 * 768, 128], F32, kind="ExternalInput").ap()
    dnorm = nc.dram_tensor("dnorm", [8, 16, 768], F32, kind="ExternalInput").ap()
    memory = nc.dram_tensor("memory", [16 * 768, 128], F32, kind="ExternalInput").ap()
    norm = nc.dram_tensor("norm", [16, 768], F32, kind="ExternalInput").ap()
    Wq = nc.dram_tensor("Wq", [D, D], F32, kind="ExternalInput").ap()
    bq = nc.dram_tensor("bq", [1, D], F32, kind="ExternalInput").ap()
    Wg = nc.dram_tensor("Wg", [D, D], F32, kind="ExternalInput").ap()
    bg = nc.dram_tensor("bg", [1, D], F32, kind="ExternalInput").ap()
    idb = nc.dram_tensor("ident_bf", [128, 128], BF16, kind="ExternalInput").ap()
    idf = nc.dram_tensor("ident_f32", [128, 128], F32, kind="ExternalInput").ap()
    onesrb = nc.dram_tensor("ones_row_bf", [1, 128], BF16, kind="ExternalInput").ap()
    x3o = nc.dram_tensor("x3", [PTOK, D], F32, kind="ExternalOutput").ap()

    x23 = x2.rearrange("(t p) d -> t p d", p=128)
    ln23 = ln2.rearrange("(t p) d -> t p d", p=128)
    x33 = x3o.rearrange("(t p) d -> t p d", p=128)
    dm4 = dmem.rearrange("r (h c p) d -> r h p c d", p=128, c=6)
    mem3 = memory.rearrange("(h c p) d -> h p c d", p=128, c=6)

    with contextlib.ExitStack() as es:
        sb = lambda n, s, d: es.enter_context(nc.sbuf_tensor(n, s, d))
        ps = lambda n, s, d: es.enter_context(nc.psum_tensor(n, s, d))
        sem = lambda n: es.enter_context(nc.semaphore(n))
        ln2sb = sb("ln2sb", [128, NTILE * D], BF16)
        ln2fm = sb("ln2fm", [128, NTILE * D], BF16)
        q_sb = sb("q_sb", [128, NTILE * D], BF16)
        gate_sb = sb("gate_sb", [128, NTILE * D], BF16)
        newmem = sb("newmem", [128, 16 * 6 * 128], BF16)
        dacc = sb("dacc", [128, 768], F32)
        nrm_sb = sb("nrm_sb", [16, 768], F32)
        nrm_acc = sb("nrm_acc", [16, 768], F32)
        newnormT = sb("newnormT", [128, 96], BF16)
        wrow = sb("wrow", [128, 2 * D], BF16)
        wT = sb("wT", [128, 16 * 512], BF16)
        bias_sb = sb("bias_sb", [1, D], BF16)
        ones_rb = sb("ones_rb", [1, 128], BF16)
        idb_sb = sb("idb_sb", [128, 128], BF16)
        idf_sb = sb("idf_sb", [128, 128], F32)
        a_sb = sb("a_sb", [128, 256], F32)
        mkraw = sb("mkraw", [128, NU_F], F32)
        mkb = sb("mkb", [128, NU_F], BF16)
        mqfm = sb("mqfm", [128, NU_F], BF16)
        ss2 = sb("ss2", [128, 1], F32)
        nrm1 = sb("nrm1", [128, 1], F32)
        nmx = sb("nmx", [128, 1], F32)
        rmu = sb("rmu", [128, 1], F32)
        dena = sb("dena", [1, 128], F32)
        dena_col = sb("dena_col", [128, 1], F32)
        rden = sb("rden", [128, 1], F32)
        numsb = sb("numsb", [128, 128], F32)
        mh_acc = sb("mh_acc", [128, D], F32)
        scr = ln2fm[:, 0:2 * 768].bitcast(F32)
        x2t = ln2fm[:, 2 * D:4 * D].bitcast(F32)
        pbf = [ps(f"pbf{i}", [128, 512], F32) for i in range(4)]
        pbb = [ps(f"pbb{i}", [128, 1024], BF16) for i in range(2)]
        gs = sem("gs")
        block = es.enter_context(nc.Block())

        P = Plan()
        A = P.add
        P.dma("sync", out=idb_sb[:], in_=idb)
        P.dma("sync", out=idf_sb[:], in_=idf)
        P.dma("sync", out=ones_rb[:], in_=onesrb)
        P.dma("sync", out=ln2sb[:].rearrange("p (t d) -> p t d", d=D),
              in_=ln2.rearrange("(t p) d -> p t d", p=128))
        for h in range(16):
            P.dma("sync", out=dacc[:].rearrange("p (c d) -> p c d", d=128), in_=dm4[0][h])
            for r in range(1, 8):
                P.dma("sync", out=scr[:, 0:768].rearrange("p (c d) -> p c d", d=128),
                      in_=dm4[r][h])
                A("vector", lambda e: e.tensor_add(dacc[:], dacc[:], scr[:, 0:768]))
            P.dma("gpsimd", out=scr[:, 0:768].rearrange("p (c d) -> p c d", d=128),
                  in_=mem3[h])
            A("vector", lambda e: e.tensor_add(dacc[:], dacc[:], scr[:, 0:768]))
            A("vector", lambda e, h=h: e.tensor_copy(
                newmem[:, h * 768:(h + 1) * 768], dacc[:]))
        P.dma("sync", out=nrm_acc[:], in_=dnorm[0])
        for r in range(1, 8):
            P.dma("sync", out=nrm_sb[:], in_=dnorm[r])
            A("vector", lambda e: e.tensor_add(nrm_acc[:], nrm_acc[:], nrm_sb[:]))
        P.dma("sync", out=nrm_sb[:], in_=norm)
        A("vector", lambda e: e.tensor_add(nrm_acc[:], nrm_acc[:], nrm_sb[:]))
        for fc in range(6):
            A("tensor", lambda e, fc=fc: e.transpose(
                pbf[1][0:128, 0:16], nrm_acc[:, fc * 128:(fc + 1) * 128],
                idf_sb[0:16, 0:16]))
            A("vector", lambda e, fc=fc: e.tensor_copy(
                newnormT[:, fc * 16:(fc + 1) * 16], pbf[1][0:128, 0:16]))
        for i in range(128):
            t, ch = divmod(i, 16)
            A("tensor", lambda e, t=t, ch=ch, i=i: e.transpose(
                pbb[i % 2][0:128, 0:128],
                ln2sb[:, t * D + ch * 128:t * D + (ch + 1) * 128], idb_sb[:]))
            A("vector", lambda e, t=t, ch=ch, i=i: e.tensor_copy(
                ln2fm[:, (ch * 8 + t) * 128:(ch * 8 + t + 1) * 128],
                pbb[i % 2][0:128, 0:128]))

        def load_wt(Wsrc, j0, npt):
            for jt in range(npt):
                slot = (jt % 2) * D
                P.dma("gpsimd", out=wrow[:, slot:slot + D],
                      in_=Wsrc.rearrange("(t p) d -> t p d", p=128)[j0 + jt])
                for ch in range(16):
                    i = ch
                    A("tensor", lambda e, ch=ch, slot=slot, i=i: e.transpose(
                        pbb[i % 2][0:128, 0:128],
                        wrow[:, slot + ch * 128:slot + (ch + 1) * 128], idb_sb[:]))
                    A("vector", lambda e, ch=ch, jt=jt, i=i: e.tensor_copy(
                        wT[:, ch * 512 + jt * 128:ch * 512 + (jt + 1) * 128],
                        pbb[i % 2][0:128, 0:128]))

        def proj_tile(t, bias_off, jw=512):
            A("tensor", lambda e, bias_off=bias_off, jw=jw: e.matmul(
                pbf[0][0:128, 0:jw], ones_rb[:],
                bias_sb[0:1, bias_off:bias_off + jw], start=True, stop=False))
            for ch in range(16):
                A("tensor", lambda e, ch=ch, t=t, jw=jw: e.matmul(
                    pbf[0][0:128, 0:jw],
                    ln2fm[:, (ch * 8 + t) * 128:(ch * 8 + t + 1) * 128],
                    wT[:, ch * 512:ch * 512 + jw],
                    start=False, stop=(ch == 15)))

        P.dma("gpsimd", out=bias_sb[:], in_=bq)
        for jg in range(4):
            load_wt(Wq, 4 * jg, 4)
            for t in range(NTILE):
                proj_tile(t, jg * 512)
                A("vector", lambda e, t=t, jg=jg: e.tensor_copy(
                    q_sb[:, t * D + jg * 512:t * D + (jg + 1) * 512],
                    pbf[0][0:128, 0:512]))
        P.dma("gpsimd", out=bias_sb[:], in_=bg)
        for jg in range(4):
            load_wt(Wg, 4 * jg, 4)
            for t in range(NTILE):
                proj_tile(t, jg * 512)
                A("scalar", lambda e, t=t, jg=jg: e.activation(
                    out=gate_sb[:, t * D + jg * 512:t * D + (jg + 1) * 512],
                    in_=pbf[0][0:128, 0:512], func=AF.Sigmoid))
        for t in range(NTILE):
            for h in range(16):
                A("scalar", lambda e, t=t, h=h: e.activation(
                    out=a_sb[:, 0:128], in_=q_sb[:, t * D + h * 128:t * D + (h + 1) * 128],
                    func=AF.Relu))
                A("scalar", lambda e, t=t, h=h: e.activation(
                    out=a_sb[:, 128:256], in_=q_sb[:, t * D + h * 128:t * D + (h + 1) * 128],
                    func=AF.Relu, scale=-1.0))
                for u in range(3):
                    j = u + 1
                    A("vector", lambda e, u=u, j=j: e.tensor_tensor(
                        mkraw[:, u * 256:u * 256 + j], a_sb[:, 0:j],
                        a_sb[:, 256 - j:256], op=ALU.mult))
                    A("vector", lambda e, u=u, j=j: e.tensor_tensor(
                        mkraw[:, u * 256 + j:(u + 1) * 256], a_sb[:, j:256],
                        a_sb[:, 0:256 - j], op=ALU.mult))
                A("scalar", lambda e: e.activation(out=scr[:], in_=mkraw[:],
                                                   func=AF.Square, accum_out=ss2[:]))
                A("scalar", lambda e: e.activation(out=nrm1[:], in_=ss2[:], func=AF.Sqrt))
                A("vector", lambda e: e.tensor_scalar_max(nmx[:], nrm1[:], 1e-12))
                A("vector", lambda e: e.reciprocal(rmu[:], nmx[:]))
                A("vector", lambda e: e.tensor_scalar_mul(mkb[:], mkraw[:], rmu[:]))
                for fc in range(6):
                    A("tensor", lambda e, fc=fc: e.transpose(
                        pbb[fc % 2][0:128, 0:128], mkb[:, fc * 128:(fc + 1) * 128],
                        idb_sb[:]))
                    A("vector", lambda e, fc=fc: e.tensor_copy(
                        mqfm[:, fc * 128:(fc + 1) * 128], pbb[fc % 2][0:128, 0:128]))
                for fc in range(6):
                    A("tensor", lambda e, fc=fc, h=h: e.matmul(
                        pbf[0][0:128, 0:128], newmem[:, (h * 6 + fc) * 128:(h * 6 + fc + 1) * 128],
                        mqfm[:, fc * 128:(fc + 1) * 128],
                        start=(fc == 0), stop=(fc == 5)))
                for fc in range(6):
                    A("tensor", lambda e, fc=fc, h=h: e.matmul(
                        pbf[1][0:1, 0:128], newnormT[:, fc * 16 + h:fc * 16 + h + 1],
                        mqfm[:, fc * 128:(fc + 1) * 128],
                        start=(fc == 0), stop=(fc == 5)))
                A("vector", lambda e: e.tensor_scalar_add(dena[:], pbf[1][0:1, 0:128], 1e-8))
                A("tensor", lambda e: e.transpose(pbf[2][0:128, 0:1], dena[:],
                                                  idf_sb[0:1, 0:1]))
                A("vector", lambda e: e.tensor_copy(dena_col[:], pbf[2][0:128, 0:1]))
                A("vector", lambda e: e.reciprocal(rden[:], dena_col[:]))
                A("vector", lambda e: e.tensor_copy(numsb[:], pbf[0][0:128, 0:128]))
                A("tensor", lambda e: e.transpose(pbf[3][0:128, 0:128], numsb[:],
                                                  idf_sb[:]))
                A("vector", lambda e, t=t, h=h: e.tensor_scalar_mul(
                    mh_acc[:, h * 128:(h + 1) * 128], pbf[3][0:128, 0:128], rden[:]))
            P.dma("sync", out=x2t[:], in_=x23[t])
            A("vector", lambda e, t=t: e.tensor_tensor(
                mh_acc[:], gate_sb[:, t * D:(t + 1) * D], mh_acc[:], op=ALU.mult))
            A("vector", lambda e: e.tensor_add(x2t[:], x2t[:], mh_acc[:]))
            A("vector", lambda e, t=t: e.tensor_add(x2t[:], x2t[:],
                                                    ln2sb[:, t * D:(t + 1) * D]))
            P.dma("sync", out=x33[t], in_=x2t[:])
        P.emit(block, nc, gs)
    return nc


def run_l1q(x2, ln2, dmem, dnorm, d):
    nc = build_l1q()
    cns = _attn_consts()
    in_maps = []
    for c in range(NCORES):
        in_maps.append({
            "x2": np.ascontiguousarray(x2[c * PTOK:(c + 1) * PTOK]),
            "ln2": np.ascontiguousarray(ln2[c * PTOK:(c + 1) * PTOK]),
            "dmem": dmem.astype(np.float32),
            "dnorm": dnorm.astype(np.float32),
            "memory": d["L1_attn_memory"].reshape(16 * 768, 128).astype(np.float32),
            "norm": d["L1_attn_norm"].astype(np.float32),
            "Wq": d["L1_attn_Wq"].astype(np.float32),
            "bq": d["L1_attn_bq"].reshape(1, D).astype(np.float32),
            "Wg": d["L1_attn_Wg"].astype(np.float32),
            "bg": d["L1_attn_bg"].reshape(1, D).astype(np.float32),
            "ident_bf": cns["ident_bf"], "ident_f32": cns["ident_f32"],
            "ones_row_bf": np.ones((1, 128), dtype=BF),
        })
    res = run_bass_kernel_spmd(nc, in_maps, core_ids=list(range(NCORES)))
    x3 = np.concatenate([res.results[c]["x3"] for c in range(NCORES)], 0)
    return x3



def build_moe(NTG=8):
    TOK = NTG * 512
    nc = bass.Bass("TRN2", target_bir_lowering=False)
    x3 = nc.dram_tensor("x3", [TOK, D], F32, kind="ExternalInput").ap()
    rw = nc.dram_tensor("rw", [16, D], F32, kind="ExternalInput").ap()
    ebias = nc.dram_tensor("ebias", [1, 16], F32, kind="ExternalInput").ap()
    gp2 = nc.dram_tensor("gp2", [2, D, 1792], F32, kind="ExternalInput").ap()
    up2 = nc.dram_tensor("up2", [2, D, 1792], F32, kind="ExternalInput").ap()
    dp2 = nc.dram_tensor("dp2", [2, 1792, D], F32, kind="ExternalInput").ap()
    esel = nc.dram_tensor("esel", [16, 256], F32, kind="ExternalInput").ap()
    idb = nc.dram_tensor("ident_bf", [128, 128], BF16, kind="ExternalInput").ap()
    idf = nc.dram_tensor("ident_f32", [128, 128], F32, kind="ExternalInput").ap()
    onesrb = nc.dram_tensor("ones_row_bf", [1, 128], BF16, kind="ExternalInput").ap()
    pout = nc.dram_tensor("pout", [TOK, D], F32, kind="ExternalOutput").ap()

    ln3d = nc.dram_tensor("ln3d", [TOK, D], BF16)
    x33 = x3.rearrange("(t p) d -> t p d", p=128)
    ln33 = ln3d.rearrange("(t p) d -> t p d", p=128)

    with contextlib.ExitStack() as es:
        sb = lambda n, s, d: es.enter_context(nc.sbuf_tensor(n, s, d))
        ps = lambda n, s, d: es.enter_context(nc.psum_tensor(n, s, d))
        sem = lambda n: es.enter_context(nc.semaphore(n))
        xt = sb("xt", [128, D], F32)
        xm = sb("xm", [128, D], F32)
        lnt = sb("lnt", [128, D], BF16)
        m_ = sb("m_", [128, 1], F32)
        v2 = sb("v2", [128, 1], F32)
        sd = sb("sd", [128, 1], F32)
        rstd = sb("rstd", [128, 1], F32)
        epsb = sb("epsb", [128, 1], F32)
        rwsb = sb("rwsb", [16, D], F32)
        routerT = sb("routerT", [128, 256], BF16)
        ebias_sb = sb("ebias_sb", [1, 16], F32)
        esel_sb = sb("esel_sb", [16, 256], F32)
        ones_rb = sb("ones_rb", [1, 128], BF16)
        ones_rf512 = sb("ones_rf512", [1, 512], F32)
        idb_sb = sb("idb_sb", [128, 128], BF16)
        idf_sb = sb("idf_sb", [128, 128], F32)
        x4fm = sb("x4fm", [128, 16 * 512], BF16)
        lg_sb = sb("lg_sb", [16, 512], F32)
        lgt = sb("lgt", [128, 16], F32)
        esc = sb("esc", [128, 16], F32)
        rmax = sb("rmax", [128, 1], F32)
        nrmax = sb("nrmax", [128, 1], F32)
        top8 = sb("top8", [128, 8], F32)
        m4 = sb("m4", [128, 1], F32)
        keep = sb("keep", [128, 16], F32)
        wsum = sb("wsum", [128, 1], F32)
        rsum = sb("rsum", [128, 1], F32)
        w_tm = sb("w_tm", [128, 64], F32)
        w_rows = sb("w_rows", [16, 512], F32)
        wb = sb("wb", [128, 2 * 512], F32)
        gw = sb("gw", [128, 16 * 128], BF16)
        uw = sb("uw", [128, 16 * 128], BF16)
        sg = sb("sg", [128, 512], BF16)
        abuf = sb("abuf", [128, 512], F32)
        act = sb("act", [128, 2 * 14 * 512], BF16)
        dwc = sb("dwc", [128, 14 * 128], BF16)
        po_sb = sb("po_sb", [128, 512], F32)
        po_t = sb("po_t", [128, 512], F32)
        pA = ps("pA", [128, 512], F32)
        pB = ps("pB", [128, 512], F32)
        pC = ps("pC", [128, 512], F32)
        pD = ps("pD", [128, 512], F32)
        gs = sem("gs")
        block = es.enter_context(nc.Block())

        P = Plan()
        A = P.add
        CH = P.ch
        P.dma("sync", out=idb_sb[:], in_=idb)
        P.dma("sync", out=idf_sb[:], in_=idf)
        P.dma("sync", out=ones_rb[:], in_=onesrb)
        A("vector", lambda e: e.memset(epsb[:], EPS_LN))
        A("vector", lambda e: e.memset(ones_rf512[:], 1.0))
        P.dma("sync", out=rwsb[:], in_=rw)
        P.dma("sync", out=ebias_sb[:], in_=ebias)
        P.dma("sync", out=esel_sb[:], in_=esel)
        for ch in range(16):
            A("tensor", lambda e, ch=ch: e.transpose(
                pA[0:128, 0:16], rwsb[:, ch * 128:(ch + 1) * 128], idf_sb[0:16, 0:16]))
            A("vector", lambda e, ch=ch: e.tensor_copy(
                routerT[:, ch * 16:(ch + 1) * 16], pA[0:128, 0:16]))
        for i in range(TOK // 128):
            P.dma("sync", out=xt[:], in_=x33[i])
            A("vector", lambda e: e.reduce_sum(m_[:], xt[:], axis=AX.X))
            A("vector", lambda e: e.tensor_scalar_mul(m_[:], m_[:], 1.0 / D))
            A("vector", lambda e: e.tensor_scalar_sub(xm[:], xt[:], m_[:]))
            A("scalar", lambda e: e.activation(out=lnt[:], in_=xm[:], func=AF.Square,
                                               accum_out=v2[:]))
            A("scalar", lambda e: e.activation(out=sd[:], in_=v2[:], func=AF.Sqrt,
                                               scale=1.0 / D, bias=epsb[:, 0:1]))
            A("vector", lambda e: e.reciprocal(rstd[:], sd[:]))
            A("scalar", lambda e: e.activation(out=lnt[:], in_=xm[:], func=AF.Identity,
                                               scale=rstd[:], bias=0.0))
            P.dma("sync", out=ln33[i], in_=lnt[:])
        for tg in range(NTG):
            for ch in range(16):
                P.add("sync", (lambda e, tg=tg, ch=ch: e.dma_start_transpose(
                    x4fm[:, ch * 512:(ch + 1) * 512],
                    ln3d[tg * 512:(tg + 1) * 512, ch * 128:(ch + 1) * 128])), 16,
                    chain=(ch > 0))
            A("tensor", lambda e: e.matmul(pA[0:16, 0:512], ebias_sb[:], ones_rf512[:],
                                           start=True, stop=False))
            for ch in range(16):
                P.add("tensor", (lambda e, ch=ch: e.matmul(
                    pA[0:16, 0:512], routerT[:, ch * 16:(ch + 1) * 16],
                    x4fm[:, ch * 512:(ch + 1) * 512],
                    start=False, stop=(ch == 15))), 1 if ch == 15 else 0, chain=(ch < 15))
            A("vector", lambda e: e.tensor_copy(lg_sb[:], pA[0:16, 0:512]))
            for j in range(4):
                A("tensor", lambda e, j=j: e.transpose(
                    pB[0:128, 0:16], lg_sb[:, j * 128:(j + 1) * 128], idf_sb[0:16, 0:16]))
                A("vector", lambda e: e.tensor_copy(lgt[:], pB[0:128, 0:16]))
                A("vector", lambda e: e.reduce_max(rmax[:], lgt[:], axis=AX.X))
                A("vector", lambda e: e.tensor_scalar_mul(nrmax[:], rmax[:], -1.0))
                A("scalar", lambda e: e.activation(out=esc[:], in_=lgt[:], func=AF.Exp,
                                                   bias=nrmax[:]))
                A("vector", lambda e: e.max(top8[:], esc[:]))
                A("vector", lambda e: e.tensor_copy(m4[:], top8[:, 3:4]))
                A("vector", lambda e: e.tensor_scalar(keep[:], esc[:], m4[:], None,
                                                      op0=ALU.is_ge))
                A("vector", lambda e: e.tensor_tensor(keep[:], keep[:], esc[:],
                                                      op=ALU.mult))
                A("vector", lambda e: e.reduce_sum(wsum[:], keep[:], axis=AX.X))
                A("vector", lambda e: e.reciprocal(rsum[:], wsum[:]))
                A("vector", lambda e, j=j: e.tensor_scalar_mul(
                    w_tm[:, j * 16:(j + 1) * 16], keep[:], rsum[:]))
                A("tensor", lambda e, j=j: e.transpose(
                    pB[0:16, 0:128], w_tm[:, j * 16:(j + 1) * 16], idf_sb[:]))
                A("vector", lambda e, j=j: e.tensor_copy(
                    w_rows[:, j * 128:(j + 1) * 128], pB[0:16, 0:128]))
            for ep in range(2):
                A("tensor", lambda e, ep=ep: e.matmul(
                    pB[0:128, 0:512], esel_sb[:, ep * 128:(ep + 1) * 128], w_rows[:],
                    start=True, stop=True))
                A("vector", lambda e, ep=ep: e.tensor_copy(
                    wb[:, ep * 512:(ep + 1) * 512], pB[0:128, 0:512]))
            for ep in range(2):
                for ic in range(14):
                    P.dma("gpsimd", out=gw[:].rearrange("p (c i) -> p c i", i=128),
                          in_=gp2[ep][:, ic * 128:(ic + 1) * 128]
                          .rearrange("(c p) i -> p c i", p=128))
                    P.dma("gpsimd", chain=True,
                          out=uw[:].rearrange("p (c i) -> p c i", i=128),
                          in_=up2[ep][:, ic * 128:(ic + 1) * 128]
                          .rearrange("(c p) i -> p c i", p=128))
                    for ch in range(16):
                        P.add("tensor", (lambda e, ch=ch: e.matmul(
                            pA[:], gw[:, ch * 128:(ch + 1) * 128],
                            x4fm[:, ch * 512:(ch + 1) * 512],
                            start=(ch == 0), stop=(ch == 15))), 0, chain=(ch > 0))
                    for ch in range(16):
                        P.add("tensor", (lambda e, ch=ch: e.matmul(
                            pB[:], uw[:, ch * 128:(ch + 1) * 128],
                            x4fm[:, ch * 512:(ch + 1) * 512],
                            start=(ch == 0), stop=(ch == 15))),
                            1 if ch == 15 else 0, chain=(ch > 0))
                    A("scalar", lambda e: e.activation(out=sg[:], in_=pA[:], func=AF.Silu))
                    A("vector", lambda e: e.tensor_tensor(abuf[:], pB[:], sg[:],
                                                          op=ALU.mult))
                    A("vector", lambda e, ep=ep, ic=ic: e.tensor_tensor(
                        act[:, (ep * 14 + ic) * 512:(ep * 14 + ic + 1) * 512],
                        abuf[:], wb[:, ep * 512:(ep + 1) * 512], op=ALU.mult))
            for hc in range(16):
                first = True
                for ep in range(2):
                    P.dma("gpsimd", out=dwc[:].rearrange("p (c h) -> p c h", h=128),
                          in_=dp2[ep][:, hc * 128:(hc + 1) * 128]
                          .rearrange("(c p) h -> p c h", p=128))
                    for ic in range(14):
                        P.add("tensor", (lambda e, ep=ep, ic=ic: e.matmul(
                            pC[:], dwc[:, ic * 128:(ic + 1) * 128],
                            act[:, (ep * 14 + ic) * 512:(ep * 14 + ic + 1) * 512],
                            start=(ep == 0 and ic == 0),
                            stop=(ep == 1 and ic == 13))),
                            1 if ic == 13 else 0, chain=(ic > 0))
                A("vector", lambda e: e.tensor_copy(po_sb[:], pC[:]))
                for tj in range(4):
                    A("tensor", lambda e, tj=tj: e.transpose(
                        pD[0:128, 0:128], po_sb[:, tj * 128:(tj + 1) * 128], idf_sb[:]))
                    A("vector", lambda e, tj=tj: e.tensor_copy(
                        po_t[:, tj * 128:(tj + 1) * 128], pD[0:128, 0:128]))
                P.dma("sync", out=pout.rearrange("(g tj p) d -> g p tj d", p=128, tj=4)
                      [tg][:, :, hc * 128:(hc + 1) * 128],
                      in_=po_t[:].rearrange("p (tj d) -> p tj d", d=128))
        P.emit(block, nc, gs)
    return nc


def run_moe(x3, d):
    NTG = 8
    nc = build_moe(NTG)
    cns = _attn_consts()
    parts = []
    gp = d["L1_ffn_gate_proj"].astype(np.float32)
    up = d["L1_ffn_up_proj"].astype(np.float32)
    dp = d["L1_ffn_down_proj"].astype(np.float32)
    for half in range(2):
        sl = slice(half * NTG * 512, (half + 1) * NTG * 512)
        in_maps = []
        for c in range(NCORES):
            esel = np.zeros((16, 256), dtype=np.float32)
            esel[2 * c, 0:128] = 1.0
            esel[2 * c + 1, 128:256] = 1.0
            in_maps.append({
                "x3": np.ascontiguousarray(x3[sl]),
                "rw": d["L1_ffn_router_w"].astype(np.float32),
                "ebias": d["L1_ffn_expert_bias"].reshape(1, 16).astype(np.float32),
                "gp2": np.ascontiguousarray(gp[2 * c:2 * c + 2]),
                "up2": np.ascontiguousarray(up[2 * c:2 * c + 2]),
                "dp2": np.ascontiguousarray(dp[2 * c:2 * c + 2]),
                "esel": esel,
                "ident_bf": cns["ident_bf"], "ident_f32": cns["ident_f32"],
                "ones_row_bf": np.ones((1, 128), dtype=BF),
            })
        res = run_bass_kernel_spmd(nc, in_maps, core_ids=list(range(NCORES)))
        parts.append(np.stack([res.results[c]["pout"] for c in range(NCORES)], 0))
    return np.concatenate(parts, 1)



def build_combine():
    nc = bass.Bass("TRN2", target_bir_lowering=False)
    x3 = nc.dram_tensor("x3", [PTOK, D], F32, kind="ExternalInput").ap()
    parts = nc.dram_tensor("parts", [8, PTOK, D], F32, kind="ExternalInput").ap()
    x4o = nc.dram_tensor("x4", [PTOK, D], F32, kind="ExternalOutput").ap()
    x33 = x3.rearrange("(t p) d -> t p d", p=128)
    x43 = x4o.rearrange("(t p) d -> t p d", p=128)
    p4 = parts.rearrange("r (t p) d -> r t p d", p=128)
    with contextlib.ExitStack() as es:
        sb = lambda n, s, d: es.enter_context(nc.sbuf_tensor(n, s, d))
        sem = lambda n: es.enter_context(nc.semaphore(n))
        acc = sb("acc", [128, D], F32)
        tmp = sb("tmp", [128, D], F32)
        gs = sem("gs")
        block = es.enter_context(nc.Block())
        P = Plan()
        A = P.add
        for i in range(NTILE):
            P.dma("sync", out=acc[:], in_=x33[i])
            for r in range(8):
                P.dma("sync", out=tmp[:], in_=p4[r][i])
                A("vector", lambda e: e.tensor_add(acc[:], acc[:], tmp[:]))
            P.dma("sync", out=x43[i], in_=acc[:])
        P.emit(block, nc, gs)
    return nc


def run_combine(x3, parts):
    nc = build_combine()
    in_maps = [{"x3": np.ascontiguousarray(x3[c * PTOK:(c + 1) * PTOK]),
                "parts": np.ascontiguousarray(parts[:, c * PTOK:(c + 1) * PTOK])}
               for c in range(NCORES)]
    res = run_bass_kernel_spmd(nc, in_maps, core_ids=list(range(NCORES)))
    return np.concatenate([res.results[c]["x4"] for c in range(NCORES)], 0)


def _attn_consts():
    masks = np.zeros((4, 128, 512), dtype=BF)
    for j in range(4):
        for kk in range(128):
            masks[j, kk, j * 128 + kk:] = 1.0
    return {
        "masks": masks,
        "ident_bf": np.eye(128, dtype=BF),
        "ident_f32": np.eye(128, dtype=np.float32),
        "ones_col_bf": np.ones((128, 1), dtype=BF),
        "ones_row_f32": np.ones((1, 128), dtype=np.float32),
    }


def run_attn0(ln, hn):
    nc = build_attn0()
    cns = _attn_consts()
    ln4 = ln.reshape(2, 4096, 16, 128)
    in_maps = []
    for c in range(NCORES):
        lnh = np.stack([np.ascontiguousarray(ln4[(4 * c + j) // 16, :, (4 * c + j) % 16])
                        for j in range(4)])
        hn_sel = np.stack([np.ascontiguousarray(hn[4 * ((4 * c + j) // 16):
                                                   4 * ((4 * c + j) // 16) + 4,
                                                   (4 * c + j) % 16])
                           for j in range(4)])
        in_maps.append(dict(lnh=lnh, hn_sel=hn_sel, **cns))
    res = run_bass_kernel_spmd(nc, in_maps, core_ids=list(range(NCORES)))
    att = np.zeros((2, 4096, 16, 128), dtype=np.float32)
    for c in range(NCORES):
        for j in range(4):
            bh = 4 * c + j
            att[bh // 16, :, bh % 16] = res.results[c]["attn"][j]
    return att.reshape(NTOK, D)





G4 = [[0, 1, 2, 3], [4, 5, 6, 7]]
G8 = [[0, 1, 2, 3, 4, 5, 6, 7]]
SCATT = 1.0 / float(np.sqrt(128.0))
NU_F2 = 768


class Arena:

    def __init__(self, ap, cols):
        self.ap = ap
        self.cols = cols
        self.off = 0

    def b(self, cols):
        self.off = (self.off + 15) & ~15
        assert self.off + cols <= self.cols, f"arena overflow {self.off}+{cols}"
        v = self.ap[:, self.off:self.off + cols]
        self.off += cols
        return v

    def f(self, cols_f32):
        v = self.b(2 * cols_f32)
        return v.bitcast(F32)

    def reset(self, off=0):
        self.off = off


def build_fused(upto="x4"):
    nc = bass.Bass("TRN2", target_bir_lowering=False)
    x_in = nc.dram_tensor("x", [PTOK, D], F32, kind="ExternalInput").ap()
    masks = nc.dram_tensor("masks", [32, 128, 1024], BF16, kind="ExternalInput").ap()
    sim = nc.dram_tensor("sim", [D, 32], F32, kind="ExternalInput").ap()
    gates = nc.dram_tensor("gates", [1, 32], F32, kind="ExternalInput").ap()
    w_v = nc.dram_tensor("w_v", [32, 64 * D], F32, kind="ExternalInput").ap()
    o_w = nc.dram_tensor("o_w", [32, 64 * D], F32, kind="ExternalInput").ap()
    Wk = nc.dram_tensor("Wk", [512, D], F32, kind="ExternalInput").ap()
    bk = nc.dram_tensor("bk", [1, 512], F32, kind="ExternalInput").ap()
    Wv = nc.dram_tensor("Wv", [512, D], F32, kind="ExternalInput").ap()
    bv = nc.dram_tensor("bv", [1, 512], F32, kind="ExternalInput").ap()
    Wmb = nc.dram_tensor("Wmb", [D, D], F32, kind="ExternalInput").ap()
    bmb = nc.dram_tensor("bmb", [1, D], F32, kind="ExternalInput").ap()
    Wq = nc.dram_tensor("Wq", [D, D], F32, kind="ExternalInput").ap()
    bq = nc.dram_tensor("bq", [1, D], F32, kind="ExternalInput").ap()
    Wg = nc.dram_tensor("Wg", [D, D], F32, kind="ExternalInput").ap()
    bg = nc.dram_tensor("bg", [1, D], F32, kind="ExternalInput").ap()
    memory = nc.dram_tensor("memory", [16 * 768, 128], F32, kind="ExternalInput").ap()
    norm = nc.dram_tensor("norm", [16, 768], F32, kind="ExternalInput").ap()
    if upto == "x4":
        rw = nc.dram_tensor("rw", [16, D], F32, kind="ExternalInput").ap()
        ebias = nc.dram_tensor("ebias", [1, 16], F32, kind="ExternalInput").ap()
        gp = nc.dram_tensor("gp", [16, D, 1792], F32, kind="ExternalInput").ap()
        upj = nc.dram_tensor("up", [16, D, 1792], F32, kind="ExternalInput").ap()
        dpj = nc.dram_tensor("dp", [16, 1792, D], F32, kind="ExternalInput").ap()
        esel = nc.dram_tensor("esel", [16, 16 * 128], F32, kind="ExternalInput").ap()
    idb = nc.dram_tensor("ident_bf", [128, 128], BF16, kind="ExternalInput").ap()
    idf = nc.dram_tensor("ident_f32", [128, 128], F32, kind="ExternalInput").ap()
    onesrb = nc.dram_tensor("ones_row_bf", [1, 128], BF16, kind="ExternalInput").ap()
    onesrf = nc.dram_tensor("ones_row_f32", [1, 128], F32, kind="ExternalInput").ap()
    onescb = nc.dram_tensor("ones_col_bf", [128, 1], BF16, kind="ExternalInput").ap()
    onescf = nc.dram_tensor("ones_col_f32", [128, 1], F32, kind="ExternalInput").ap()
    out = nc.dram_tensor("out", [PTOK, D], F32, kind="ExternalOutput").ap()

    ag1_in = nc.dram_tensor("ag1_in", [PTOK, D], BF16)
    ag1_out = nc.dram_tensor("ag1_out", [4 * PTOK, D], BF16)
    ag2_in = nc.dram_tensor("ag2_in", [D, PTOK], BF16)
    ag2_out = nc.dram_tensor("ag2_out", [4 * D, PTOK], BF16)
    arm_in = nc.dram_tensor("arm_in", [16, 1], F32)
    arm_out = nc.dram_tensor("arm_out", [16, 1], F32)
    ass_in = nc.dram_tensor("ass_in", [1, D], F32)
    ass_out = nc.dram_tensor("ass_out", [1, D], F32)
    dar_in = nc.dram_tensor("dar_in", [12384, 128], F32)
    dar_out = nc.dram_tensor("dar_out", [12384, 128], F32)
    wv_dyn = nc.dram_tensor("wv_dyn", [64, D], BF16)
    wo_dyn = nc.dram_tensor("wo_dyn", [64, D], BF16)
    xspill = nc.dram_tensor("xspill", [PTOK, D], F32)
    ln2d = nc.dram_tensor("ln2d", [PTOK, D], BF16)
    ln3d = nc.dram_tensor("ln3d", [PTOK, D], BF16)

    xin3 = x_in.rearrange("(t p) d -> t p d", p=128)
    out3 = out.rearrange("(t p) d -> t p d", p=128)
    mem3 = memory.rearrange("(h c p) d -> h p c d", p=128, c=6)
    ln2d3 = ln2d.rearrange("(t p) d -> t p d", p=128)
    xsp3 = xspill.rearrange("(t p) d -> t p d", p=128)
    ln3d3 = ln3d.rearrange("(t p) d -> t p d", p=128)

    ACOLS = 100352
    with contextlib.ExitStack() as es:
        sb = lambda n, s, d: es.enter_context(nc.sbuf_tensor(n, s, d))
        ps = lambda n, s, d: es.enter_context(nc.psum_tensor(n, s, d))
        sem = lambda n: es.enter_context(nc.semaphore(n))
        AR = Arena(sb("arena", [128, ACOLS], BF16)[:], ACOLS)
        m_ = sb("m_", [128, 1], F32)
        v2 = sb("v2", [128, 1], F32)
        sd = sb("sd", [128, 1], F32)
        rstd = sb("rstd", [128, 1], F32)
        epsb = sb("epsb", [128, 1], F32)
        ss2 = sb("ss2", [128, 1], F32)
        nrm1 = sb("nrm1", [128, 1], F32)
        nmx = sb("nmx", [128, 1], F32)
        rmu = sb("rmu", [128, 1], F32)
        negMb = [sb(f"negMb{h}", [128, 1], F32) for h in range(16)]
        rmax = sb("rmax", [128, 1], F32)
        nrmax = sb("nrmax", [128, 1], F32)
        wsum = sb("wsum", [128, 1], F32)
        rsum = sb("rsum", [128, 1], F32)
        rden = sb("rden", [128, 1], F32)
        g_s2 = sb("g_s2", [1, 1], F32)
        g_sn = sb("g_sn", [1, 1], F32)
        g_rn = sb("g_rn", [1, 1], F32)
        g_nm = sb("g_nm", [1, 1], F32)
        g_es = sb("g_es", [1, 1], F32)
        g_rs = sb("g_rs", [1, 1], F32)
        idb_sb = sb("idb_sb", [128, 128], BF16)
        idf_sb = sb("idf_sb", [128, 128], F32)
        ones_rb = sb("ones_rb", [1, 128], BF16)
        ones_rf = sb("ones_rf", [1, 128], F32)
        ones_cb = sb("ones_cb", [128, 1], BF16)
        ones_cf = sb("ones_cf", [128, 1], F32)
        ones_rf512 = sb("ones_rf512", [1, 512], F32)
        pbf = [ps(f"pbf{i}", [128, 512], F32) for i in range(6)]
        pbb = [ps(f"pbb{i}", [128, 1024], BF16) for i in range(2)]
        gs = sem("gs")
        block = es.enter_context(nc.Block())

        P = Plan()
        A = P.add

        P.dma("sync", out=idb_sb[:], in_=idb)
        P.dma("sync", out=idf_sb[:], in_=idf)
        P.dma("sync", out=ones_rb[:], in_=onesrb)
        P.dma("sync", out=ones_rf[:], in_=onesrf)
        P.dma("sync", out=ones_cb[:], in_=onescb)
        P.dma("sync", out=ones_cf[:], in_=onescf)
        A("vector", lambda e: e.memset(epsb[:], EPS_LN))
        A("vector", lambda e: e.memset(ones_rf512[:], 1.0))

        xres = AR.f(16384)
        ln0tm = AR.b(16384)
        ln0fm = AR.b(16384)
        xm = AR.f(2048)
        sqt = AR.f(2048)
        hs = AR.f(16)
        hmax = AR.f(16)
        hrow = AR.f(16)
        ld16 = AR.f(16)

        for i in range(NTILE):
            P.dma("sync", out=xres[:, i * D:(i + 1) * D], in_=xin3[i],
                  chain=(i > 0))
        A("vector", lambda e: e.memset(hmax[:], 0.0))

        def emit_ln(src_view, dst_view, hn=False):
            A("vector", lambda e: e.reduce_sum(m_[:], src_view, axis=AX.X))
            A("vector", lambda e: e.tensor_scalar_mul(m_[:], m_[:], 1.0 / D))
            A("vector", lambda e: e.tensor_scalar_sub(xm[:], src_view, m_[:]))
            A("scalar", lambda e: e.activation(out=sqt[:], in_=xm[:], func=AF.Square,
                                               accum_out=v2[:]))
            A("scalar", lambda e: e.activation(out=sd[:], in_=v2[:], func=AF.Sqrt,
                                               scale=1.0 / D, bias=epsb[:, 0:1]))
            A("vector", lambda e: e.reciprocal(rstd[:], sd[:]))
            if hn:
                A("vector", lambda e: e.reduce_sum(
                    hs[:], sqt[:].rearrange("p (h k) -> p h k", k=128), axis=AX.X))
                A("vector", lambda e: e.tensor_scalar(hrow[:, 0:1], rstd[:], rstd[:],
                                                      None, op0=ALU.mult))
                A("vector", lambda e: e.tensor_scalar_mul(hs[:], hs[:], hrow[:, 0:1]))
                A("vector", lambda e: e.tensor_max(hmax[:], hmax[:], hs[:]))
            A("scalar", lambda e: e.activation(out=dst_view, in_=xm[:],
                                               func=AF.Identity, scale=rstd[:],
                                               bias=0.0))

        for i in range(NTILE):
            emit_ln(xres[:, i * D:(i + 1) * D], ln0tm[:, i * D:(i + 1) * D], hn=True)
        for i in range(128):
            t, ch = divmod(i, 16)
            A("tensor", lambda e, t=t, ch=ch, i=i: e.transpose(
                pbb[i % 2][0:128, 0:128],
                ln0tm[:, t * D + ch * 128:t * D + (ch + 1) * 128], idb_sb[:]))
            A("vector", lambda e, t=t, ch=ch, i=i: e.tensor_copy(
                ln0fm[:, (ch * 8 + t) * 128:(ch * 8 + t + 1) * 128],
                pbb[i % 2][0:128, 0:128]))
        A("tensor", lambda e: e.transpose(pbf[0][0:16, 0:128], hmax[:, 0:16],
                                          idf_sb[:]))
        A("vector", lambda e: e.reduce_max(hrow[0:16, 0:1], pbf[0][0:16, 0:128],
                                           axis=AX.X))

        P.dma("sync", out=ag1_in.ap().rearrange("(t p) d -> p t d", p=128),
              in_=ln0tm[:].rearrange("p (t d) -> p t d", d=D))
        P.dma("sync", chain=True,
              out=ag2_in.ap().rearrange("(c p) (t j) -> p c t j", p=128, j=128),
              in_=ln0fm[:].rearrange("p (c t j) -> p c t j", t=8, j=128))
        P.dma("sync", chain=True, out=arm_in.ap(), in_=hrow[0:16, 0:1])
        A("gpsimd", lambda e: e.collective_compute(
            "AllGather", ALU.bypass, replica_groups=G4,
            ins=[ag1_in.ap().opt()], outs=[ag1_out.ap().opt()]))
        A("gpsimd", lambda e: e.collective_compute(
            "AllGather", ALU.bypass, replica_groups=G4,
            ins=[ag2_in.ap().opt()], outs=[ag2_out.ap().opt()]))
        A("gpsimd", lambda e: e.collective_compute(
            "AllReduce", ALU.max, replica_groups=G4,
            ins=[arm_in.ap().opt()], outs=[arm_out.ap().opt()]))
        P.dma("sync", out=ld16[0:16, 0:1], in_=arm_out.ap())
        A("vector", lambda e: e.tensor_scalar_mul(ld16[0:16, 0:1], ld16[0:16, 0:1],
                                                  -SCATT))
        A("tensor", lambda e: e.transpose(pbf[0][0:1, 0:16], ld16[0:16, 0:1],
                                          idf_sb[0:16, 0:16]))
        A("vector", lambda e: e.tensor_copy(ld16[0:1, 0:16], pbf[0][0:1, 0:16]))
        A("tensor", lambda e: e.matmul(pbf[1][0:128, 0:16], ones_rf[:],
                                       ld16[0:1, 0:16], start=True, stop=True))
        for h in range(16):
            A("vector", lambda e, h=h: e.tensor_copy(negMb[h][:], pbf[1][:, h:h + 1]))

        kbuf = AR.b(2 * 128)
        vbuf = AR.b(2 * 128)
        mbuf = AR.b(2 * 1024)
        ptb = AR.b(2 * 512)
        sums_sb = AR.f(512)
        recip_sb = AR.f(512)
        sums_cols = AR.f(8)
        acc_sb = AR.f(512)
        attc = AR.f(512)
        ag1r = ag1_out.ap()
        ag2r = ag2_out.ap()
        def attn_dma(h, kt):
            sl = kt % 2
            P.dma("sync", out=kbuf[:, sl * 128:(sl + 1) * 128],
                  in_=ag2r[(kt // 8) * D + h * 128:(kt // 8) * D + (h + 1) * 128,
                           (kt % 8) * 128:(kt % 8 + 1) * 128])
            P.dma("sync", chain=True, out=vbuf[:, sl * 128:(sl + 1) * 128],
                  in_=ag1r[kt * 128:(kt + 1) * 128, h * 128:(h + 1) * 128])
            P.dma("sync", chain=True, out=mbuf[:, sl * 1024:(sl + 1) * 1024],
                  in_=masks[kt])

        attn_dma(0, 0)
        for h in range(16):
            for kt in range(32):
                sl = kt % 2
                if kt + 1 < 32:
                    attn_dma(h, kt + 1)
                elif h + 1 < 16:
                    attn_dma(h + 1, 0)
                A("tensor", lambda e, h=h, sl=sl: e.matmul(
                    pbf[4][:, 0:512], kbuf[:, sl * 128:(sl + 1) * 128],
                    ln0fm[:, (h * 8) * 128:(h * 8 + 4) * 128],
                    start=True, stop=True))
                P.add("tensor", (lambda e, h=h, sl=sl: e.matmul(
                    pbf[5][:, 0:512], kbuf[:, sl * 128:(sl + 1) * 128],
                    ln0fm[:, (h * 8 + 4) * 128:(h * 8 + 8) * 128],
                    start=True, stop=True)), 1, chain=True)
                A("scalar", lambda e, h=h: e.activation(
                    out=ptb[:, 0:512], in_=pbf[4][:, 0:512],
                    func=AF.Exp, scale=SCATT, bias=negMb[h][:]))
                P.add("scalar", (lambda e, h=h: e.activation(
                    out=ptb[:, 512:1024], in_=pbf[5][:, 0:512],
                    func=AF.Exp, scale=SCATT, bias=negMb[h][:])), 1, chain=True)
                A("vector", lambda e, sl=sl: e.tensor_tensor(
                    ptb[:, 0:512], ptb[:, 0:512],
                    mbuf[:, sl * 1024:sl * 1024 + 512], op=ALU.mult))
                P.add("vector", (lambda e, sl=sl: e.tensor_tensor(
                    ptb[:, 512:1024], ptb[:, 512:1024],
                    mbuf[:, sl * 1024 + 512:sl * 1024 + 1024], op=ALU.mult)),
                    1, chain=True)
                A("tensor", lambda e, sl=sl, kt=kt: e.matmul(
                    pbf[0][:, 0:512], vbuf[:, sl * 128:(sl + 1) * 128],
                    ptb[:, 0:512], start=(kt == 0), stop=(kt == 31)))
                P.add("tensor", (lambda e, kt=kt: e.matmul(
                    pbf[2][0:1, 0:512], ones_cb[:], ptb[:, 0:512],
                    start=(kt == 0), stop=(kt == 31))), 0, chain=True)
                P.add("tensor", (lambda e, sl=sl, kt=kt: e.matmul(
                    pbf[1][:, 0:512], vbuf[:, sl * 128:(sl + 1) * 128],
                    ptb[:, 512:1024], start=(kt == 0), stop=(kt == 31))),
                    0, chain=True)
                P.add("tensor", (lambda e, kt=kt: e.matmul(
                    pbf[3][0:1, 0:512], ones_cb[:], ptb[:, 512:1024],
                    start=(kt == 0), stop=(kt == 31))), 1, chain=True)
            for qg in range(2):
                A("vector", lambda e, qg=qg: e.tensor_copy(sums_sb[0:1, :],
                                                           pbf[2 + qg][0:1, 0:512]))
                A("vector", lambda e: e.reciprocal(recip_sb[0:1, :], sums_sb[0:1, :]))
                for j in range(4):
                    A("tensor", lambda e, qg=qg, j=j: e.transpose(
                        pbf[2 + qg][0:128, j:j + 1],
                        recip_sb[0:1, j * 128:(j + 1) * 128], idf_sb[0:1, 0:1]))
                A("vector", lambda e, qg=qg: e.tensor_copy(sums_cols[:, 0:4],
                                                           pbf[2 + qg][0:128, 0:4]))
                A("vector", lambda e, qg=qg: e.tensor_copy(acc_sb[:],
                                                           pbf[qg][:, 0:512]))
                for j in range(4):
                    A("tensor", lambda e, qg=qg, j=j: e.transpose(
                        pbf[2 + qg][0:128, j * 128:(j + 1) * 128],
                        acc_sb[:, j * 128:(j + 1) * 128], idf_sb[:]))
                for j in range(4):
                    A("vector", lambda e, qg=qg, j=j: e.tensor_scalar_mul(
                        attc[:, j * 128:(j + 1) * 128],
                        pbf[2 + qg][0:128, j * 128:(j + 1) * 128],
                        sums_cols[:, j:j + 1]))
                for j in range(4):
                    o = (qg * 4 + j) * D + h * 128
                    A("vector", lambda e, o=o, j=j: e.tensor_add(
                        xres[:, o:o + 128], xres[:, o:o + 128],
                        attc[:, j * 128:(j + 1) * 128]))
        if upto == "x1":
            for i in range(NTILE):
                P.dma("sync", out=out3[i], in_=xres[:, i * D:(i + 1) * D])
            P.emit(block, nc, gs)
            return nc

        AR.reset(32768)
        ln1fm = AR.b(16384)
        lntile = AR.b(2048)
        xm4v = AR.f(2048)
        scr4 = AR.f(2048)

        def emit_ln_s(src_view, dst_view, xmv, scrv):
            A("vector", lambda e: e.reduce_sum(m_[:], src_view, axis=AX.X))
            A("vector", lambda e: e.tensor_scalar_mul(m_[:], m_[:], 1.0 / D))
            A("vector", lambda e: e.tensor_scalar_sub(xmv[:], src_view, m_[:]))
            A("scalar", lambda e: e.activation(out=scrv[:], in_=xmv[:],
                                               func=AF.Square, accum_out=v2[:]))
            A("scalar", lambda e: e.activation(out=sd[:], in_=v2[:], func=AF.Sqrt,
                                               scale=1.0 / D, bias=epsb[:, 0:1]))
            A("vector", lambda e: e.reciprocal(rstd[:], sd[:]))
            A("scalar", lambda e: e.activation(out=dst_view, in_=xmv[:],
                                               func=AF.Identity, scale=rstd[:],
                                               bias=0.0))

        for i in range(NTILE):
            emit_ln_s(xres[:, i * D:(i + 1) * D], lntile[:], xm4v, scr4)
            for j in range(4):
                A("tensor", lambda e, i=i, j=j: e.matmul(
                    pbf[2 + j][0:1, 0:512], rstd[:], xm4v[:, j * 512:(j + 1) * 512],
                    start=(i == 0), stop=(i == NTILE - 1)))
            for ch in range(16):
                ii = i * 16 + ch
                A("tensor", lambda e, ch=ch, ii=ii: e.transpose(
                    pbb[ii % 2][0:128, 0:128], lntile[:, ch * 128:(ch + 1) * 128],
                    idb_sb[:]))
                A("vector", lambda e, i=i, ch=ch, ii=ii: e.tensor_copy(
                    ln1fm[:, (ch * 8 + i) * 128:(ch * 8 + i + 1) * 128],
                    pbb[ii % 2][0:128, 0:128]))
        ssrow = AR.f(2048)
        for j in range(4):
            A("vector", lambda e, j=j: e.tensor_copy(
                ssrow[0:1, j * 512:(j + 1) * 512], pbf[2 + j][0:1, 0:512]))
        P.dma("sync", out=ass_in.ap(), in_=ssrow[0:1, 0:2048])
        A("gpsimd", lambda e: e.collective_compute(
            "AllReduce", ALU.add, replica_groups=G4,
            ins=[ass_in.ap().opt()], outs=[ass_out.ap().opt()]))
        srep = AR.f(2048)
        simf = AR.f(512)
        srf4 = AR.f(16)
        aff = AR.f(32)
        t32a = AR.f(32)
        t32b = AR.f(32)
        t8 = AR.f(8)
        row32a = AR.f(32)
        row32b = AR.f(32)
        rt_bf = AR.b(2)
        ro_bf = AR.b(2)
        P.dma("sync", out=srep[0:1, :], in_=ass_out.ap())
        A("vector", lambda e: e.tensor_scalar_mul(srep[0:1, :], srep[0:1, :],
                                                  1.0 / 4096.0))
        P.dma("sync", out=simf[:].rearrange("p (c e) -> p c e", e=32),
              in_=sim.rearrange("(c p) e -> p c e", p=128))
        A("scalar", lambda e: e.activation(out=xm4v[0:1, :], in_=srep[0:1, :],
                                           func=AF.Square, accum_out=g_s2[0:1, :]))
        A("scalar", lambda e: e.activation(out=g_sn[0:1, :], in_=g_s2[0:1, :],
                                           func=AF.Sqrt))
        A("vector", lambda e: e.reciprocal(g_rn[0:1, :], g_sn[0:1, :]))
        for ch in range(16):
            A("tensor", lambda e, ch=ch: e.transpose(
                pbf[3][0:128, 0:1], srep[0:1, ch * 128:(ch + 1) * 128],
                idf_sb[0:1, 0:1]))
            A("vector", lambda e, ch=ch: e.tensor_copy(srf4[:, ch:ch + 1],
                                                       pbf[3][0:128, 0:1]))
        for ch in range(16):
            A("tensor", lambda e, ch=ch: e.matmul(
                pbf[3][0:1, 0:32], srf4[:, ch:ch + 1],
                simf[:, ch * 32:(ch + 1) * 32], start=(ch == 0), stop=(ch == 15)))
        A("vector", lambda e: e.tensor_copy(aff[0:1, :], pbf[3][0:1, 0:32]))
        A("vector", lambda e: e.tensor_scalar_mul(aff[0:1, :], aff[0:1, :],
                                                  g_rn[0:1, 0:1]))
        A("scalar", lambda e: e.activation(out=simf[:], in_=simf[:], func=AF.Square))
        A("tensor", lambda e: e.matmul(pbf[3][0:1, 0:512], ones_cf[:], simf[:],
                                       start=True, stop=True))
        A("vector", lambda e: e.reduce_sum(
            row32a[0:1, :], pbf[3][0:1, 0:512].rearrange("p (c e) -> p e c", e=32),
            axis=AX.X))
        A("scalar", lambda e: e.activation(out=row32a[0:1, :], in_=row32a[0:1, :],
                                           func=AF.Sqrt))
        A("vector", lambda e: e.reciprocal(row32b[0:1, :], row32a[0:1, :]))
        A("vector", lambda e: e.tensor_tensor(aff[0:1, :], aff[0:1, :],
                                              row32b[0:1, :], op=ALU.mult))
        gz = AR.f(32)
        P.dma("sync", out=gz[0:1, :], in_=gates)
        A("scalar", lambda e: e.activation(out=t32a[0:1, :], in_=gz[0:1, 0:32],
                                           func=AF.Sigmoid))
        logit = aff
        A("vector", lambda e: e.tensor_sub(logit[0:1, :], aff[0:1, :], t32a[0:1, :]))
        gated = t32a
        A("vector", lambda e: e.tensor_scalar_max(gated[0:1, :], logit[0:1, :], 0.0))
        maskt = t32b
        A("vector", lambda e: e.tensor_scalar(maskt[0:1, :], logit[0:1, :], 0.0,
                                              None, op0=ALU.is_gt))
        A("vector", lambda e: e.reduce_sum(g_nm[0:1, :], maskt[0:1, :], axis=AX.X))
        A("vector", lambda e: e.tensor_scalar(g_nm[0:1, :], g_nm[0:1, :], 0.0, None,
                                              op0=ALU.is_equal))
        A("vector", lambda e: e.max(t8[0:1, :], logit[0:1, :]))
        m8v = g_es
        A("vector", lambda e: e.tensor_copy(m8v[0:1, :], t8[0:1, 7:8]))
        pen = row32a
        A("vector", lambda e: e.tensor_scalar(pen[0:1, :], logit[0:1, :],
                                              m8v[0:1, 0:1], -1e30,
                                              op0=ALU.is_ge, op1=ALU.mult))
        A("vector", lambda e: e.tensor_add(pen[0:1, :], logit[0:1, :], pen[0:1, :]))
        A("vector", lambda e: e.max(t8[0:1, :], pen[0:1, :]))
        A("vector", lambda e: e.tensor_copy(m8v[0:1, :], t8[0:1, 7:8]))
        fbm = row32a
        A("vector", lambda e: e.tensor_scalar(fbm[0:1, :], logit[0:1, :],
                                              m8v[0:1, 0:1], None, op0=ALU.is_ge))
        A("vector", lambda e: e.tensor_scalar_mul(fbm[0:1, :], fbm[0:1, :],
                                                  g_nm[0:1, 0:1]))
        selm = row32b
        A("vector", lambda e: e.tensor_max(selm[0:1, :], maskt[0:1, :], fbm[0:1, :]))
        ml = aff
        A("vector", lambda e: e.tensor_tensor(ml[0:1, :], gated[0:1, :],
                                              selm[0:1, :], op=ALU.mult))
        pen2 = t32a
        A("vector", lambda e: e.tensor_scalar(pen2[0:1, :], selm[0:1, :], 1e30,
                                              -1e30, op0=ALU.mult, op1=ALU.add))
        A("vector", lambda e: e.tensor_add(ml[0:1, :], ml[0:1, :], pen2[0:1, :]))
        A("vector", lambda e: e.reduce_max(g_nm[0:1, :], ml[0:1, :], axis=AX.X))
        A("vector", lambda e: e.tensor_scalar_mul(g_nm[0:1, :], g_nm[0:1, :], -1.0))
        A("scalar", lambda e: e.activation(out=pen2[0:1, :], in_=ml[0:1, :],
                                           func=AF.Exp, bias=g_nm[0:1, 0:1],
                                           accum_out=g_es[0:1, :]))
        A("vector", lambda e: e.reciprocal(g_rs[0:1, :], g_es[0:1, :]))
        routing = ml
        A("vector", lambda e: e.tensor_scalar_mul(routing[0:1, :], pen2[0:1, :],
                                                  g_rs[0:1, 0:1]))
        routm = pen2
        A("vector", lambda e: e.tensor_tensor(routm[0:1, :], routing[0:1, :],
                                              selm[0:1, :], op=ALU.mult))
        A("tensor", lambda e: e.transpose(pbf[3][0:32, 0:1], routm[0:1, 0:32],
                                          idf_sb[0:1, 0:1]))
        A("vector", lambda e: e.tensor_copy(rt_bf[0:32, 0:1], pbf[3][0:32, 0:1]))
        A("tensor", lambda e: e.transpose(pbf[3][0:32, 0:1], routing[0:1, 0:32],
                                          idf_sb[0:1, 0:1]))
        A("vector", lambda e: e.tensor_copy(ro_bf[0:32, 0:1], pbf[3][0:32, 0:1]))
        if upto == "gat":
            dbg = AR.f(2048)
            P.dma("sync", out=out[0:1, :], in_=srep[0:1, :])
            P.dma("sync", out=out[1:2, 0:32], in_=logit[0:1, :])
            P.dma("sync", out=out[2:3, 0:32], in_=maskt[0:1, :])
            P.dma("sync", out=out[3:4, 0:32], in_=selm[0:1, :])
            P.dma("sync", out=out[4:5, 0:32], in_=routing[0:1, :])
            P.dma("sync", out=out[5:6, 0:32], in_=routm[0:1, :])
        wvf = AR.b(2 * D)
        dynsb = AR.b(2 * D)
        for wsrc, rt, wdst in ((w_v, rt_bf, wv_dyn), (o_w, ro_bf, wo_dyn)):
            for chk in range(64):
                b0 = (chk % 2) * D
                P.dma("gpsimd", out=wvf[0:32, b0:b0 + D],
                      in_=wsrc[:, chk * D:(chk + 1) * D])
                for j in range(4):
                    P.add("tensor", (lambda e, rt=rt, b0=b0, j=j: e.matmul(
                        pbf[j][0:1, 0:512], rt[0:32, 0:1],
                        wvf[0:32, b0 + j * 512:b0 + (j + 1) * 512],
                        start=True, stop=True)), 1 if j == 3 else 0, chain=(j > 0))
                for j in range(4):
                    P.add("vector", (lambda e, b0=b0, j=j: e.tensor_copy(
                        dynsb[0:1, b0 + j * 512:b0 + (j + 1) * 512],
                        pbf[j][0:1, 0:512])), 1 if j == 3 else 0, chain=(j > 0))
                P.dma("sync", out=wdst.ap().rearrange("h c -> (h c)")
                      [chk * D:(chk + 1) * D].rearrange("(a b) -> a b", a=1),
                      in_=dynsb[0:1, b0:b0 + D])
        wvT = AR.b(16 * 64)
        wo_sel = AR.b(2048)
        for ch in range(16):
            P.add("sync", (lambda e, ch=ch: e.dma_start_transpose(
                wvT[:, ch * 64:(ch + 1) * 64],
                wv_dyn.ap()[:, ch * 128:(ch + 1) * 128])), 16, chain=(ch > 0))
        P.dma("sync", out=wo_sel[0:64, :], in_=wo_dyn.ap())
        if upto == "gat":
            A("vector", lambda e: e.tensor_copy(dbg[0:64, 0:64], wvT[0:64, 0:64]))
            P.dma("sync", out=out[6:7, 0:512], in_=wvT[0:1, 0:1024].bitcast(F32))
            P.dma("sync", out=out[7:8, 0:512], in_=wo_sel[0:1, 0:1024].bitcast(F32))
            for i in range(8, NTILE):
                pass
            P.emit(block, nc, gs)
            return nc
        cbT = AR.b(1024)
        for t in range(NTILE):
            for ch in range(16):
                P.add("tensor", (lambda e, t=t, ch=ch: e.matmul(
                    pbf[2][0:128, 0:64],
                    ln1fm[:, (ch * 8 + t) * 128:(ch * 8 + t + 1) * 128],
                    wvT[:, ch * 64:(ch + 1) * 64],
                    start=(ch == 0), stop=(ch == 15))), 1 if ch == 15 else 0,
                    chain=(ch > 0))
            A("vector", lambda e, t=t: e.tensor_copy(lntile[:, 0:64],
                                                     pbf[2][0:128, 0:64]))
            A("tensor", lambda e, t=t: e.transpose(pbb[0][0:64, 0:128],
                                                   lntile[:, 0:64], idb_sb[:]))
            A("vector", lambda e, t=t: e.tensor_copy(cbT[0:64, t * 128:(t + 1) * 128],
                                                     pbb[0][0:64, 0:128]))
        if upto == "cmb":
            P.dma("sync", out=out[0:64, 0:512], in_=cbT[0:64, 0:1024].bitcast(F32))
            P.emit(block, nc, gs)
            return nc
        ffn_t = AR.f(512)
        for t in range(NTILE):
            for j in range(4):
                A("tensor", lambda e, t=t, j=j: e.matmul(
                    pbf[3][0:128, :], cbT[0:64, t * 128:(t + 1) * 128],
                    wo_sel[0:64, j * 512:(j + 1) * 512], start=True, stop=True))
                A("vector", lambda e: e.tensor_copy(ffn_t[:], pbf[3][0:128, :]))
                A("vector", lambda e, t=t, j=j: e.tensor_add(
                    xres[:, t * D + j * 512:t * D + (j + 1) * 512],
                    xres[:, t * D + j * 512:t * D + (j + 1) * 512], ffn_t[:]))
        if upto == "x2":
            for i in range(NTILE):
                P.dma("sync", out=out3[i], in_=xres[:, i * D:(i + 1) * D])
            P.emit(block, nc, gs)
            return nc

        AR.reset(32768)
        multiuse = AR.b(16384)
        ln2fm = AR.b(16384)
        wT = AR.b(8192)
        wrow = AR.b(4096)
        v_sb = AR.b(4096)
        ktile = AR.f(512)
        a_sb = AR.f(256)
        mkraw = AR.f(768)
        numsb = AR.f(1024)
        den_sb = AR.f(1024)
        den_cols = AR.f(8)
        rden_cols = AR.f(8)
        coef_cols = AR.f(8)
        coefb = AR.b(8)
        normT = AR.b(96)
        mem_sb = AR.b(768)
        dm_sb = AR.f(128)
        dn_sb = AR.f(768)
        wmv_sb = AR.b(1024)
        biasr = AR.b(2048)
        xm5 = wT[:, 0:4096].bitcast(F32)
        scr5 = wT[:, 4096:6144]
        for i in range(NTILE):
            emit_ln_s(xres[:, i * D:(i + 1) * D], multiuse[:, i * D:(i + 1) * D],
                      xm5, scr5)
            P.dma("sync", out=ln2d3[i], in_=multiuse[:, i * D:(i + 1) * D])
        for i in range(NTILE):
            P.dma("sync", out=xsp3[i], in_=xres[:, i * D:(i + 1) * D],
                  chain=(i > 0))
        xresb = AR.ap[:, 0:32768]
        mkfm_a = [xresb[:, g * 6144:(g + 1) * 6144] for g in range(4)]
        mktm_g = xresb[:, 24576:30720]
        for i in range(128):
            t, ch = divmod(i, 16)
            A("tensor", lambda e, t=t, ch=ch, i=i: e.transpose(
                pbb[i % 2][0:128, 0:128],
                multiuse[:, t * D + ch * 128:t * D + (ch + 1) * 128], idb_sb[:]))
            A("vector", lambda e, t=t, ch=ch, i=i: e.tensor_copy(
                ln2fm[:, (ch * 8 + t) * 128:(ch * 8 + t + 1) * 128],
                pbb[i % 2][0:128, 0:128]))

        def load_wt(Wsrc, j0, npt):
            W3 = Wsrc.rearrange("(t p) d -> t p d", p=128)
            P.dma("gpsimd", out=wrow[:, 0:D], in_=W3[j0])
            for jt in range(npt):
                slot = (jt % 2) * D
                if jt + 1 < npt:
                    P.dma("gpsimd", out=wrow[:, ((jt + 1) % 2) * D:((jt + 1) % 2) * D + D],
                          in_=W3[j0 + jt + 1])
                for ch in range(0, 16, 2):
                    A("tensor", lambda e, ch=ch, slot=slot: e.transpose(
                        pbb[0][0:128, 0:128],
                        wrow[:, slot + ch * 128:slot + (ch + 1) * 128], idb_sb[:]))
                    P.add("tensor", (lambda e, ch=ch, slot=slot: e.transpose(
                        pbb[1][0:128, 0:128],
                        wrow[:, slot + (ch + 1) * 128:slot + (ch + 2) * 128],
                        idb_sb[:])), 1, chain=True)
                    A("vector", lambda e, ch=ch, jt=jt: e.tensor_copy(
                        wT[:, ch * 512 + jt * 128:ch * 512 + (jt + 1) * 128],
                        pbb[0][0:128, 0:128]))
                    P.add("vector", (lambda e, ch=ch, jt=jt: e.tensor_copy(
                        wT[:, (ch + 1) * 512 + jt * 128:(ch + 1) * 512 + (jt + 1) * 128],
                        pbb[1][0:128, 0:128])), 1, chain=True)

        def proj_tile(t, bias_off, fmv, jw=512):
            A("tensor", lambda e, bias_off=bias_off, jw=jw: e.matmul(
                pbf[0][0:128, 0:jw], ones_rb[:],
                biasr[0:1, bias_off:bias_off + jw], start=True, stop=False))
            for ch in range(16):
                P.add("tensor", (lambda e, ch=ch, t=t, jw=jw, fmv=fmv: e.matmul(
                    pbf[0][0:128, 0:jw],
                    fmv[:, (ch * 8 + t) * 128:(ch * 8 + t + 1) * 128],
                    wT[:, ch * 512:ch * 512 + jw],
                    start=False, stop=(ch == 15))), 1 if ch == 15 else 0,
                    chain=(ch > 0))

        def dpfp(srcv, dstv):
            A("scalar", lambda e: e.activation(out=a_sb[:, 0:128], in_=srcv,
                                               func=AF.Relu))
            A("scalar", lambda e: e.activation(out=a_sb[:, 128:256], in_=srcv,
                                               func=AF.Relu, scale=-1.0))
            for u in range(3):
                j = u + 1
                A("vector", lambda e, u=u, j=j: e.tensor_tensor(
                    mkraw[:, u * 256:u * 256 + j], a_sb[:, 0:j],
                    a_sb[:, 256 - j:256], op=ALU.mult))
                A("vector", lambda e, u=u, j=j: e.tensor_tensor(
                    mkraw[:, u * 256 + j:(u + 1) * 256], a_sb[:, j:256],
                    a_sb[:, 0:256 - j], op=ALU.mult))
            A("scalar", lambda e: e.activation(out=numsb[:, 0:768], in_=mkraw[:],
                                               func=AF.Square, accum_out=ss2[:]))
            A("scalar", lambda e: e.activation(out=nrm1[:], in_=ss2[:], func=AF.Sqrt))
            A("vector", lambda e: e.tensor_scalar_max(nmx[:], nrm1[:], 1e-12))
            A("vector", lambda e: e.reciprocal(rmu[:], nmx[:]))
            A("vector", lambda e: e.tensor_scalar_mul(dstv, mkraw[:], rmu[:]))

        P.dma("gpsimd", out=biasr[0:1, 0:512], in_=bk)
        load_wt(Wk, 0, 4)
        mkt_t = AR.b(768)
        for t in range(NTILE):
            proj_tile(t, 0, ln2fm)
            A("vector", lambda e: e.tensor_copy(ktile[:], pbf[0][0:128, 0:512]))
            for g in range(4):
                dpfp(ktile[:, g * 128:(g + 1) * 128], mkt_t[:])
                for fc in range(6):
                    A("tensor", lambda e, fc=fc: e.transpose(
                        pbb[fc % 2][0:128, 0:128],
                        mkt_t[:, fc * 128:(fc + 1) * 128], idb_sb[:]))
                    A("vector", lambda e, g=g, t=t, fc=fc: e.tensor_copy(
                        mkfm_a[g][:, fc * 1024 + t * 128:fc * 1024 + (t + 1) * 128],
                        pbb[fc % 2][0:128, 0:128]))
        P.dma("gpsimd", out=biasr[0:1, 0:512], in_=bv)
        load_wt(Wv, 0, 4)
        for t in range(NTILE):
            proj_tile(t, 0, ln2fm)
            A("vector", lambda e, t=t: e.tensor_copy(v_sb[:, t * 512:(t + 1) * 512],
                                                     pbf[0][0:128, 0:512]))
        P.dma("gpsimd", out=biasr[0:1, :], in_=bmb)
        for jg in range(4):
            load_wt(Wmb, 4 * jg, 4)
            for t in range(NTILE):
                proj_tile(t, jg * 512, ln2fm)
                A("scalar", lambda e, t=t, jg=jg: e.activation(
                    out=multiuse[:, t * D + jg * 512:t * D + (jg + 1) * 512],
                    in_=pbf[0][0:128, 0:512], func=AF.Sigmoid))
        nsb = numsb
        P.dma("sync", out=nsb[0:16, 0:768], in_=norm)
        for fc in range(6):
            A("tensor", lambda e, fc=fc: e.transpose(
                pbf[1][0:128, 0:16], nsb[0:16, fc * 128:(fc + 1) * 128],
                idf_sb[0:16, 0:16]))
            A("vector", lambda e, fc=fc: e.tensor_copy(
                normT[:, fc * 16:(fc + 1) * 16], pbf[1][0:128, 0:16]))
        for g in range(4):
            for i in range(48):
                fc, t = divmod(i, 8)
                A("tensor", lambda e, g=g, fc=fc, t=t, i=i: e.transpose(
                    pbb[i % 2][0:128, 0:128],
                    mkfm_a[g][:, fc * 1024 + t * 128:fc * 1024 + (t + 1) * 128],
                    idb_sb[:]))
                A("vector", lambda e, fc=fc, t=t, i=i: e.tensor_copy(
                    mktm_g[:, t * 768 + fc * 128:t * 768 + (fc + 1) * 128],
                    pbb[i % 2][0:128, 0:128]))
            for h in range(4 * g, 4 * g + 4):
                for half in range(2):
                    for fc in range(6):
                        P.add("tensor", (lambda e, h=h, fc=fc, half=half, g=g: e.matmul(
                            pbf[half][0:1, 0:512],
                            normT[:, fc * 16 + h:fc * 16 + h + 1],
                            mkfm_a[g][:, fc * 1024 + half * 512:fc * 1024 + (half + 1) * 512],
                            start=(fc == 0), stop=(fc == 5))),
                            1 if fc == 5 else 0, chain=(fc > 0))
                for half in range(2):
                    A("vector", lambda e, half=half: e.tensor_scalar_add(
                        den_sb[0:1, half * 512:(half + 1) * 512],
                        pbf[half][0:1, 0:512], 1e-8))
                for t in range(NTILE):
                    A("tensor", lambda e, t=t: e.transpose(
                        pbf[2][0:128, 0:1], den_sb[0:1, t * 128:(t + 1) * 128],
                        idf_sb[0:1, 0:1]))
                    A("vector", lambda e, t=t: e.tensor_copy(
                        den_cols[:, t:t + 1], pbf[2][0:128, 0:1]))
                A("vector", lambda e: e.reciprocal(rden_cols[:], den_cols[:]))
                A("vector", lambda e: e.tensor_scalar(coef_cols[:], den_cols[:],
                                                      -1.0, 1.0, op0=ALU.mult,
                                                      op1=ALU.add))
                A("vector", lambda e: e.tensor_scalar(coef_cols[:], coef_cols[:],
                                                      0.0, 1.0, op0=ALU.max,
                                                      op1=ALU.min))
                A("vector", lambda e: e.tensor_copy(coefb[:], coef_cols[:]))
                P.dma("gpsimd", out=mem_sb[:].rearrange("p (c d) -> p c d", d=128),
                      in_=mem3[h])
                for half in range(2):
                    for fc in range(6):
                        P.add("tensor", (lambda e, fc=fc, half=half, g=g: e.matmul(
                            pbf[half][0:128, 0:512],
                            mem_sb[:, fc * 128:(fc + 1) * 128],
                            mkfm_a[g][:, fc * 1024 + half * 512:fc * 1024 + (half + 1) * 512],
                            start=(fc == 0), stop=(fc == 5))),
                            1 if fc == 5 else 0, chain=(fc > 0))
                for half in range(2):
                    A("vector", lambda e, half=half: e.tensor_copy(
                        numsb[:, half * 512:(half + 1) * 512],
                        pbf[half][0:128, 0:512]))
                for t in range(NTILE):
                    A("tensor", lambda e, t=t: e.transpose(
                        pbf[2][0:128, 0:128], numsb[:, t * 128:(t + 1) * 128],
                        idf_sb[:]))
                    A("vector", lambda e, t=t: e.tensor_scalar_mul(
                        dm_sb[:], pbf[2][0:128, 0:128], rden_cols[:, t:t + 1]))
                    A("vector", lambda e, t=t, g=g: e.tensor_sub(
                        dm_sb[:], v_sb[:, t * 512 + g * 128:t * 512 + (g + 1) * 128],
                        dm_sb[:]))
                    A("vector", lambda e, t=t, h=h: e.tensor_tensor(
                        wmv_sb[:, t * 128:(t + 1) * 128], dm_sb[:],
                        multiuse[:, t * D + h * 128:t * D + (h + 1) * 128],
                        op=ALU.mult))
                for fc in range(6):
                    for t in range(NTILE):
                        P.add("tensor", (lambda e, fc=fc, t=t: e.matmul(
                            pbf[2][0:128, 0:128],
                            mktm_g[:, t * 768 + fc * 128:t * 768 + (fc + 1) * 128],
                            wmv_sb[:, t * 128:(t + 1) * 128],
                            start=(t == 0), stop=(t == NTILE - 1))),
                            1 if t == NTILE - 1 else 0, chain=(t > 0))
                    A("vector", lambda e: e.tensor_copy(dm_sb[:],
                                                        pbf[2][0:128, 0:128]))
                    P.dma("sync", out=dar_in.ap()
                          [h * 768 + fc * 128:h * 768 + (fc + 1) * 128, :],
                          in_=dm_sb[:])
                for t in range(NTILE):
                    P.add("tensor", (lambda e, t=t: e.matmul(
                        pbf[3][0:1, 0:512], coefb[:, t:t + 1],
                        mktm_g[:, t * 768:t * 768 + 512],
                        start=(t == 0), stop=(t == NTILE - 1))), 0, chain=(t > 0))
                    P.add("tensor", (lambda e, t=t: e.matmul(
                        pbf[4][0:1, 0:256], coefb[:, t:t + 1],
                        mktm_g[:, t * 768 + 512:t * 768 + 768],
                        start=(t == 0), stop=(t == NTILE - 1))),
                        1 if t == NTILE - 1 else 0, chain=(t > 0))
                A("vector", lambda e: e.tensor_copy(dn_sb[0:1, 0:512],
                                                    pbf[3][0:1, 0:512]))
                A("vector", lambda e: e.tensor_copy(dn_sb[0:1, 512:768],
                                                    pbf[4][0:1, 0:256]))
                P.dma("sync",
                      out=dar_in.ap()[12288 + h * 6:12288 + (h + 1) * 6, :]
                      .rearrange("r d -> (r d)").rearrange("(a b) -> a b", a=1),
                      in_=dn_sb[0:1, 0:768])
        A("gpsimd", lambda e: e.collective_compute(
            "AllReduce", ALU.add, replica_groups=G8,
            ins=[dar_in.ap().opt()], outs=[dar_out.ap().opt()]))

        AR.reset(0)
        newmem = AR.b(12288)
        ln2fm6 = AR.b(16384)
        q_sb = AR.b(16384)
        gate_sb = AR.b(16384)
        wT6 = AR.b(8192)
        wrow6 = AR.b(4096)
        mqfm = AR.b(768)
        mkb6 = AR.b(768)
        newnormT = AR.b(96)
        biasr6 = AR.b(2048)
        dacc = AR.f(768)
        scrq = AR.f(768)
        a6 = AR.f(256)
        mkraw6 = AR.f(768)
        dena = AR.f(128)
        numq = AR.f(128)
        mh_acc = AR.f(2048)
        x2t6 = AR.f(2048)
        ln2t6 = AR.b(2048)
        nrm6 = AR.f(768)
        ntmp = AR.f(768)
        dar3 = dar_out.ap()[0:12288, :].rearrange("(h c p) d -> h p c d", p=128, c=6)

        def load_wt6(Wsrc, j0, npt):
            W3 = Wsrc.rearrange("(t p) d -> t p d", p=128)
            P.dma("gpsimd", out=wrow6[:, 0:D], in_=W3[j0])
            for jt in range(npt):
                slot = (jt % 2) * D
                if jt + 1 < npt:
                    P.dma("gpsimd", out=wrow6[:, ((jt + 1) % 2) * D:((jt + 1) % 2) * D + D],
                          in_=W3[j0 + jt + 1])
                for ch in range(0, 16, 2):
                    A("tensor", lambda e, ch=ch, slot=slot: e.transpose(
                        pbb[0][0:128, 0:128],
                        wrow6[:, slot + ch * 128:slot + (ch + 1) * 128], idb_sb[:]))
                    P.add("tensor", (lambda e, ch=ch, slot=slot: e.transpose(
                        pbb[1][0:128, 0:128],
                        wrow6[:, slot + (ch + 1) * 128:slot + (ch + 2) * 128],
                        idb_sb[:])), 1, chain=True)
                    A("vector", lambda e, ch=ch, jt=jt: e.tensor_copy(
                        wT6[:, ch * 512 + jt * 128:ch * 512 + (jt + 1) * 128],
                        pbb[0][0:128, 0:128]))
                    P.add("vector", (lambda e, ch=ch, jt=jt: e.tensor_copy(
                        wT6[:, (ch + 1) * 512 + jt * 128:(ch + 1) * 512 + (jt + 1) * 128],
                        pbb[1][0:128, 0:128])), 1, chain=True)

        def proj_tile6(t, bias_off, fmv, jw=512):
            A("tensor", lambda e, bias_off=bias_off, jw=jw: e.matmul(
                pbf[0][0:128, 0:jw], ones_rb[:],
                biasr6[0:1, bias_off:bias_off + jw], start=True, stop=False))
            for ch in range(16):
                P.add("tensor", (lambda e, ch=ch, t=t, jw=jw, fmv=fmv: e.matmul(
                    pbf[0][0:128, 0:jw],
                    fmv[:, (ch * 8 + t) * 128:(ch * 8 + t + 1) * 128],
                    wT6[:, ch * 512:ch * 512 + jw],
                    start=False, stop=(ch == 15))), 1 if ch == 15 else 0,
                    chain=(ch > 0))

        for h in range(16):
            P.dma("sync", out=dacc[:].rearrange("p (c d) -> p c d", d=128),
                  in_=dar3[h])
            P.dma("gpsimd", chain=True,
                  out=scrq[:].rearrange("p (c d) -> p c d", d=128), in_=mem3[h])
            A("vector", lambda e: e.tensor_add(dacc[:], dacc[:], scrq[:]))
            A("vector", lambda e, h=h: e.tensor_copy(newmem[:, h * 768:(h + 1) * 768],
                                                     dacc[:]))
        P.dma("sync", out=nrm6[0:16, 0:768],
              in_=dar_out.ap()[12288:12384, :].rearrange("(h s) d -> h (s d)", h=16))
        P.dma("sync", out=ntmp[0:16, 0:768], in_=norm)
        A("vector", lambda e: e.tensor_add(nrm6[0:16, :], nrm6[0:16, :],
                                           ntmp[0:16, :]))
        for fc in range(6):
            A("tensor", lambda e, fc=fc: e.transpose(
                pbf[1][0:128, 0:16], nrm6[0:16, fc * 128:(fc + 1) * 128],
                idf_sb[0:16, 0:16]))
            A("vector", lambda e, fc=fc: e.tensor_copy(
                newnormT[:, fc * 16:(fc + 1) * 16], pbf[1][0:128, 0:16]))
        for t in range(NTILE):
            P.dma("sync", out=ln2t6[:], in_=ln2d3[t])
            for ch in range(16):
                i = t * 16 + ch
                A("tensor", lambda e, ch=ch, i=i: e.transpose(
                    pbb[i % 2][0:128, 0:128], ln2t6[:, ch * 128:(ch + 1) * 128],
                    idb_sb[:]))
                A("vector", lambda e, t=t, ch=ch, i=i: e.tensor_copy(
                    ln2fm6[:, (ch * 8 + t) * 128:(ch * 8 + t + 1) * 128],
                    pbb[i % 2][0:128, 0:128]))
        P.dma("gpsimd", out=biasr6[0:1, :], in_=bq)
        for jg in range(4):
            load_wt6(Wq, 4 * jg, 4)
            for t in range(NTILE):
                proj_tile6(t, jg * 512, ln2fm6)
                A("vector", lambda e, t=t, jg=jg: e.tensor_copy(
                    q_sb[:, t * D + jg * 512:t * D + (jg + 1) * 512],
                    pbf[0][0:128, 0:512]))
        P.dma("gpsimd", out=biasr6[0:1, :], in_=bg)
        for jg in range(4):
            load_wt6(Wg, 4 * jg, 4)
            for t in range(NTILE):
                proj_tile6(t, jg * 512, ln2fm6)
                A("scalar", lambda e, t=t, jg=jg: e.activation(
                    out=gate_sb[:, t * D + jg * 512:t * D + (jg + 1) * 512],
                    in_=pbf[0][0:128, 0:512], func=AF.Sigmoid))
        for t in range(NTILE):
            for h in range(16):
                A("scalar", lambda e, t=t, h=h: e.activation(
                    out=a6[:, 0:128],
                    in_=q_sb[:, t * D + h * 128:t * D + (h + 1) * 128],
                    func=AF.Relu))
                A("scalar", lambda e, t=t, h=h: e.activation(
                    out=a6[:, 128:256],
                    in_=q_sb[:, t * D + h * 128:t * D + (h + 1) * 128],
                    func=AF.Relu, scale=-1.0))
                for u in range(3):
                    j = u + 1
                    A("vector", lambda e, u=u, j=j: e.tensor_tensor(
                        mkraw6[:, u * 256:u * 256 + j], a6[:, 0:j],
                        a6[:, 256 - j:256], op=ALU.mult))
                    A("vector", lambda e, u=u, j=j: e.tensor_tensor(
                        mkraw6[:, u * 256 + j:(u + 1) * 256], a6[:, j:256],
                        a6[:, 0:256 - j], op=ALU.mult))
                A("scalar", lambda e: e.activation(out=scrq[:], in_=mkraw6[:],
                                                   func=AF.Square, accum_out=ss2[:]))
                A("scalar", lambda e: e.activation(out=nrm1[:], in_=ss2[:],
                                                   func=AF.Sqrt))
                A("vector", lambda e: e.tensor_scalar_max(nmx[:], nrm1[:], 1e-12))
                A("vector", lambda e: e.reciprocal(rmu[:], nmx[:]))
                A("vector", lambda e: e.tensor_scalar_mul(mkb6[:], mkraw6[:], rmu[:]))
                for fc in range(6):
                    A("tensor", lambda e, fc=fc: e.transpose(
                        pbb[fc % 2][0:128, 0:128], mkb6[:, fc * 128:(fc + 1) * 128],
                        idb_sb[:]))
                    A("vector", lambda e, fc=fc: e.tensor_copy(
                        mqfm[:, fc * 128:(fc + 1) * 128], pbb[fc % 2][0:128, 0:128]))
                for fc in range(6):
                    P.add("tensor", (lambda e, fc=fc, h=h: e.matmul(
                        pbf[0][0:128, 0:128],
                        newmem[:, (h * 6 + fc) * 128:(h * 6 + fc + 1) * 128],
                        mqfm[:, fc * 128:(fc + 1) * 128],
                        start=(fc == 0), stop=(fc == 5))), 0, chain=(fc > 0))
                for fc in range(6):
                    P.add("tensor", (lambda e, fc=fc, h=h: e.matmul(
                        pbf[1][0:1, 0:128],
                        newnormT[:, fc * 16 + h:fc * 16 + h + 1],
                        mqfm[:, fc * 128:(fc + 1) * 128],
                        start=(fc == 0), stop=(fc == 5))),
                        1 if fc == 5 else 0, chain=(fc > 0))
                A("vector", lambda e: e.tensor_scalar_add(dena[0:1, 0:128],
                                                          pbf[1][0:1, 0:128], 1e-8))
                A("tensor", lambda e: e.transpose(pbf[2][0:128, 0:1],
                                                  dena[0:1, 0:128],
                                                  idf_sb[0:1, 0:1]))
                A("vector", lambda e: e.tensor_copy(nmx[:], pbf[2][0:128, 0:1]))
                A("vector", lambda e: e.reciprocal(rden[:], nmx[:]))
                A("vector", lambda e: e.tensor_copy(numq[:], pbf[0][0:128, 0:128]))
                A("tensor", lambda e: e.transpose(pbf[3][0:128, 0:128],
                                                  numq[:, 0:128], idf_sb[:]))
                A("vector", lambda e, h=h: e.tensor_scalar_mul(
                    mh_acc[:, h * 128:(h + 1) * 128], pbf[3][0:128, 0:128],
                    rden[:]))
            P.dma("sync", out=x2t6[:], in_=xsp3[t])
            P.dma("sync", chain=True, out=ln2t6[:], in_=ln2d3[t])
            A("vector", lambda e, t=t: e.tensor_tensor(
                mh_acc[:], gate_sb[:, t * D:(t + 1) * D], mh_acc[:], op=ALU.mult))
            A("vector", lambda e: e.tensor_add(x2t6[:], x2t6[:], mh_acc[:]))
            A("vector", lambda e: e.tensor_add(x2t6[:], x2t6[:], ln2t6[:]))
            P.dma("sync", out=xsp3[t], in_=x2t6[:])
        if upto == "x3":
            for i in range(NTILE):
                P.dma("sync", out=x2t6[:], in_=xsp3[i])
                P.dma("sync", out=out3[i], in_=x2t6[:])
            P.emit(block, nc, gs)
            return nc

        AR.reset(0)
        x4fm = AR.b(2 * 8192)
        act = AR.b(14 * 512)
        gw7 = AR.b(2 * 2048)
        uw7 = AR.b(2 * 2048)
        dwc7 = AR.b(2 * 1792)
        routerT = AR.b(256)
        eselb = AR.f(16 * 128)
        lg_sb = AR.f(512)
        lgt = AR.f(16)
        escm = AR.f(16)
        keep = AR.f(16)
        w_tm = AR.f(64)
        w_rows = AR.f(512)
        wb7 = AR.f(16 * 512)
        sg7 = AR.b(512)
        abuf = AR.f(512)
        po_sb = AR.f(512)
        po_t = AR.f(512)
        xt7 = AR.f(2048)
        lnt7 = AR.b(2048)
        xm7 = AR.f(2048)
        scr7 = AR.b(2048)
        res7 = AR.f(2048)
        rw7 = AR.f(2048)
        eb7 = AR.f(16)
        t8b = AR.f(8)
        for i in range(NTILE):
            P.dma("sync", out=xt7[:], in_=xsp3[i])
            emit_ln_s(xt7[:], lnt7[:], xm7, scr7)
            P.dma("sync", out=ln3d3[i], in_=lnt7[:])
        P.dma("sync", out=rw7[0:16, :], in_=rw)
        P.dma("sync", out=eb7[0:1, 0:16], in_=ebias)
        P.dma("sync", out=eselb[0:16, :], in_=esel)
        for ch in range(16):
            A("tensor", lambda e, ch=ch: e.transpose(
                pbf[0][0:128, 0:16], rw7[0:16, ch * 128:(ch + 1) * 128],
                idf_sb[0:16, 0:16]))
            A("vector", lambda e, ch=ch: e.tensor_copy(
                routerT[:, ch * 16:(ch + 1) * 16], pbf[0][0:128, 0:16]))
        for tg in range(2):
            for ch in range(16):
                P.add("sync", (lambda e, tg=tg, ch=ch: e.dma_start_transpose(
                    x4fm[:, tg * 8192 + ch * 512:tg * 8192 + (ch + 1) * 512],
                    ln3d[tg * 512:(tg + 1) * 512, ch * 128:(ch + 1) * 128])), 16,
                    chain=(ch > 0))
            A("tensor", lambda e: e.matmul(pbf[0][0:16, 0:512], eb7[0:1, 0:16],
                                           ones_rf512[:], start=True, stop=False))
            for ch in range(16):
                P.add("tensor", (lambda e, tg=tg, ch=ch: e.matmul(
                    pbf[0][0:16, 0:512], routerT[:, ch * 16:(ch + 1) * 16],
                    x4fm[:, tg * 8192 + ch * 512:tg * 8192 + (ch + 1) * 512],
                    start=False, stop=(ch == 15))), 1 if ch == 15 else 0,
                    chain=(ch > 0))
            A("vector", lambda e: e.tensor_copy(lg_sb[0:16, :], pbf[0][0:16, 0:512]))
            for j in range(4):
                A("tensor", lambda e, j=j: e.transpose(
                    pbf[1][0:128, 0:16], lg_sb[0:16, j * 128:(j + 1) * 128],
                    idf_sb[0:16, 0:16]))
                A("vector", lambda e: e.tensor_copy(lgt[:], pbf[1][0:128, 0:16]))
                A("vector", lambda e: e.reduce_max(rmax[:], lgt[:], axis=AX.X))
                A("vector", lambda e: e.tensor_scalar_mul(nrmax[:], rmax[:], -1.0))
                A("scalar", lambda e: e.activation(out=escm[:], in_=lgt[:],
                                                   func=AF.Exp, bias=nrmax[:]))
                A("vector", lambda e: e.max(t8b[:], escm[:]))
                A("vector", lambda e: e.tensor_copy(wsum[:], t8b[:, 3:4]))
                A("vector", lambda e: e.tensor_scalar(keep[:], escm[:], wsum[:],
                                                      None, op0=ALU.is_ge))
                A("vector", lambda e: e.tensor_tensor(keep[:], keep[:], escm[:],
                                                      op=ALU.mult))
                A("vector", lambda e: e.reduce_sum(wsum[:], keep[:], axis=AX.X))
                A("vector", lambda e: e.reciprocal(rsum[:], wsum[:]))
                A("vector", lambda e, j=j: e.tensor_scalar_mul(
                    w_tm[:, j * 16:(j + 1) * 16], keep[:], rsum[:]))
                A("tensor", lambda e, j=j: e.transpose(
                    pbf[1][0:16, 0:128], w_tm[:, j * 16:(j + 1) * 16], idf_sb[:]))
                A("vector", lambda e, j=j: e.tensor_copy(
                    w_rows[0:16, j * 128:(j + 1) * 128], pbf[1][0:16, 0:128]))
            for ep in range(16):
                A("tensor", lambda e, ep=ep: e.matmul(
                    pbf[1][0:128, 0:512], eselb[0:16, ep * 128:(ep + 1) * 128],
                    w_rows[0:16, :], start=True, stop=True))
                A("vector", lambda e, ep=ep: e.tensor_copy(
                    wb7[:, ep * 512:(ep + 1) * 512], pbf[1][0:128, 0:512]))
            def gu_dma(ep, ic, sl):
                P.dma("gpsimd",
                      out=gw7[:, sl * 2048:(sl + 1) * 2048]
                      .rearrange("p (c i) -> p c i", i=128),
                      in_=gp[ep][:, ic * 128:(ic + 1) * 128]
                      .rearrange("(c p) i -> p c i", p=128))
                P.dma("gpsimd", chain=True,
                      out=uw7[:, sl * 2048:(sl + 1) * 2048]
                      .rearrange("p (c i) -> p c i", i=128),
                      in_=upj[ep][:, ic * 128:(ic + 1) * 128]
                      .rearrange("(c p) i -> p c i", p=128))

            def dw_dma(ep, hc, sl):
                P.dma("gpsimd",
                      out=dwc7[:, sl * 1792:(sl + 1) * 1792]
                      .rearrange("p (c h) -> p c h", h=128),
                      in_=dpj[ep][:, hc * 128:(hc + 1) * 128]
                      .rearrange("(c p) h -> p c h", p=128))

            for ep in range(16):
                if ep == 0:
                    gu_dma(0, 0, 0)
                for ic in range(14):
                    sl = ic % 2
                    if ic + 1 < 14:
                        gu_dma(ep, ic + 1, (ic + 1) % 2)
                    elif ep + 1 < 16:
                        gu_dma(ep + 1, 0, (ic + 1) % 2)
                    for ch in range(16):
                        P.add("tensor", (lambda e, tg=tg, ch=ch, sl=sl: e.matmul(
                            pbf[2][:, 0:512],
                            gw7[:, sl * 2048 + ch * 128:sl * 2048 + (ch + 1) * 128],
                            x4fm[:, tg * 8192 + ch * 512:tg * 8192 + (ch + 1) * 512],
                            start=(ch == 0), stop=(ch == 15))), 0, chain=(ch > 0))
                    for ch in range(16):
                        P.add("tensor", (lambda e, tg=tg, ch=ch, sl=sl: e.matmul(
                            pbf[3][:, 0:512],
                            uw7[:, sl * 2048 + ch * 128:sl * 2048 + (ch + 1) * 128],
                            x4fm[:, tg * 8192 + ch * 512:tg * 8192 + (ch + 1) * 512],
                            start=(ch == 0), stop=(ch == 15))),
                            1 if ch == 15 else 0, chain=(ch > 0))
                    A("scalar", lambda e: e.activation(out=sg7[:], in_=pbf[2][:, 0:512],
                                                       func=AF.Silu))
                    A("vector", lambda e: e.tensor_tensor(abuf[:], pbf[3][:, 0:512],
                                                          sg7[:], op=ALU.mult))
                    P.add("vector", (lambda e, ep=ep, ic=ic: e.tensor_tensor(
                        act[:, ic * 512:(ic + 1) * 512],
                        abuf[:], wb7[:, ep * 512:(ep + 1) * 512], op=ALU.mult)),
                        1, chain=True)
                dw_dma(ep, 0, 0)
                for hc in range(16):
                    sl = hc % 2
                    if hc + 1 < 16:
                        dw_dma(ep, hc + 1, (hc + 1) % 2)
                    for ic in range(14):
                        P.add("tensor", (lambda e, ic=ic, sl=sl: e.matmul(
                            pbf[4][:, 0:512],
                            dwc7[:, sl * 1792 + ic * 128:sl * 1792 + (ic + 1) * 128],
                            act[:, ic * 512:(ic + 1) * 512],
                            start=(ic == 0), stop=(ic == 13))),
                            1 if ic == 13 else 0, chain=(ic > 0))
                    A("vector", lambda e: e.tensor_copy(po_sb[:], pbf[4][:, 0:512]))
                    for tj in range(4):
                        P.add("tensor", (lambda e, tj=tj: e.transpose(
                            pbf[5][0:128, tj * 128:(tj + 1) * 128],
                            po_sb[:, tj * 128:(tj + 1) * 128],
                            idf_sb[:])), 1 if tj == 3 else 0, chain=(tj > 0))
                    for tj in range(4):
                        P.add("vector", (lambda e, tj=tj: e.tensor_copy(
                            po_t[:, tj * 128:(tj + 1) * 128],
                            pbf[5][0:128, tj * 128:(tj + 1) * 128])),
                            1 if tj == 3 else 0, chain=(tj > 0))
                    P.add("gpsimd", (lambda e, tg=tg, ep=ep, hc=hc: e.dma_start(
                        out=out.rearrange("(g tj p) d -> g p tj d", p=128, tj=4)
                        [tg][:, :, hc * 128:(hc + 1) * 128],
                        in_=po_t[:].rearrange("p (tj d) -> p tj d", d=128),
                        **({} if ep == 0 else {"accum_op": ALU.add}))), 16)
        for i in range(NTILE):
            P.dma("sync", out=xt7[:], in_=xsp3[i])
            P.dma("sync", out=res7[:], in_=out3[i])
            A("vector", lambda e: e.tensor_add(res7[:], res7[:], xt7[:]))
            P.dma("sync", out=out3[i], in_=res7[:])
        P.emit(block, nc, gs)
        return nc



def fused_in_maps(d, upto="x4"):
    cns = _attn_consts()
    x = np.ascontiguousarray(
        d["L0_attn_x"].astype(np.float32, copy=False).reshape(NTOK, D))
    base = {
        "sim": d["L0_ffn_sim_matrix"].astype(np.float32),
        "gates": d["L0_ffn_gates"].reshape(1, 32).astype(np.float32),
        "w_v": d["L0_ffn_w_v"].reshape(32, 64 * D).astype(np.float32),
        "o_w": d["L0_ffn_o_weights"].reshape(32, 64 * D).astype(np.float32),
        "Wk": d["L1_attn_Wk"].astype(np.float32),
        "bk": d["L1_attn_bk"].reshape(1, 512).astype(np.float32),
        "Wv": d["L1_attn_Wv"].astype(np.float32),
        "bv": d["L1_attn_bv"].reshape(1, 512).astype(np.float32),
        "Wmb": d["L1_attn_Wmb"].astype(np.float32),
        "bmb": d["L1_attn_bmb"].reshape(1, D).astype(np.float32),
        "Wq": d["L1_attn_Wq"].astype(np.float32),
        "bq": d["L1_attn_bq"].reshape(1, D).astype(np.float32),
        "Wg": d["L1_attn_Wg"].astype(np.float32),
        "bg": d["L1_attn_bg"].reshape(1, D).astype(np.float32),
        "memory": d["L1_attn_memory"].reshape(16 * 768, 128).astype(np.float32),
        "norm": d["L1_attn_norm"].astype(np.float32),
        "ident_bf": cns["ident_bf"], "ident_f32": cns["ident_f32"],
        "ones_row_bf": np.ones((1, 128), dtype=BF),
        "ones_row_f32": np.ones((1, 128), dtype=np.float32),
        "ones_col_bf": np.ones((128, 1), dtype=BF),
        "ones_col_f32": np.ones((128, 1), dtype=np.float32),
    }
    if upto == "x4":
        es = np.zeros((16, 16 * 128), dtype=np.float32)
        for e in range(16):
            es[e, e * 128:(e + 1) * 128] = 1.0
        base.update({
            "rw": d["L1_ffn_router_w"].astype(np.float32),
            "ebias": d["L1_ffn_expert_bias"].reshape(1, 16).astype(np.float32),
            "gp": d["L1_ffn_gate_proj"].astype(np.float32),
            "up": d["L1_ffn_up_proj"].astype(np.float32),
            "dp": d["L1_ffn_down_proj"].astype(np.float32),
            "esel": es,
        })
    in_maps = []
    for c in range(NCORES):
        qbase = (c % 4) * 1024
        mk = np.zeros((32, 128, 1024), dtype=BF)
        for kt in range(32):
            kpos = kt * 128 + np.arange(128)[:, None]
            qpos = qbase + np.arange(1024)[None, :]
            mk[kt] = (kpos <= qpos).astype(BF)
        m = dict(base)
        m["x"] = np.ascontiguousarray(x[c * PTOK:(c + 1) * PTOK])
        m["masks"] = mk
        in_maps.append(m)
    return in_maps


def run_fused(d, upto="x4"):
    nc = build_fused(upto)
    in_maps = fused_in_maps(d, upto)
    res = run_bass_kernel_spmd(nc, in_maps, core_ids=list(range(NCORES)))
    return np.concatenate([res.results[c]["out"] for c in range(NCORES)], 0)


def kernel(**inputs):
    x4 = run_fused(inputs, "x4")
    return x4.reshape(2, 4096, D)
